# version 01
# baseline (speedup 1.0000x reference)
import sys, os, time
sys.path.insert(0, "/opt/trn_rl_repo")
import numpy as np
import ml_dtypes
import concourse.bass as bass
import concourse.mybir as mybir
from concourse.bass_utils import run_bass_kernel_spmd

F32 = mybir.dt.float32
BF16 = mybir.dt.bfloat16
I32 = mybir.dt.int32
AF = mybir.ActivationFunctionType
ALU = mybir.AluOpType
bfd = ml_dtypes.bfloat16

D, I = 4096, 11008
DC, ICN = 32, 86
T, NT = 512, 4
N = T * NT
EPS = 1e-5
KB = 1024

DWN_SLICES = [(0, 22 * 128), (22 * 128, 22 * 128), (44 * 128, 21 * 128), (65 * 128, 21 * 128)]
GUP_SLICES = [(0, 2048), (2048, 2048)]


class Ctr:
    def __init__(self, nc, name):
        self.sem = nc.alloc_semaphore(name)
        self.n = 0

    def inc(self, k=1):
        self.n += k
        return self.n


class DS(Ctr):
    pass


class Prog:
    ENGS = ("sync", "gpsimd", "vector", "scalar", "tensor")

    def __init__(self):
        self.streams = {e: [] for e in self.ENGS}

    def em(self, eng, fn):
        self.streams[eng].append(fn)

    def install(self, nc):
        with nc.Block() as block:
            @block.sync
            def _(eng):
                for fn in self.streams["sync"]:
                    fn(eng)

            @block.gpsimd
            def _(eng):
                for fn in self.streams["gpsimd"]:
                    fn(eng)

            @block.vector
            def _(eng):
                for fn in self.streams["vector"]:
                    fn(eng)

            @block.scalar
            def _(eng):
                for fn in self.streams["scalar"]:
                    fn(eng)

            @block.tensor
            def _(eng):
                for fn in self.streams["tensor"]:
                    fn(eng)


def alloc_buffers(nc):
    B = {}
    BASE = 16512

    def A(name, shape, dtype, offset):
        return nc.alloc_sbuf_tensor_at(name, shape, dtype, offset=BASE + offset)

    B["xn"] = A("xn", [128, DC * N], BF16, offset=0)
    B["h_half"] = [A(f"hh{i}", [128, 43 * T], BF16, offset=i * 43 * KB) for i in range(2)]
    B["xr"] = [A(f"xr{i}", [128, T], F32, offset=86 * KB + i * 2 * KB) for i in range(4)]
    B["dnq"] = A("dnq", [128, 22 * 128], BF16, offset=94 * KB)
    B["dns"] = A("dns", [128, 22 * 128], BF16, offset=100 * KB)
    B["dnz"] = A("dnz", [128, 22 * 128], BF16, offset=106 * KB)
    B["dnt"] = A("dnt", [128, 22 * 128], F32, offset=112 * KB)
    B["ig_x"] = [A(f"igx{i}", [128, D], F32, offset=i * 16 * KB) for i in range(2)]
    B["ig_st"] = [A(f"igst{i}", [128, 512], F32, offset=32 * KB + i * 4 * KB)
                  for i in range(2)]
    B["szrow"] = A("szrow", [86, D], F32, offset=64 * KB)
    B["szrow2"] = A("szrow2", [86, D], F32, offset=80 * KB)

    B["pan_g"] = [A(f"pang{i}", [128, D], BF16, offset=128 * KB + i * 8 * KB)
                  for i in range(2)]
    B["pan_u"] = [A(f"panu{i}", [128, D], BF16, offset=144 * KB + i * 8 * KB)
                  for i in range(2)]
    B["pan_d"] = [A(f"pand{i}", [128, I], BF16, offset=128 * KB + i * 22 * KB)
                  for i in range(2)]
    B["guq"] = A("guq", [128, 2048], BF16, offset=160 * KB)
    B["gus"] = A("gus", [128, 2048], BF16, offset=164 * KB)
    B["guz"] = A("guz", [128, 2048], BF16, offset=168 * KB)
    B["gut"] = A("gut", [128, 2048], F32, offset=172 * KB)

    B["ln_x"] = [A(f"lnx{i}", [128, T], F32, offset=180 * KB + i * 2 * KB)
                 for i in range(2)]
    B["ln_sq"] = [A(f"lnsq{i}", [128, T], BF16, offset=184 * KB + i * KB)
                  for i in range(2)]
    B["ln_t"] = A("ln_t", [128, T], F32, offset=186 * KB)
    B["silu"] = [A(f"silu{i}", [128, T], F32, offset=188 * KB + i * 2 * KB)
                 for i in range(2)]
    B["h_out"] = [A(f"hout{i}", [128, T], BF16, offset=192 * KB + i * KB)
                  for i in range(4)]
    B["rows"] = A("rows", [1, 4 * T], F32, offset=196 * KB)
    B["ones_f"] = A("ones_f", [1, 128], F32, offset=204 * KB)
    B["ones_b"] = A("ones_b", [128, 1], BF16, offset=204 * KB + 512)
    B["eye_f"] = A("eye_f", [128, 128], F32, offset=204 * KB + 544)
    B["zeros128"] = A("zeros128", [128, 128], F32, offset=205 * KB + 64)

    B["xn_t"] = A("xn_t", [128, DC * T], BF16, offset=0)
    B["qr_t"] = A("qr_t", [128, 32 * T], BF16, offset=32 * KB)
    B["q_roll"] = A("q_roll", [128, 32 * 768], BF16, offset=0)
    B["att_o"] = A("att_o", [128, 32 * T], BF16, offset=48 * KB)
    B["carry"] = A("carry", [128, 32 * 256], BF16, offset=80 * KB)
    B["ab"] = A("ab", [128, 4 * 256], BF16, offset=96 * KB)
    B["mrow"] = A("mrow", [1, 512], F32, offset=98 * KB)
    B["rrow"] = A("rrow", [1, 1024], F32, offset=100 * KB)
    B["jrow"] = A("jrow", [1, 128], F32, offset=104 * KB)
    B["flo"] = A("flo", [1, 8], F32, offset=104 * KB + 512)
    B["tpos"] = A("tpos", [1, 2560], F32, offset=32 * KB)
    B["arows"] = A("arows", [1, 1024], F32, offset=123 * KB)
    B["wpan"] = [A(f"wpan{i}", [128, D], BF16, offset=128 * KB + i * 8 * KB)
                 for i in range(2)]
    B["k_ext"] = A("k_ext", [128, 2560], BF16, offset=144 * KB)
    B["v_ext"] = A("v_ext", [128, 2560], BF16, offset=149 * KB)
    B["cos_e"] = A("cos_e", [128, 2560], BF16, offset=154 * KB)
    B["sinr_e"] = A("sinr_e", [128, 2560], BF16, offset=159 * KB)
    B["wkT"] = A("wkT", [128, D], BF16, offset=164 * KB)
    B["wvT"] = A("wvT", [128, D], BF16, offset=172 * KB)
    B["axr"] = [A(f"axr{i}", [128, T], F32, offset=180 * KB + i * 2 * KB)
                for i in range(2)]
    B["sig"] = [A(f"sig{i}", [128, T], BF16, offset=105 * KB + i * KB)
                for i in range(4)]
    B["w_sb"] = [A(f"wsb{i}", [128, T], BF16, offset=109 * KB + i * KB)
                 for i in range(2)]
    B["rope_f"] = A("rope_f", [128, T], F32, offset=188 * KB)
    B["rope_g"] = A("rope_g", [128, T], F32, offset=190 * KB)
    B["rope_i"] = A("rope_i", [128, T], I32, offset=192 * KB)
    B["qc_t"] = A("qc_t", [128, T], F32, offset=188 * KB)
    B["qs_t"] = A("qs_t", [128, T], F32, offset=190 * KB)
    B["kmask"] = A("kmask", [20, 128], F32, offset=205 * KB + 640)
    B["kmask_c"] = A("kmask_c", [128, 20], F32, offset=206 * KB + 128)
    B["eye_b"] = A("eye_b", [128, 128], BF16, offset=206 * KB + 256)

    def row(i):
        return B["rows"][0:1, i * T:(i + 1) * T]
    B["r_mu"], B["r_var"], B["r_t1"], B["r_r"] = row(0), row(1), row(2), row(3)

    B["p_bank"] = [nc.alloc_psum_tensor(f"pb{i}", [128, T], F32) for i in range(8)]
    B["p_g"] = B["p_bank"][0:2]
    B["p_u"] = B["p_bank"][2:4]
    B["p_sx"] = B["p_bank"][4][0:1, :]
    B["p_sq"] = B["p_bank"][5][0:1, :]
    B["p_mu"] = B["p_bank"][6]
    B["p_r"] = B["p_bank"][7]
    B["p_y"] = B["p_g"]
    B["p_ing"] = [B["p_g"][0], B["p_u"][0]]
    return B


def emit_p0(P, S, B, ds_ap, dz_ap, ds2, dz2, d_eyef):
    VE = S["VE"]
    p0a, p0b = S["p0a"], S["p0b"]
    a0 = p0a.inc(32)
    def sld(e):
        e.dma_start(out=B["szrow"][:], in_=ds_ap).then_inc(p0a.sem, 16)
        e.dma_start(out=B["szrow2"][:], in_=dz_ap).then_inc(p0a.sem, 16)
    P.em("sync", sld)

    def v(e, a0=a0):
        e.memset(B["ones_f"][:], 1.0)
        e.memset(B["ones_b"][:], 1.0)
        e.memset(B["zeros128"][:], 0.0)
        e.wait_ge(p0a.sem, a0)
        e.tensor_scalar(B["szrow"][:], B["szrow"][:], 0.5, None, ALU.mult)
        e.tensor_scalar(B["szrow2"][:], B["szrow2"][:], 0.5, None,
                        ALU.mult).then_inc(VE.sem, 1)
    P.em("vector", v)
    v0 = VE.inc(1)

    b0 = p0b.inc(48)
    def g(e, v0=v0, b0=b0):
        e.wait_ge(VE.sem, v0)
        e.dma_start(out=ds2, in_=B["szrow"][:]).then_inc(p0b.sem, 16)
        e.dma_start(out=dz2, in_=B["szrow2"][:]).then_inc(p0b.sem, 16)
        e.dma_start(out=d_eyef, in_=B["zeros128"][:]).then_inc(p0b.sem, 16)
        e.wait_ge(p0b.sem, b0)
        diag = bass.AP(d_eyef.tensor, 0, [[129, 128]])
        e.dma_start(out=diag, in_=B["ones_f"][:],
                    allow_slow_non_contiguous=True).then_inc(p0b.sem, 16)
    P.em("gpsimd", g)
    b1 = p0b.inc(16)

    p0c = S["p0c"]
    b2 = p0c.inc(16)
    def sld2(e, b1=b1):
        e.wait_ge(p0b.sem, b1)
        e.dma_start(out=B["eye_f"][:], in_=d_eyef).then_inc(p0c.sem, 16)
    P.em("sync", sld2)
    return {"sz_p0b": b1, "eye_p0b": b2}


def emit_ingress(P, S, B, x_in, xfm, p0m):
    PE, VE = S["PE"], S["VE"]
    igx, igst = S["igx"], S["igst"]
    NSUB = N // 128
    pe_sub = {}
    ev_marks = []
    st_marks = []
    gi = 0
    for s in range(NSUB):
        slot = s % 2
        xm = igx[slot].inc(16)
        def gld(e, s=s, slot=slot):
            if s >= 2:
                e.wait_ge(PE.sem, pe_sub[s - 2])
            e.dma_start(out=B["ig_x"][slot][:],
                        in_=x_in[s * 128:(s + 1) * 128, :]).then_inc(igx[slot].sem, 16)
        P.em("gpsimd", gld)

        for grp in range(8):
            def t(e, s=s, slot=slot, grp=grp, gi=gi, xm=xm):
                if grp == 0:
                    e.wait_ge(igx[slot].sem, xm)
                    if s == 0:
                        e.wait_ge(S["p0c"].sem, p0m["eye_p0b"])
                if gi >= 2:
                    e.wait_ge(VE.sem, ev_marks[gi - 2])
                pg = B["p_ing"][gi % 2]
                ins = None
                for k in range(4):
                    dc = grp * 4 + k
                    ins = e.transpose(pg[:, k * 128:(k + 1) * 128],
                                      B["ig_x"][slot][:, dc * 128:(dc + 1) * 128],
                                      B["eye_f"][:])
                ins.then_inc(PE.sem, 1)
            P.em("tensor", t)
            PE.inc(1)
            if grp == 7:
                pe_sub[s] = PE.n

            def v(e, gi=gi, pm=PE.n,
                  stm=(st_marks[gi - 2] if gi >= 2 else None)):
                e.wait_ge(PE.sem, pm)
                if stm is not None:
                    e.wait_ge(igst[gi % 2].sem, stm)
                e.tensor_copy(B["ig_st"][gi % 2][:],
                              B["p_ing"][gi % 2][:, 0:512]).then_inc(VE.sem, 1)
            P.em("vector", v)
            ev_marks.append(VE.inc(1))

            def g2(e, s=s, grp=grp, gi=gi, vm=VE.n):
                e.wait_ge(VE.sem, vm)
                dst = xfm.rearrange("dc p t -> p dc t")[:, grp * 4:(grp + 1) * 4,
                                                        s * 128:(s + 1) * 128]
                src = B["ig_st"][gi % 2][:].rearrange("p (dc j) -> p dc j", dc=4)
                e.dma_start(out=dst, in_=src).then_inc(igst[gi % 2].sem, 16)
            P.em("gpsimd", g2)
            st_marks.append(igst[gi % 2].inc(16))
            gi += 1
    return {"wr_done": [(igst[0].sem, igst[0].n), (igst[1].sem, igst[1].n)],
            "last_ev": ev_marks[-1]}


def emit_ffn(P, S, B, xfm, h_scr, caches, qsrc, mode, dep_marks, psum_free_ve,
             stage="full", dump_ap=None, pan_gate_pe=None):
    PE, VE, AC = S["PE"], S["VE"], S["AC"]
    lnx, dqs = S["lnx"], S["dqs"]

    ln_parked, ln_mm, ln_norm = [], [], []
    vrow_marks = []
    for t in range(NT):
        for dc in range(DC):
            i = t * DC + dc
            xm = lnx[i % 2].inc(16)
            def gld(e, t=t, dc=dc, i=i):
                if i == 0:
                    for sm, vv_ in dep_marks:
                        e.wait_ge(sm, vv_)
                if i >= 2:
                    e.wait_ge(VE.sem, ln_parked[i - 2])
                e.dma_start(out=B["ln_x"][i % 2][:],
                            in_=xfm[dc, :, t * T:(t + 1) * T]).then_inc(lnx[i % 2].sem, 16)
            P.em("gpsimd", gld)

            def vpark(e, t=t, dc=dc, i=i, xm=xm):
                e.wait_ge(lnx[i % 2].sem, xm)
                e.tensor_copy(B["xn"][:, dc * N + t * T: dc * N + t * T + T],
                              B["ln_x"][i % 2][:]).then_inc(VE.sem, 1)
            P.em("vector", vpark)
            ln_parked.append(VE.inc(1))

            def apark(e, t=t, dc=dc, i=i, vm=VE.n):
                e.wait_ge(VE.sem, vm)
                if i >= 2:
                    e.wait_ge(PE.sem, ln_mm[i - 2])
                e.activation(B["ln_sq"][i % 2][:],
                             B["xn"][:, dc * N + t * T: dc * N + t * T + T],
                             AF.Square).then_inc(AC.sem, 1)
            P.em("scalar", apark)
            AC.inc(1)

            vrm = vrow_marks[t - 1] if (t >= 1 and dc == 0) else None

            def tmm(e, t=t, dc=dc, i=i, am=AC.n, vrm=vrm):
                e.wait_ge(AC.sem, am)
                if vrm is not None:
                    e.wait_ge(VE.sem, vrm)
                if i == 0:
                    e.wait_ge(VE.sem, psum_free_ve)
                e.matmul(B["p_sx"], B["ones_b"][:],
                         B["xn"][:, dc * N + t * T: dc * N + t * T + T],
                         start=(dc == 0), stop=(dc == DC - 1))
                e.matmul(B["p_sq"], B["ones_b"][:], B["ln_sq"][i % 2][:],
                         start=(dc == 0), stop=(dc == DC - 1)).then_inc(PE.sem, 1)
            P.em("tensor", tmm)
            ln_mm.append(PE.inc(1))

        v_base = VE.n
        arow_prev = AC.n if t >= 1 else None

        def vrow(e, t=t, pm=PE.n, v_base=v_base, arow_prev=arow_prev):
            e.wait_ge(PE.sem, pm)
            if arow_prev is not None:
                e.wait_ge(AC.sem, arow_prev)
            e.tensor_scalar(B["r_mu"], B["p_sx"], 1.0 / D, None,
                            ALU.mult).then_inc(VE.sem, 1)
            e.wait_ge(VE.sem, v_base + 1)
            e.tensor_tensor(B["r_t1"], B["r_mu"], B["r_mu"],
                            ALU.mult).then_inc(VE.sem, 1)
            e.wait_ge(VE.sem, v_base + 2)
            e.tensor_scalar(B["r_t1"], B["r_t1"], EPS, None,
                            ALU.subtract).then_inc(VE.sem, 1)
            e.wait_ge(VE.sem, v_base + 3)
            e.scalar_tensor_tensor(B["r_var"], B["p_sq"], 1.0 / D,
                                   B["r_t1"], ALU.mult,
                                   ALU.subtract).then_inc(VE.sem, 1)
        P.em("vector", vrow)
        VE.inc(4)
        vrow_marks.append(VE.n)

        a_base = AC.n

        def arow(e, t=t, vm=VE.n, a_base=a_base):
            e.wait_ge(VE.sem, vm)
            e.activation(B["r_t1"], B["r_var"], AF.Ln).then_inc(AC.sem, 1)
            e.wait_ge(AC.sem, a_base + 1)
            e.activation(B["r_r"], B["r_t1"], AF.Exp,
                         scale=-0.5).then_inc(AC.sem, 1)
        P.em("scalar", arow)
        AC.inc(2)

        def tbc(e, t=t, am=AC.n):
            e.wait_ge(AC.sem, am)
            if t >= 1:
                e.wait_ge(VE.sem, ln_norm[t * DC - 1])
            e.matmul(B["p_mu"][:], B["ones_f"][:], B["r_mu"], start=True, stop=True)
            e.matmul(B["p_r"][:], B["ones_f"][:], B["r_r"],
                     start=True, stop=True).then_inc(PE.sem, 1)
        P.em("tensor", tbc)
        bc_pe = PE.inc(1)

        for dc in range(DC):
            v_base2 = VE.n

            def vn(e, t=t, dc=dc, pm=bc_pe, v_base2=v_base2):
                if dc == 0:
                    e.wait_ge(PE.sem, pm)
                e.wait_ge(VE.sem, v_base2)
                sl = slice(dc * N + t * T, dc * N + t * T + T)
                e.tensor_tensor(B["ln_t"][:], B["xn"][:, sl], B["p_mu"][:],
                                ALU.subtract).then_inc(VE.sem, 1)
                e.wait_ge(VE.sem, v_base2 + 1)
                e.tensor_tensor(B["xn"][:, sl], B["ln_t"][:], B["p_r"][:],
                                ALU.mult).then_inc(VE.sem, 1)
            P.em("vector", vn)
            VE.inc(2)
            ln_norm.append(VE.n)

    if stage == "ln":
        marks = []
        rs = S["rsw"][0]
        for dc in range(DC):
            def gdump(e, dc=dc, vm=ln_norm[-1], base=rs.n):
                e.wait_ge(VE.sem, vm)
                e.wait_ge(rs.sem, base)
                e.dma_start(out=dump_ap[dc * 128:(dc + 1) * 128, :],
                            in_=B["xn"][:, dc * N:(dc + 1) * N]).then_inc(rs.sem, 16)
            P.em("gpsimd", gdump)
            marks.append(rs.inc(16))
        return {"res_done": [(rs.sem, rs.n)], "last_yev": 0}

    pan_ready = {}
    gu_done = []
    dq_hist = []
    cw_marks = {}

    def produce_gu(kind, idx):
        slot = idx % 2
        prev_pe = None
        if idx >= 2:
            prev_pe = gu_done[(idx - 2) * NT + NT - 1]
        psem = S["pang"][slot] if kind == "g" else S["panu"][slot]
        if mode == "cached":
            m = psem.inc(16)
            def sld(e, kind=kind, idx=idx, slot=slot, prev_pe=prev_pe, psem=psem,
                    pg=(pan_gate_pe if idx < 2 else None)):
                if pg is not None:
                    e.wait_ge(PE.sem, pg)
                if prev_pe is not None:
                    e.wait_ge(PE.sem, prev_pe)
                e.dma_start(out=B["pan_" + kind][slot][:],
                            in_=caches[kind][idx]).then_inc(psem.sem, 16)
            P.em("sync", sld)
            pan_ready[(kind, idx)] = (psem.sem, m)
            return

        qap = qsrc[kind + "q"]
        sap = qsrc[kind + "s"]
        zap = qsrc[kind + "z"]
        cwsem = (S["cwg"] if kind == "g" else S["cwu"])[slot]
        cwm_prev = cw_marks.get((kind, idx - 2))
        for sl, (c0, cn) in enumerate(GUP_SLICES):
            gl0, gn = c0 // 128, cn // 128
            q_in = qap.rearrange("(g p) (ic j) -> ic p g j", p=128, j=128)[idx, :, gl0:gl0 + gn, :]
            s_in = sap.rearrange("g (ic j) -> ic g j", j=128)[idx, gl0:gl0 + gn, :]\
                      .unsqueeze(0).partition_broadcast(128)
            z_in = zap.rearrange("g (ic j) -> ic g j", j=128)[idx, gl0:gl0 + gn, :]\
                      .unsqueeze(0).partition_broadcast(128)
            stg_free = dq_hist[-1] if dq_hist else None

            d0 = dqs.inc(48)
            def gq(e, q_in=q_in, s_in=s_in, z_in=z_in, stg_free=stg_free):
                if stg_free is not None:
                    e.wait_ge(VE.sem, stg_free)
                e.dma_start(out=B["guq"][:, 0:2048], in_=q_in).then_inc(dqs.sem, 16)
                e.dma_start(out=B["gus"][:, 0:2048], in_=s_in).then_inc(dqs.sem, 16)
                e.dma_start(out=B["guz"][:, 0:2048], in_=z_in).then_inc(dqs.sem, 16)
            P.em("gpsimd", gq)

            v_base3 = VE.n

            def vdq(e, kind=kind, slot=slot, c0=c0, d0=d0, prev_pe=prev_pe, sl=sl,
                    v_base3=v_base3):
                e.wait_ge(dqs.sem, d0)
                e.wait_ge(VE.sem, v_base3)
                if sl == 0 and prev_pe is not None:
                    e.wait_ge(PE.sem, prev_pe)
                if sl == 0 and cwm_prev is not None:
                    e.wait_ge(cwsem.sem, cwm_prev)
                e.scalar_tensor_tensor(B["gut"][:, 0:2048], B["guq"][:, 0:2048], -8.0,
                                       B["gus"][:, 0:2048], ALU.add,
                                       ALU.mult).then_inc(VE.sem, 1)
                e.wait_ge(VE.sem, v_base3 + 1)
                e.tensor_tensor(B["pan_" + kind][slot][:, c0:c0 + 2048],
                                B["gut"][:, 0:2048], B["guz"][:, 0:2048],
                                ALU.add).then_inc(VE.sem, 1)
            P.em("vector", vdq)
            VE.inc(2)
            dq_hist.append(VE.n)
        pan_ready[(kind, idx)] = (VE.sem, VE.n)

        def gw(e, kind=kind, idx=idx, slot=slot, vm=VE.n, cwsem=cwsem):
            e.wait_ge(VE.sem, vm)
            e.dma_start(out=caches[kind][idx],
                        in_=B["pan_" + kind][slot][:]).then_inc(cwsem.sem, 16)
        P.em("gpsimd", gw)
        cw_marks[(kind, idx)] = cwsem.inc(16)

    down_mm = {}

    def produce_d(idx, pe_gate=None):
        slot = idx % 2
        prev_pe = down_mm.get(idx - 2)
        psem = S["pand"][slot]
        gu_cw_finals = ([(S["cwg"][k].sem, S["cwg"][k].n) for k in range(2)] +
                        [(S["cwu"][k].sem, S["cwu"][k].n) for k in range(2)]) \
            if (mode == "dequant" and idx < 2) else None
        if mode == "cached":
            m = psem.inc(16)
            def sld(e, idx=idx, slot=slot, prev_pe=prev_pe, pe_gate=pe_gate, psem=psem):
                if pe_gate is not None:
                    e.wait_ge(PE.sem, pe_gate)
                if prev_pe is not None:
                    e.wait_ge(PE.sem, prev_pe)
                e.dma_start(out=B["pan_d"][slot][:],
                            in_=caches["d"][idx]).then_inc(psem.sem, 16)
            P.em("sync", sld)
            pan_ready[("d", idx)] = (psem.sem, m)
            return
        qap, sap, zap = qsrc["dq"], qsrc["ds2"], qsrc["dz2"]
        cwsem = S["cwd"][slot]
        cwm_prev = cw_marks.get(("d", idx - 2))
        for sl, (c0, cn) in enumerate(DWN_SLICES):
            gl0, gn = c0 // 128, cn // 128
            q_in = qap.rearrange("(g p) (ic j) -> ic p g j", p=128, j=128)[idx, :, gl0:gl0 + gn, :]
            s_in = sap.rearrange("g (ic j) -> ic g j", j=128)[idx, gl0:gl0 + gn, :]\
                      .unsqueeze(0).partition_broadcast(128)
            z_in = zap.rearrange("g (ic j) -> ic g j", j=128)[idx, gl0:gl0 + gn, :]\
                      .unsqueeze(0).partition_broadcast(128)
            stg_free = dq_hist[-1] if dq_hist else None

            d0 = dqs.inc(48)
            def gq(e, q_in=q_in, s_in=s_in, z_in=z_in, stg_free=stg_free, cn=cn,
                   pe_gate=pe_gate, sz=p0m_sz):
                e.wait_ge(sz[0], sz[1])
                if pe_gate is not None:
                    e.wait_ge(PE.sem, pe_gate)
                if stg_free is not None:
                    e.wait_ge(VE.sem, stg_free)
                e.dma_start(out=B["dnq"][:, 0:cn], in_=q_in).then_inc(dqs.sem, 16)
                e.dma_start(out=B["dns"][:, 0:cn], in_=s_in).then_inc(dqs.sem, 16)
                e.dma_start(out=B["dnz"][:, 0:cn], in_=z_in).then_inc(dqs.sem, 16)
            P.em("gpsimd", gq)

            v_base4 = VE.n

            def vdq(e, slot=slot, c0=c0, cn=cn, d0=d0, prev_pe=prev_pe, sl=sl,
                    v_base4=v_base4, gu_cw_finals=gu_cw_finals):
                e.wait_ge(dqs.sem, d0)
                e.wait_ge(VE.sem, v_base4)
                if sl == 0 and prev_pe is not None:
                    e.wait_ge(PE.sem, prev_pe)
                if sl == 0 and cwm_prev is not None:
                    e.wait_ge(cwsem.sem, cwm_prev)
                if sl == 0 and gu_cw_finals is not None:
                    for sm, vv_ in gu_cw_finals:
                        e.wait_ge(sm, vv_)
                e.scalar_tensor_tensor(B["dnt"][:, 0:cn], B["dnq"][:, 0:cn], -8.0,
                                       B["dns"][:, 0:cn], ALU.add,
                                       ALU.mult).then_inc(VE.sem, 1)
                e.wait_ge(VE.sem, v_base4 + 1)
                e.tensor_tensor(B["pan_d"][slot][:, c0:c0 + cn],
                                B["dnt"][:, 0:cn], B["dnz"][:, 0:cn],
                                ALU.add).then_inc(VE.sem, 1)
            P.em("vector", vdq)
            VE.inc(2)
            dq_hist.append(VE.n)
        pan_ready[("d", idx)] = (VE.sem, VE.n)

        def gw(e, idx=idx, slot=slot, vm=VE.n, cwsem=cwsem):
            e.wait_ge(VE.sem, vm)
            e.dma_start(out=caches["d"][idx],
                        in_=B["pan_d"][slot][:]).then_inc(cwsem.sem, 16)
        P.em("gpsimd", gw)
        cw_marks[("d", idx)] = cwsem.inc(16)

    if stage == "deq":
        produce_gu("g", 0)
        produce_gu("u", 0)
        rs = S["rsw"][0]
        semg, vg = pan_ready[("g", 0)]
        semu, vu = pan_ready[("u", 0)]
        def gdumpg(e, vg=vg, semg=semg):
            e.wait_ge(semg, vg)
            e.dma_start(out=dump_ap[0:256, 0:2048].rearrange("(a p) c -> p a c", p=128),
                        in_=B["pan_g"][0][:].rearrange("p (a c) -> p a c", a=2)).then_inc(rs.sem, 16)
        P.em("gpsimd", gdumpg)
        rs.inc(16)
        def gdumpu(e, vu=vu, semu=semu):
            e.wait_ge(semu, vu)
            e.dma_start(out=dump_ap[256:512, 0:2048].rearrange("(a p) c -> p a c", p=128),
                        in_=B["pan_u"][0][:].rearrange("p (a c) -> p a c", a=2)).then_inc(rs.sem, 16)
        P.em("gpsimd", gdumpu)
        return {"res_done": [(rs.sem, rs.inc(16))], "last_yev": 0}

    hep, hst_marks = [], []
    hstm = S["hstm"]
    for ic in range(min(2, ICN)):
        produce_gu("g", ic)
        produce_gu("u", ic)

    for ic in range(ICN):
        semg, vg = pan_ready[("g", ic)]
        semu, vu = pan_ready[("u", ic)]
        for t in range(NT):
            gi = ic * NT + t
            def tmm(e, ic=ic, t=t, gi=gi, semg=semg, vg=vg, semu=semu, vu=vu):
                if t == 0:
                    e.wait_ge(semg, vg)
                    e.wait_ge(semu, vu)
                if ic == 0:
                    e.wait_ge(VE.sem, ln_norm[(t + 1) * DC - 1])
                if gi >= 2:
                    e.wait_ge(VE.sem, hep[gi - 2])
                pg, pu = B["p_g"][t % 2], B["p_u"][t % 2]
                for dc in range(DC):
                    rhs = B["xn"][:, dc * N + t * T: dc * N + t * T + T]
                    e.matmul(pg[:], B["pan_g"][ic % 2][:, dc * 128:(dc + 1) * 128], rhs,
                             start=(dc == 0), stop=(dc == DC - 1))
                ins = None
                for dc in range(DC):
                    rhs = B["xn"][:, dc * N + t * T: dc * N + t * T + T]
                    ins = e.matmul(pu[:], B["pan_u"][ic % 2][:, dc * 128:(dc + 1) * 128],
                                   rhs, start=(dc == 0), stop=(dc == DC - 1))
                ins.then_inc(PE.sem, 1)
            P.em("tensor", tmm)
            gu_done.append(PE.inc(1))

            def aep(e, t=t, gi=gi, pm=PE.n,
                    hv=(hep[gi - 2] if gi >= 2 else None)):
                e.wait_ge(PE.sem, pm)
                if hv is not None:
                    e.wait_ge(VE.sem, hv)
                e.activation(B["silu"][gi % 2][:], B["p_g"][t % 2][:],
                             AF.Silu).then_inc(AC.sem, 1)
            P.em("scalar", aep)
            AC.inc(1)

            def vep(e, t=t, gi=gi, am=AC.n,
                    hsm=(hst_marks[gi - 4] if gi >= 4 else None)):
                e.wait_ge(AC.sem, am)
                if hsm is not None:
                    e.wait_ge(hstm[gi % 4].sem, hsm)
                e.tensor_tensor(B["h_out"][gi % 4][:], B["silu"][gi % 2][:],
                                B["p_u"][t % 2][:], ALU.mult).then_inc(VE.sem, 1)
            P.em("vector", vep)
            hep.append(VE.inc(1))

            def ghw(e, ic=ic, t=t, gi=gi, vm=VE.n):
                e.wait_ge(VE.sem, vm)
                e.dma_start(out=h_scr[ic, :, t * T:(t + 1) * T],
                            in_=B["h_out"][gi % 4][:]).then_inc(hstm[gi % 4].sem, 16)
            P.em("gpsimd", ghw)
            hst_marks.append(hstm[gi % 4].inc(16))

        if ic + 2 < ICN:
            produce_gu("g", ic + 2)
            produce_gu("u", ic + 2)

    if stage == "gate":
        marks = []
        for ic in range(DC):
            rs = S["rbl"][ic % 2]
            def gdump(e, ic=ic, rs=rs,
                      hm=[(hstm[k].sem, hstm[k].n) for k in range(4)],
                      pv=(marks[ic - 2] if ic >= 2 else None), base=rs.n):
                for sm, vv_ in hm:
                    e.wait_ge(sm, vv_)
                if pv is not None:
                    e.wait_ge(rs.sem, pv)
                e.dma_start(out=B["ln_x"][ic % 2][:],
                            in_=h_scr[ic, :, 0:T]).then_inc(rs.sem, 16)
                e.wait_ge(rs.sem, base + 16)
                e.dma_start(out=dump_ap[ic * 128:(ic + 1) * 128, 0:T],
                            in_=B["ln_x"][ic % 2][:]).then_inc(rs.sem, 16)
            P.em("gpsimd", gdump)
            marks.append(rs.inc(32))
        return {"res_done": [(S["rbl"][0].sem, S["rbl"][0].n),
                             (S["rbl"][1].sem, S["rbl"][1].n)], "last_yev": 0}

    if mode == "dequant":
        produce_d(0, pe_gate=gu_done[-1])
        produce_d(1, pe_gate=gu_done[-1])
    yev = []
    hld = S["hld"]
    xrl = S["xrl"]
    rsw = S["rsw"]
    res_marks = []
    hload = {}
    down_tile_last = {}
    hst_all = [(hstm[k].sem, hstm[k].n) for k in range(4)]
    for t in range(NT):
        for half in range(2):
            lo = 0 if half == 0 else 43
            def ghl(e, t=t, half=half, lo=lo):
                if t >= 1:
                    e.wait_ge(PE.sem, down_tile_last[t - 1])
                if t == 0 and half == 0:
                    for sm, vv_ in hst_all:
                        e.wait_ge(sm, vv_)
                src = h_scr[lo:lo + 43, :, t * T:(t + 1) * T].rearrange("ic p t -> p ic t")
                e.dma_start(out=B["h_half"][half][:, 0:21 * T],
                            in_=src[:, 0:21, :]).then_inc(hld[half].sem, 16)
                e.dma_start(out=B["h_half"][half][:, 21 * T:43 * T],
                            in_=src[:, 21:43, :]).then_inc(hld[half].sem, 16)
            P.em("gpsimd", ghl)
            hload[(t, half)] = hld[half].inc(32)

        for dc in range(DC):
            di = t * DC + dc
            if mode == "dequant":
                if t == 0:
                    pass
                else:
                    prev_pe = down_mm.get(di - 2)
                    psem = S["pand"][dc % 2]
                    m = psem.inc(16)
                    def sld(e, dc=dc, prev_pe=prev_pe, psem=psem,
                            cwm=[(S["cwd"][k].sem, S["cwd"][k].n) for k in range(2)]):
                        if prev_pe is not None:
                            e.wait_ge(PE.sem, prev_pe)
                        for sm, vv_ in cwm:
                            e.wait_ge(sm, vv_)
                        e.dma_start(out=B["pan_d"][dc % 2][:],
                                    in_=caches["d"][dc]).then_inc(psem.sem, 16)
                    P.em("sync", sld)
                    pan_ready[("d", dc)] = (psem.sem, m)
            else:
                prev_pe = down_mm.get(di - 2)
                psem = S["pand"][dc % 2]
                m = psem.inc(16)
                pe_gate2 = gu_done[-1] if di < 2 else None
                def sldc(e, dc=dc, prev_pe=prev_pe, psem=psem, pe_gate2=pe_gate2):
                    if pe_gate2 is not None:
                        e.wait_ge(PE.sem, pe_gate2)
                    if prev_pe is not None:
                        e.wait_ge(PE.sem, prev_pe)
                    e.dma_start(out=B["pan_d"][dc % 2][:],
                                in_=caches["d"][dc]).then_inc(psem.sem, 16)
                P.em("sync", sldc)
                pan_ready[("d", dc)] = (psem.sem, m)

            semd, vd = pan_ready[("d", dc)]
            xm = xrl[di % 4].inc(16)
            def gxr(e, t=t, dc=dc, di=di,
                    rm=(res_marks[di - 4] if di >= 4 else None)):
                if di >= 4:
                    e.wait_ge(VE.sem, yev[di - 4])
                if rm is not None:
                    e.wait_ge(rsw[di % 4].sem, rm)
                e.dma_start(out=B["xr"][di % 4][:],
                            in_=xfm[dc, :, t * T:(t + 1) * T]).then_inc(xrl[di % 4].sem, 16)
            P.em("gpsimd", gxr)

            def tdn(e, t=t, dc=dc, di=di, semd=semd, vd=vd,
                    hm0=hload[(t, 0)], hm1=hload[(t, 1)]):
                e.wait_ge(semd, vd)
                e.wait_ge(hld[0].sem, hm0)
                e.wait_ge(hld[1].sem, hm1)
                if di >= 2:
                    e.wait_ge(VE.sem, yev[di - 2])
                if di == 0:
                    e.wait_ge(VE.sem, hep[-1])
                py = B["p_y"][dc % 2]
                ins = None
                for ic in range(ICN):
                    half = 0 if ic < 43 else 1
                    ofs = (ic - (0 if half == 0 else 43)) * T
                    ins = e.matmul(py[:], B["pan_d"][dc % 2][:, ic * 128:(ic + 1) * 128],
                                   B["h_half"][half][:, ofs:ofs + T],
                                   start=(ic == 0), stop=(ic == ICN - 1))
                ins.then_inc(PE.sem, 1)
            P.em("tensor", tdn)
            down_mm[di] = PE.inc(1)
            if dc == DC - 1:
                down_tile_last[t] = PE.n

            def vy(e, t=t, dc=dc, di=di, pm=PE.n, xm=xm):
                e.wait_ge(PE.sem, pm)
                e.wait_ge(xrl[di % 4].sem, xm)
                e.tensor_tensor(B["xr"][di % 4][:], B["xr"][di % 4][:],
                                B["p_y"][dc % 2][:], ALU.add).then_inc(VE.sem, 1)
            P.em("vector", vy)
            yev.append(VE.inc(1))

            def gyw(e, t=t, dc=dc, di=di, vm=VE.n):
                e.wait_ge(VE.sem, vm)
                e.dma_start(out=xfm[dc, :, t * T:(t + 1) * T],
                            in_=B["xr"][di % 4][:]).then_inc(rsw[di % 4].sem, 16)
            P.em("gpsimd", gyw)
            res_marks.append(rsw[di % 4].inc(16))
            if mode == "dequant" and t == 0 and dc + 2 < DC:
                produce_d(dc + 2)

    return {"res_done": [(rsw[k].sem, rsw[k].n) for k in range(4)],
            "last_yev": yev[-1]}


H = 32
DH = 128
SCALE = 1.0 / float(np.sqrt(128.0))
EE = 2560
L2C = float(np.log2(10000.0) / 64.0)
L_HI = float(np.float32(round(L2C * (1 << 17)) / (1 << 17)))
L_LO = float(np.float32(L2C - L_HI))
LN2 = float(np.log(2.0))
INV2PI = float(1.0 / (2 * np.pi))
C1 = float(np.float32(6.28125))
C2 = float(np.float32(2 * np.pi - 6.28125))
C3 = float(np.float32(2 * np.pi - C1 - float(np.float32(2 * np.pi - 6.28125))))
PI = float(np.pi)


def emit_wstage(P, S, B, wq_ap, wk_ap, wv_ap, wo_ap, wqT_d, woT_d, wk_bf_d, wv_bf_d,
                dep_marks):
    PE, VE, AC = S["PE"], S["VE"], S["AC"]
    ws1, ws2, ws3 = S["ws1"], S["ws2"], S["ws3"]
    def vcast(e):
        e.tensor_copy(B["eye_b"][:], B["eye_f"][:]).then_inc(VE.sem, 1)
    P.em("vector", vcast)
    eyeb_ve = VE.inc(1)

    for wi, (w_ap, wT_d) in enumerate([(wq_ap, wqT_d), (wo_ap, woT_d)]):
        for ch in range(32):
            wch = wi * 32 + ch
            sl = wch % 2
            m = ws1[sl].inc(16)
            def ld(e, w_ap=w_ap, ch=ch, sl=sl, wch=wch,
                   pv=(AC2M.get(wch - 2)), dep_marks=dep_marks):
                if wch == 0:
                    for sm, vv_ in dep_marks:
                        e.wait_ge(sm, vv_)
                if pv is not None:
                    e.wait_ge(S["ac2"][sl].sem, pv)
                e.dma_start(out=B["ig_x"][sl][:],
                            in_=w_ap[ch * 128:(ch + 1) * 128, :]).then_inc(ws1[sl].sem, 16)
            P.em("sync", ld)

            def cst(e, sl=sl, m=m, wch=wch, wv=(WS2M.get(wch - 2))):
                e.wait_ge(ws1[sl].sem, m)
                if wv is not None:
                    e.wait_ge(ws2[sl].sem, wv)
                e.activation(B["wpan"][sl][:], B["ig_x"][sl][:],
                             AF.Copy).then_inc(S["ac2"][sl].sem, 1)
            P.em("scalar", cst)
            AC2M[wch] = S["ac2"][sl].inc(1)

            def st(e, ch=ch, sl=sl, wch=wch, am=AC2M[wch], wbf=(S["wbfd"][wi])):
                e.wait_ge(S["ac2"][sl].sem, am)
                e.dma_start(out=wbf[ch * 128:(ch + 1) * 128, :],
                            in_=B["wpan"][sl][:]).then_inc(ws2[sl].sem, 16)
            P.em("gpsimd", st)
            WS2M[wch] = ws2[sl].inc(16)

    for wi, wT_d in enumerate([wqT_d, woT_d]):
        for dc in range(32):
            tch = wi * 32 + dc
            sl = tch % 2
            m = ws3[sl].inc(16)
            def tld(e, wi=wi, dc=dc, sl=sl, tch=tch,
                    wm=[(ws2[k].sem, ws2[k].n) for k in range(2)],
                    pv=(WS4M.get(tch - 2)), wbf=(S["wbfd"][wi])):
                for sm, vv_ in wm:
                    e.wait_ge(sm, vv_)
                if pv is not None:
                    e.wait_ge(S["ws4"][sl].sem, pv)
                e.dma_start_transpose(B["wpan"][sl][:],
                                      wbf[:, dc * 128:(dc + 1) * 128]).then_inc(ws3[sl].sem, 16)
            P.em("sync", tld)

            def tst(e, dc=dc, sl=sl, m=m, wT_d=wT_d):
                e.wait_ge(ws3[sl].sem, m)
                e.dma_start(out=wT_d[dc * 128:(dc + 1) * 128, :],
                            in_=B["wpan"][sl][:]).then_inc(S["ws4"][sl].sem, 16)
            P.em("gpsimd", tst)
            WS4M[tch] = S["ws4"][sl].inc(16)

    for wi2, (w_ap, wbf) in enumerate([(wk_ap, wk_bf_d), (wv_ap, wv_bf_d)]):
        sl = wi2 % 2
        m = ws1[sl].inc(16)
        def ldk(e, w_ap=w_ap, sl=sl,
                kv=[(S["ws4"][k].sem, S["ws4"][k].n) for k in range(2)]):
            for sm, vv_ in kv:
                e.wait_ge(sm, vv_)
            e.dma_start(out=B["ig_x"][sl][:], in_=w_ap).then_inc(ws1[sl].sem, 16)
        P.em("sync", ldk)

        def cstk(e, sl=sl, m=m):
            e.wait_ge(ws1[sl].sem, m)
            e.activation(B["wpan"][sl][:], B["ig_x"][sl][:],
                         AF.Copy).then_inc(S["ac2"][sl].sem, 1)
        P.em("scalar", cstk)
        am = S["ac2"][sl].inc(1)

        def stk(e, sl=sl, am=am, wbf=wbf):
            e.wait_ge(S["ac2"][sl].sem, am)
            e.dma_start(out=wbf, in_=B["wpan"][sl][:]).then_inc(ws2[sl].sem, 16)
        P.em("gpsimd", stk)
        WSKV.append(ws2[sl].inc(16))

    for wi, (wbf, dst) in enumerate([(wk_bf_d, "wkT"), (wv_bf_d, "wvT")]):
        for dc in range(32):
            m2 = S["ws5"].inc(16)
            def tl2(e, wi=wi, dc=dc, dst=dst, wbf=wbf,
                    kv=[(ws2[k].sem, ws2[k].n) for k in range(2)]):
                if dc == 0:
                    for sm, vv_ in kv:
                        e.wait_ge(sm, vv_)
                e.dma_start_transpose(B[dst][:, dc * 128:(dc + 1) * 128],
                                      wbf[:, dc * 128:(dc + 1) * 128]).then_inc(S["ws5"].sem, 16)
            P.em("sync", tl2)
    return {"wT_done": [(S["ws4"][0].sem, S["ws4"][0].n), (S["ws4"][1].sem, S["ws4"][1].n)],
            "kvT_done": [(S["ws5"].sem, S["ws5"].n)], "eyeb_ve": eyeb_ve}


def emit_rope_alpha(P, S, B, tpos_ap, kmask_ap, flo_ap, eyeb_ve, dep_yev):
    PE, VE, AC = S["PE"], S["VE"], S["AC"]
    rp = S["rp"]
    m0 = rp.inc(48)
    def ld(e):
        e.dma_start(out=B["tpos"][:], in_=tpos_ap).then_inc(rp.sem, 16)
        e.dma_start(out=B["kmask"][:], in_=kmask_ap).then_inc(rp.sem, 16)
        e.dma_start(out=B["flo"][0:1, 0:1], in_=flo_ap).then_inc(rp.sem, 16)
    P.em("gpsimd", ld)

    def gio(e, dy=dep_yev):
        e.wait_ge(S["VE"].sem, dy)
        e.iota(B["jrow"][0:1, :].rearrange("p (a b) -> p a b", b=2), [[1, 64], [0, 2]],
               base=0, channel_multiplier=0, allow_small_or_imprecise_dtypes=True)
        e.iota(B["arows"][0:1, 256:512], [[1, 256]], base=0, channel_multiplier=0,
               allow_small_or_imprecise_dtypes=True).then_inc(S["GP"].sem, 1)
    P.em("gpsimd", gio)
    gp0 = S["GP"].inc(1)

    vb = VE.n
    def vif(e, gp0=gp0, m0=m0, vb=vb):
        e.wait_ge(S["GP"].sem, gp0)
        e.wait_ge(rp.sem, m0)
        J, G_, NF, NI, PH = (B["jrow"][0:1, :], B["rope_g"][0:1, 0:128],
                            B["rope_f"][0:1, 0:128], B["rope_i"][0:1, 0:128],
                            B["rope_f"][0:1, 128:256])
        e.tensor_scalar(G_, J, L_HI, None, ALU.mult).then_inc(VE.sem, 1)
        e.wait_ge(VE.sem, vb + 1)
        e.scalar_tensor_tensor(G_, J, L_LO, G_, ALU.mult, ALU.add).then_inc(VE.sem, 1)
        e.wait_ge(VE.sem, vb + 2)
        e.tensor_scalar(NF, G_, -0.5, 12582912.0, ALU.add, ALU.add).then_inc(VE.sem, 1)
        e.wait_ge(VE.sem, vb + 3)
        e.tensor_scalar(NF, NF, 12582912.0, None, ALU.subtract).then_inc(VE.sem, 1)
        e.wait_ge(VE.sem, vb + 4)
        e.nop().then_inc(VE.sem, 1)
        e.wait_ge(VE.sem, vb + 5)
        e.tensor_tensor(PH, G_, NF, ALU.subtract).then_inc(VE.sem, 1)
        e.wait_ge(VE.sem, vb + 6)
        e.tensor_scalar(PH, PH, -LN2, None, ALU.mult).then_inc(VE.sem, 1)
        e.wait_ge(VE.sem, vb + 7)
        e.memset(G_, 1.0).then_inc(VE.sem, 1)
        k_ = vb + 8
        for kk in range(10, 0, -1):
            e.wait_ge(VE.sem, k_)
            e.tensor_tensor(B["rope_g"][0:1, 128:256], G_, PH, ALU.mult).then_inc(VE.sem, 1)
            k_ += 1
            e.wait_ge(VE.sem, k_)
            e.tensor_scalar(G_, B["rope_g"][0:1, 128:256], 1.0 / kk, 1.0,
                            ALU.mult, ALU.add).then_inc(VE.sem, 1)
            k_ += 1
        e.wait_ge(VE.sem, k_)
        e.tensor_scalar(NF, NF, -8388608.0, 1065353216.0, ALU.mult, ALU.add).then_inc(VE.sem, 1)
        k_ += 1
        e.wait_ge(VE.sem, k_)
        e.tensor_copy(NI, NF).then_inc(VE.sem, 1)
        k_ += 1
        e.wait_ge(VE.sem, k_)
        e.tensor_tensor(B["rope_f"][0:1, 0:128], G_, NI.bitcast(F32),
                        ALU.mult).then_inc(VE.sem, 1)
    P.em("vector", vif)
    VE.inc(8 + 20 + 3)
    vif_m = VE.n

    vb2 = VE.n
    def val(e, vb2=vb2):
        AR = B["arows"]
        e.tensor_scalar(AR[0:1, 256:512], AR[0:1, 256:512], 1.0 / 255.0, None,
                        ALU.mult).then_inc(VE.sem, 1)
        e.wait_ge(VE.sem, vb2 + 1)
        e.tensor_scalar(AR[0:1, 0:256], AR[0:1, 256:512], -1.0, 1.0,
                        ALU.mult, ALU.add).then_inc(VE.sem, 1)
        e.wait_ge(VE.sem, vb2 + 2)
        e.tensor_scalar(AR[0:1, 512:768], AR[0:1, 0:256], B["flo"][0:1, 0:1], None,
                        ALU.mult).then_inc(VE.sem, 1)
        e.wait_ge(VE.sem, vb2 + 3)
        e.tensor_scalar(AR[0:1, 768:1024], AR[0:1, 512:768], -1.0, 1.0,
                        ALU.mult, ALU.add).then_inc(VE.sem, 1)
    P.em("vector", val)
    VE.inc(4)
    val_m = VE.n

    def tkm(e, vm=val_m, dy=dep_yev):
        e.wait_ge(VE.sem, max(vm, dy))
        e.transpose(B["p_u"][0][:, 0:20], B["kmask"][:], B["eye_f"][0:20, 0:20]).then_inc(PE.sem, 1)
    P.em("tensor", tkm)
    tk_m = PE.inc(1)

    def vkm(e, pm=tk_m):
        e.wait_ge(PE.sem, pm)
        e.tensor_copy(B["kmask_c"][:], B["p_u"][0][:, 0:20]).then_inc(VE.sem, 1)
    P.em("vector", vkm)
    vkm_m = VE.inc(1)

    def tab(e, vm=vkm_m, dy=dep_yev):
        e.wait_ge(VE.sem, max(vm, dy))
        e.matmul(B["p_bank"][2][:, 0:512], B["ones_f"][:], B["arows"][0:1, 0:512],
                 start=True, stop=True)
        e.matmul(B["p_bank"][3][:, 0:512], B["ones_f"][:], B["arows"][0:1, 512:1024],
                 start=True, stop=True).then_inc(PE.sem, 1)
    P.em("tensor", tab)
    ab_pe = PE.inc(1)

    def vab(e, pm=ab_pe):
        e.wait_ge(PE.sem, pm)
        e.tensor_copy(B["ab"][:, 0:512], B["p_bank"][2][:, 0:512]).then_inc(VE.sem, 1)
        e.tensor_copy(B["ab"][:, 512:1024], B["p_bank"][3][:, 0:512]).then_inc(VE.sem, 1)
    P.em("vector", vab)
    VE.inc(2)
    ab_ve = VE.n

    for ss in range(EE // T):
        def tmf(e, ss=ss, vm=vif_m, vk=vkm_m, dy=dep_yev):
            if ss == 0:
                e.wait_ge(VE.sem, max(vm, vk, dy))
            else:
                e.wait_ge(VE.sem, ROPE_VM[ss - 1])
            e.matmul(B["p_g"][0][:], B["rope_f"][0:1, 0:128],
                     B["tpos"][0:1, ss * T:(ss + 1) * T],
                     start=True, stop=True).then_inc(PE.sem, 1)
        P.em("tensor", tmf)
        pm = PE.inc(1)

        vb3 = VE.n
        def vrope(e, ss=ss, pm=pm, vb3=vb3):
            e.wait_ge(PE.sem, pm)
            FF, TT_, KI = B["rope_f"], B["rope_g"], B["rope_i"]
            e.tensor_copy(FF[:], B["p_g"][0][:]).then_inc(VE.sem, 1)
            e.wait_ge(VE.sem, vb3 + 1)
            e.tensor_scalar(TT_[:], FF[:], INV2PI, 12582912.0, ALU.mult,
                            ALU.add).then_inc(VE.sem, 1)
            e.wait_ge(VE.sem, vb3 + 2)
            e.tensor_scalar(TT_[:], TT_[:], 12582912.0, None,
                            ALU.subtract).then_inc(VE.sem, 1)
            e.wait_ge(VE.sem, vb3 + 3)
            e.nop().then_inc(VE.sem, 1)
            e.wait_ge(VE.sem, vb3 + 4)
            e.scalar_tensor_tensor(FF[:], TT_[:], -C1, FF[:], ALU.mult, ALU.add).then_inc(VE.sem, 1)
            e.wait_ge(VE.sem, vb3 + 5)
            e.scalar_tensor_tensor(FF[:], TT_[:], -C2, FF[:], ALU.mult, ALU.add).then_inc(VE.sem, 1)
            e.wait_ge(VE.sem, vb3 + 6)
            e.scalar_tensor_tensor(FF[:], TT_[:], -C3, FF[:], ALU.mult, ALU.add).then_inc(VE.sem, 1)
        P.em("vector", vrope)
        VE.inc(7)
        vr_m = VE.n

        def arope(e, ss=ss, vm=vr_m):
            e.wait_ge(VE.sem, vm)
            e.activation(B["sinr_e"][:, ss * T:(ss + 1) * T], B["rope_f"][:],
                         AF.Sin).then_inc(AC.sem, 1)
        P.em("scalar", arope)
        a1 = AC.inc(1)

        vb4 = VE.n
        def vwrap(e, ss=ss, am=a1, vb4=vb4):
            e.wait_ge(AC.sem, am)
            FF, TT_ = B["rope_f"], B["rope_g"]
            e.tensor_scalar(FF[:], FF[:], PI / 2, None, ALU.add).then_inc(VE.sem, 1)
            e.wait_ge(VE.sem, vb4 + 1)
            e.tensor_single_scalar(TT_[:], FF[:], PI, ALU.is_gt).then_inc(VE.sem, 1)
            e.wait_ge(VE.sem, vb4 + 2)
            e.scalar_tensor_tensor(FF[:], TT_[:], -2 * PI, FF[:], ALU.mult,
                                   ALU.add).then_inc(VE.sem, 1)
        P.em("vector", vwrap)
        VE.inc(3)
        vw_m = VE.n

        def arope2(e, ss=ss, vm=vw_m):
            e.wait_ge(VE.sem, vm)
            e.activation(B["cos_e"][:, ss * T:(ss + 1) * T], B["rope_f"][:],
                         AF.Sin).then_inc(AC.sem, 1)
        P.em("scalar", arope2)
        a2 = AC.inc(1)

        vb5 = VE.n
        def vsr(e, ss=ss, am=a2, vb5=vb5):
            e.wait_ge(AC.sem, am)
            sl_ = slice(ss * T, (ss + 1) * T)
            e.tensor_scalar(B["sinr_e"][0:64, sl_], B["sinr_e"][0:64, sl_],
                            -1.0, None, ALU.mult).then_inc(VE.sem, 1)
        P.em("vector", vsr)
        ROPE_VM.append(VE.inc(1))
    return {"rope_ve": max(ROPE_VM[-1], ab_ve), "alpha_ve": val_m}


def emit_attn(P, S, B, xfm, q_scr, wqT_d, woT_d, kg_d, vg_d, kc_d, vc_d,
              dep_marks, wsm, rm, ncores):
    PE, VE, AC, GP = S["PE"], S["VE"], S["AC"], S["GP"]
    lnx = S["lnx"]
    NTT = NT

    kv_ev = []
    q_w = []
    for t in range(NTT):
        parked, mm_m = [], []
        for dc in range(DC):
            i = t * DC + dc
            xm = lnx[i % 2].inc(16)
            def gld(e, t=t, dc=dc, i=i,
                    deps=(dep_marks if (t == 0 and dc == 0) else None),
                    pk=(parked[dc - 2] if dc >= 2 else None),
                    qpe=(QT_PE[t - 1] if (t >= 1 and dc == 0) else None)):
                if deps is not None:
                    for sm, vv_ in deps:
                        e.wait_ge(sm, vv_)
                if pk is not None:
                    e.wait_ge(VE.sem, pk)
                if qpe is not None:
                    e.wait_ge(PE.sem, qpe)
                e.dma_start(out=B["ln_x"][i % 2][:],
                            in_=xfm[dc, :, t * T:(t + 1) * T]).then_inc(lnx[i % 2].sem, 16)
            P.em("gpsimd", gld)

            def vpark(e, dc=dc, i=i, xm=xm):
                e.wait_ge(lnx[i % 2].sem, xm)
                e.tensor_copy(B["xn_t"][:, dc * T:(dc + 1) * T],
                              B["ln_x"][i % 2][:]).then_inc(VE.sem, 1)
            P.em("vector", vpark)
            parked.append(VE.inc(1))

            def apark(e, dc=dc, i=i, vm=VE.n,
                      pmm=(mm_m[dc - 2] if dc >= 2 else None)):
                e.wait_ge(VE.sem, vm)
                if pmm is not None:
                    e.wait_ge(PE.sem, pmm)
                e.activation(B["ln_sq"][i % 2][:], B["xn_t"][:, dc * T:(dc + 1) * T],
                             AF.Square).then_inc(AC.sem, 1)
            P.em("scalar", apark)
            AC.inc(1)

            def tmm(e, t=t, dc=dc, i=i, am=AC.n,
                    rope_dep=(rm["rope_ve"] if (t == 0 and dc == 0) else None),
                    qev=(QROPE_V[-1] if (t >= 1 and dc == 0 and QROPE_V) else None)):
                e.wait_ge(AC.sem, am)
                if rope_dep is not None:
                    e.wait_ge(VE.sem, rope_dep)
                if qev is not None:
                    e.wait_ge(VE.sem, qev)
                if dc == 0 and t >= 1:
                    e.wait_ge(VE.sem, AROW_V[t - 1])
                e.matmul(B["p_sx"], B["ones_b"][:], B["xn_t"][:, dc * T:(dc + 1) * T],
                         start=(dc == 0), stop=(dc == DC - 1))
                e.matmul(B["p_sq"], B["ones_b"][:], B["ln_sq"][i % 2][:],
                         start=(dc == 0), stop=(dc == DC - 1)).then_inc(PE.sem, 1)
            P.em("tensor", tmm)
            mm_m.append(PE.inc(1))

        vb = VE.n
        def vrow(e, pm=PE.n, vb=vb):
            e.wait_ge(PE.sem, pm)
            e.tensor_scalar(B["r_mu"], B["p_sx"], 1.0 / D, None,
                            ALU.mult).then_inc(VE.sem, 1)
            e.wait_ge(VE.sem, vb + 1)
            e.tensor_tensor(B["r_t1"], B["r_mu"], B["r_mu"], ALU.mult).then_inc(VE.sem, 1)
            e.wait_ge(VE.sem, vb + 2)
            e.tensor_scalar(B["r_t1"], B["r_t1"], EPS, None, ALU.subtract).then_inc(VE.sem, 1)
            e.wait_ge(VE.sem, vb + 3)
            e.scalar_tensor_tensor(B["r_var"], B["p_sq"], 1.0 / D, B["r_t1"],
                                   ALU.mult, ALU.subtract).then_inc(VE.sem, 1)
        P.em("vector", vrow)
        VE.inc(4)
        AROW_V.append(VE.n)
        vrow_m = VE.n

        ab = AC.n
        def arow(e, vm=vrow_m, ab=ab):
            e.wait_ge(VE.sem, vm)
            e.activation(B["r_t1"], B["r_var"], AF.Ln).then_inc(AC.sem, 1)
            e.wait_ge(AC.sem, ab + 1)
            e.activation(B["r_r"], B["r_t1"], AF.Exp, scale=-0.5).then_inc(AC.sem, 1)
        P.em("scalar", arow)
        AC.inc(2)

        def tbc(e, t=t, am=AC.n, nv=(NORM_V[t - 1] if t >= 1 else None)):
            e.wait_ge(AC.sem, am)
            if nv is not None:
                e.wait_ge(VE.sem, nv)
            e.matmul(B["p_mu"][:], B["ones_f"][:], B["r_mu"], start=True, stop=True)
            e.matmul(B["p_r"][:], B["ones_f"][:], B["r_r"],
                     start=True, stop=True).then_inc(PE.sem, 1)
        P.em("tensor", tbc)
        bc_pe = PE.inc(1)

        norm_m = []
        for dc in range(DC):
            vb2 = VE.n
            def vn(e, dc=dc, pm=bc_pe, vb2=vb2):
                if dc == 0:
                    e.wait_ge(PE.sem, pm)
                e.wait_ge(VE.sem, vb2)
                sl_ = slice(dc * T, (dc + 1) * T)
                e.tensor_tensor(B["ln_t"][:], B["xn_t"][:, sl_], B["p_mu"][:],
                                ALU.subtract).then_inc(VE.sem, 1)
                e.wait_ge(VE.sem, vb2 + 1)
                e.tensor_tensor(B["xn_t"][:, sl_], B["ln_t"][:], B["p_r"][:],
                                ALU.mult).then_inc(VE.sem, 1)
            P.em("vector", vn)
            VE.inc(2)
            norm_m.append(VE.n)
        NORM_V.append(VE.n)

        def tkv(e, t=t, nm=norm_m[-1], kvm=(wsm["kvT_done"])):
            e.wait_ge(VE.sem, nm)
            if t == 0:
                for sm, vv_ in kvm:
                    e.wait_ge(sm, vv_)
            for dc in range(DC):
                e.matmul(B["p_g"][0][:], B["wkT"][:, dc * 128:(dc + 1) * 128],
                         B["xn_t"][:, dc * T:(dc + 1) * T],
                         start=(dc == 0), stop=(dc == DC - 1))
            ins = None
            for dc in range(DC):
                ins = e.matmul(B["p_g"][1][:], B["wvT"][:, dc * 128:(dc + 1) * 128],
                               B["xn_t"][:, dc * T:(dc + 1) * T],
                               start=(dc == 0), stop=(dc == DC - 1))
            ins.then_inc(PE.sem, 1)
        P.em("tensor", tkv)
        kv_pe = PE.inc(1)

        ofs = 256 + t * T
        vb3 = VE.n
        def vkr(e, t=t, pm=kv_pe, ofs=ofs, vb3=vb3):
            e.wait_ge(PE.sem, pm)
            e.wait_ge(VE.sem, vb3)
            sl_ = slice(ofs, ofs + T)
            e.tensor_tensor(B["qc_t"][:], B["p_g"][0][:], B["cos_e"][:, sl_],
                            ALU.mult).then_inc(VE.sem, 1)
            e.tensor_tensor(B["qs_t"][0:64, :], B["p_g"][0][64:128, :],
                            B["sinr_e"][0:64, sl_], ALU.mult).then_inc(VE.sem, 1)
            e.tensor_tensor(B["qs_t"][64:128, :], B["p_g"][0][0:64, :],
                            B["sinr_e"][64:128, sl_], ALU.mult).then_inc(VE.sem, 1)
            e.wait_ge(VE.sem, vb3 + 3)
            e.tensor_tensor(B["k_ext"][:, sl_], B["qc_t"][:], B["qs_t"][:],
                            ALU.add).then_inc(VE.sem, 1)
            e.wait_ge(VE.sem, vb3 + 4)
            e.tensor_copy(B["w_sb"][0][:], B["p_g"][1][:]).then_inc(VE.sem, 1)
        P.em("vector", vkr)
        VE.inc(5)
        vkr_m = VE.n

        def tvt(e, t=t, vm=vkr_m, ev=wsm["eyeb_ve"]):
            e.wait_ge(VE.sem, vm)
            pview = B["p_bank"][2][:].bitcast(BF16)
            ins = None
            for k in range(4):
                ins = e.transpose(pview[:, k * 128:(k + 1) * 128],
                                  B["w_sb"][0][:, k * 128:(k + 1) * 128],
                                  B["eye_b"][:])
            ins.then_inc(PE.sem, 1)
        P.em("tensor", tvt)
        vt_pe = PE.inc(1)

        def vvt(e, t=t, pm=vt_pe, ofs=ofs):
            e.wait_ge(PE.sem, pm)
            e.tensor_copy(B["v_ext"][:, ofs:ofs + T],
                          B["p_bank"][2][:].bitcast(BF16)[:, 0:512]).then_inc(VE.sem, 1)
        P.em("vector", vvt)
        kv_ev.append(VE.inc(1))

        for og in range(4):
            gidx = t * 4 + og
            for dc in range(DC):
                sl2 = QPN[0] % 2
                m = S["wqp"][sl2].inc(16)
                def pld(e, dc=dc, sl2=sl2,
                        pv=(QPM.get(QPN[0] - 2)), wtd=wsm["wT_done"],
                        first=(QPN[0] < 2), wqT_d=wqT_d):
                    if first:
                        for sm, vv_ in wtd:
                            e.wait_ge(sm, vv_)
                    if pv is not None:
                        e.wait_ge(PE.sem, pv)
                    e.dma_start(out=B["wpan"][sl2][:],
                                in_=wqT_d[dc * 128:(dc + 1) * 128, :]).then_inc(S["wqp"][sl2].sem, 16)
                P.em("sync", pld)

                def tq(e, t=t, og=og, dc=dc, sl2=sl2, m=m, gidx=gidx,
                       kv=kv_ev[-1]):
                    e.wait_ge(S["wqp"][sl2].sem, m)
                    if dc == 0:
                        e.wait_ge(VE.sem, kv)
                        if gidx >= 1:
                            e.wait_ge(VE.sem, QROPE_V[gidx - 1])
                    ins = None
                    for hh in range(8):
                        h = og * 8 + hh
                        ins = e.matmul(B["p_bank"][hh][:],
                                       B["wpan"][sl2][:, h * 128:(h + 1) * 128],
                                       B["xn_t"][:, dc * T:(dc + 1) * T],
                                       start=(dc == 0), stop=(dc == DC - 1))
                    ins.then_inc(PE.sem, 1)
                P.em("tensor", tq)
                QPM[QPN[0]] = PE.inc(1)
                QPN[0] += 1

            eofs = 256 + t * T
            vb4 = VE.n
            def vqr(e, t=t, og=og, pm=PE.n, eofs=eofs, vb4=vb4,
                    qwp=(QR_WR[t - 1] if (t >= 1 and og == 0) else None)):
                e.wait_ge(PE.sem, pm)
                if qwp is not None:
                    e.wait_ge(qwp[0], qwp[1])
                sl_ = slice(eofs, eofs + T)
                k_ = vb4
                for hh in range(8):
                    h = og * 8 + hh
                    Qp = B["p_bank"][hh][:]
                    e.wait_ge(VE.sem, k_)
                    e.tensor_tensor(B["qc_t"][:], Qp, B["cos_e"][:, sl_],
                                    ALU.mult).then_inc(VE.sem, 1)
                    e.tensor_tensor(B["qs_t"][0:64, :], Qp[64:128, :],
                                    B["sinr_e"][0:64, sl_], ALU.mult).then_inc(VE.sem, 1)
                    e.tensor_tensor(B["qs_t"][64:128, :], Qp[0:64, :],
                                    B["sinr_e"][64:128, sl_], ALU.mult).then_inc(VE.sem, 1)
                    k_ += 3
                    e.wait_ge(VE.sem, k_)
                    e.tensor_tensor(B["qr_t"][:, h * T:(h + 1) * T],
                                    B["qc_t"][:], B["qs_t"][:],
                                    ALU.add).then_inc(VE.sem, 1)
                    k_ += 1
            P.em("vector", vqr)
            VE.inc(32)
            QROPE_V.append(VE.n)

        def gqw(e, t=t, vm=QROPE_V[-1]):
            e.wait_ge(VE.sem, vm)
            dst = q_scr.rearrange("h p t -> p h t")[:, :, t * T:(t + 1) * T]
            src = B["qr_t"][:].rearrange("p (h t) -> p h t", h=H)
            e.dma_start(out=dst, in_=src).then_inc(S["qsw"][t % 2].sem, 16)
        P.em("gpsimd", gqw)
        q_w.append(S["qsw"][t % 2].inc(16))
        QT_PE.append(PE.n)
        QR_WR.append((S["qsw"][t % 2].sem, q_w[-1]))

    ex = S["ex"]
    m1 = ex.inc(16 * 6)
    def gcon(e, kv=kv_ev[-1]):
        e.wait_ge(VE.sem, kv)
        e.dma_start(out=kc_d[0:128, :], in_=B["k_ext"][:, 256:512]).then_inc(ex.sem, 16)
        e.dma_start(out=kc_d[128:256, :], in_=B["k_ext"][:, 2048:2304]).then_inc(ex.sem, 16)
        for si in range(2):
            e.dma_start(out=vc_d[si * 128:(si + 1) * 128, :],
                        in_=B["v_ext"][:, (2 + si) * 128:(3 + si) * 128]).then_inc(ex.sem, 16)
        for si in range(2):
            e.dma_start(out=vc_d[(2 + si) * 128:(3 + si) * 128, :],
                        in_=B["v_ext"][:, (16 + si) * 128:(17 + si) * 128]).then_inc(ex.sem, 16)
    P.em("gpsimd", gcon)

    cc = S["cc"]
    def gcc(e, m1=m1):
        e.wait_ge(ex.sem, m1)
        e.collective_compute("AllGather", ALU.bypass,
                             replica_groups=[list(range(ncores))],
                             ins=[kc_d.opt()], outs=[kg_d.opt()]).then_inc(cc.sem, 1)
        e.collective_compute("AllGather", ALU.bypass,
                             replica_groups=[list(range(ncores))],
                             ins=[vc_d.opt()], outs=[vg_d.opt()]).then_inc(cc.sem, 1)
    P.em("gpsimd", gcc)
    cc_m = cc.inc(2)

    m2 = ex.inc(16 * 6)
    def ghal(e, cc_m=cc_m):
        e.wait_ge(cc.sem, cc_m)
        pid = e.partition_id()
        lidx = (pid + ncores - 1) % ncores
        ridx = (pid + 1) % ncores
        e.dma_start(out=B["k_ext"][:, 0:256],
                    in_=kg_d[bass.ts(lidx * 2 + 1, 128)]).then_inc(ex.sem, 16)
        e.dma_start(out=B["k_ext"][:, 2304:2560],
                    in_=kg_d[bass.ts(ridx * 2, 128)]).then_inc(ex.sem, 16)
        for si in range(2):
            e.dma_start(out=B["v_ext"][:, si * 128:(si + 1) * 128],
                        in_=vg_d[bass.ts(lidx * 4 + 2 + si, 128)]).then_inc(ex.sem, 16)
        for si in range(2):
            e.dma_start(out=B["v_ext"][:, (18 + si) * 128:(19 + si) * 128],
                        in_=vg_d[bass.ts(ridx * 4 + si, 128)]).then_inc(ex.sem, 16)
    P.em("gpsimd", ghal)

    xrl = S["xrl"]
    arsw = S["arsw"]
    axr_m = {}
    res_m = {}
    och_pe = {}

    blend_cnt = [0]

    for t in range(NTT + 1):
        lo = max(512 * t, 256)
        hi = 512 * t + 768
        if t == NTT:
            lo, hi = 2048, 2304
            rofs = 512
            tt_ = 3
        else:
            rofs = lo - 512 * t
            tt_ = t
        m = S["qrl"][t % 2].inc(16)
        def gqr(e, t=t, lo=lo, hi=hi, rofs=rofs, tt_=tt_,
                qw=(QR_WR[min(tt_, NTT - 1)]),
                qwall=([(S["qsw"][k].sem, S["qsw"][k].n) for k in range(2)]
                       if t == 0 else None),
                exm=(ex.n if t == 0 else None),
                pvw=(WIN_PE[-1] if (t >= 2 and WIN_PE) else None)):
            e.wait_ge(qw[0], qw[1])
            if qwall is not None:
                for sm, vv_ in qwall:
                    e.wait_ge(sm, vv_)
            if exm is not None:
                e.wait_ge(ex.sem, exm)
            if pvw is not None:
                e.wait_ge(PE.sem, pvw)
            if t == NTT:
                dst = B["q_roll"][:].rearrange("p (h t) -> p h t", h=H)[:, :, 512:768]
            else:
                dst = B["q_roll"][:].rearrange("p (h t) -> p h t", h=H)[:, :, rofs:768]
            src = q_scr.rearrange("h p t -> p h t")[:, :, lo - 256:hi - 256]
            e.dma_start(out=dst, in_=src).then_inc(S["qrl"][t % 2].sem, 16)
        P.em("gpsimd", gqr)

        if t == 0:
            wins = [(0, 256, 512), (1, 256, 768)]
        elif t < NTT:
            wins = [(2 * t, 512 * t, 512 * t + 512), (2 * t + 1, 512 * t + 256, 512 * t + 768)]
        else:
            wins = [(8, 2048, 2304)]

        for h in range(H):
            prev_r = None
            for wi_, (j, qlo, qhi) in enumerate(wins):
                qn = qhi - qlo
                klo = 256 * j
                rbase = 512 * tt_ if t < NTT else 512 * 3
                qof = qlo - rbase if t < NTT else 512
                def tsc(e, t=t, h=h, j=j, qn=qn, klo=klo, qof=qof,
                        m=m, sgm=(SIG_V[-1] if SIG_V else None),
                        opm=(OP_EV[-1] if OP_EV else None)):
                    e.wait_ge(S["qrl"][t % 2].sem, m)
                    if sgm is not None:
                        e.wait_ge(AC.sem, sgm)
                    if opm is not None:
                        e.wait_ge(VE.sem, opm)
                    pb = B["p_bank"]
                    ins = None
                    for ks in range(4):
                        ins = e.matmul(pb[ks][:, 0:qn],
                                       B["k_ext"][:, klo + ks * 128: klo + (ks + 1) * 128],
                                       B["q_roll"][:, h * 768 + qof: h * 768 + qof + qn],
                                       start=True, stop=True)
                    ins.then_inc(PE.sem, 1)
                P.em("tensor", tsc)
                sc_pe = PE.inc(1)

                def asg(e, t=t, h=h, j=j, qn=qn, klo=klo, pm=sc_pe,
                        wv=(WOUT_V[-1] if WOUT_V else None)):
                    e.wait_ge(PE.sem, pm)
                    if wv is not None:
                        e.wait_ge(VE.sem, wv)
                    pb = B["p_bank"]
                    ins = None
                    for ks in range(4):
                        ins = e.activation(B["sig"][ks][:, 0:qn], pb[ks][:, 0:qn],
                                           AF.Sigmoid, scale=SCALE,
                                           bias=B["kmask_c"][:, klo // 128 + ks: klo // 128 + ks + 1])
                    ins.then_inc(AC.sem, 1)
                P.em("scalar", asg)
                SIG_V.append(AC.inc(1))

                def tsum(e, t=t, h=h, qn=qn, klo=klo, am=SIG_V[-1],
                         rv=(RRD_V[-1] if RRD_V else None)):
                    e.wait_ge(AC.sem, am)
                    if rv is not None:
                        e.wait_ge(VE.sem, rv)
                    ins = None
                    for ks in range(4):
                        ins = e.matmul(B["p_sx"][0:1, 0:qn], B["ones_b"][:],
                                       B["sig"][ks][:, 0:qn],
                                       start=(ks == 0), stop=(ks == 3))
                    for ks in range(4):
                        ins = e.matmul(B["p_mu"][:, 0:qn],
                                       B["v_ext"][:, (klo // 128 + ks) * 128:(klo // 128 + ks + 1) * 128],
                                       B["sig"][ks][:, 0:qn],
                                       start=(ks == 0), stop=(ks == 3))
                    ins.then_inc(PE.sem, 1)
                P.em("tensor", tsum)
                sum_pe = PE.inc(1)
                WIN_PE.append(sum_pe)

                rslot = wi_ % 2
                vb6 = VE.n
                def vwo(e, t=t, h=h, qn=qn, pm=sum_pe, rslot=rslot, vb6=vb6):
                    e.wait_ge(PE.sem, pm)
                    e.tensor_copy(B["w_sb"][rslot][:, 0:qn],
                                  B["p_mu"][:, 0:qn]).then_inc(VE.sem, 1)
                    e.tensor_scalar(B["rrow"][0:1, rslot * T: rslot * T + qn],
                                    B["p_sx"][0:1, 0:qn], 1.0, 1e-30,
                                    ALU.mult, ALU.add).then_inc(VE.sem, 1)
                    e.wait_ge(VE.sem, vb6 + 2)
                    e.reciprocal(B["rrow"][0:1, rslot * T: rslot * T + qn],
                                 B["rrow"][0:1, rslot * T: rslot * T + qn]).then_inc(VE.sem, 1)
                P.em("vector", vwo)
                VE.inc(3)
                WOUT_V.append(VE.n)
                RRD_V.append(VE.n)

                emit_blk = None
                if t == 0 and wi_ == 1:
                    emit_blk = ("b0",)
                elif 0 < t < NTT:
                    if wi_ == 0:
                        emit_blk = ("carry",)
                    else:
                        emit_blk = ("intra",)
                elif t == NTT:
                    emit_blk = ("carry",)

                if emit_blk is not None:
                    kind = emit_blk[0]
                    bslot = blend_cnt[0] % 2
                    blend_cnt[0] += 1
                    bank_a = 7 if bslot == 0 else 5
                    vb7 = VE.n
                    def vmr(e, kind=kind, rslot=rslot, vb7=vb7, wv=WOUT_V[-1]):
                        e.wait_ge(VE.sem, wv)
                        e.tensor_copy(B["mrow"][0:1, 0:256],
                                      B["rrow"][0:1, rslot * T: rslot * T + 256]).then_inc(VE.sem, 1)
                        if kind != "carry":
                            src_ = (B["rrow"][0:1, (1 - rslot) * T:(1 - rslot) * T + 256]
                                    if kind == "b0" else
                                    B["rrow"][0:1, (1 - rslot) * T + 256:(1 - rslot) * T + 512])
                            e.wait_ge(VE.sem, vb7 + 1)
                            e.tensor_copy(B["mrow"][0:1, 256:512], src_).then_inc(VE.sem, 1)
                    P.em("vector", vmr)
                    VE.inc(1 if kind == "carry" else 2)
                    mr_m = VE.n

                    def tbr(e, bank_a=bank_a, vm=mr_m,
                            bv=(BLD_V[-2] if len(BLD_V) >= 2 else None)):
                        e.wait_ge(VE.sem, vm)
                        if bv is not None:
                            e.wait_ge(VE.sem, bv)
                        e.matmul(B["p_bank"][bank_a][:, 0:512], B["ones_f"][:],
                                 B["mrow"][0:1, 0:512],
                                 start=True, stop=True).then_inc(PE.sem, 1)
                    P.em("tensor", tbr)
                    br_pe = PE.inc(1)

                    vb8 = VE.n
                    def vbl(e, t=t, h=h, kind=kind, rslot=rslot, pm=br_pe,
                            bank_a=bank_a, vb8=vb8):
                        e.wait_ge(PE.sem, pm)
                        e.wait_ge(VE.sem, vb8)
                        pbk = B["p_bank"][bank_a]
                        e.tensor_tensor(B["qc_t"][:, 0:256], B["w_sb"][rslot][:, 0:256],
                                        pbk[:, 0:256], ALU.mult).then_inc(VE.sem, 1)
                        if kind == "carry":
                            e.tensor_copy(B["qs_t"][:, 0:256],
                                          B["carry"][:, h * 256:(h + 1) * 256]).then_inc(VE.sem, 1)
                        else:
                            pw = B["w_sb"][1 - rslot][:, 0:256] if kind == "b0" \
                                else B["w_sb"][1 - rslot][:, 256:512]
                            e.tensor_tensor(B["qs_t"][:, 0:256], pw, pbk[:, 256:512],
                                            ALU.mult).then_inc(VE.sem, 1)
                        e.wait_ge(VE.sem, vb8 + 2)
                        aoff = (512, 768) if kind == "b0" else (0, 256)
                        e.tensor_tensor(B["qs_t"][:, 256:512], B["qs_t"][:, 0:256],
                                        B["ab"][:, aoff[0]:aoff[0] + 256],
                                        ALU.mult).then_inc(VE.sem, 1)
                        e.tensor_tensor(B["qc_t"][:, 256:512], B["qc_t"][:, 0:256],
                                        B["ab"][:, aoff[1]:aoff[1] + 256],
                                        ALU.mult).then_inc(VE.sem, 1)
                        e.wait_ge(VE.sem, vb8 + 4)
                        if kind == "b0" or kind == "carry":
                            cof = 0
                        else:
                            cof = 256
                        e.tensor_tensor(B["att_o"][:, h * T + cof: h * T + cof + 256],
                                        B["qs_t"][:, 256:512], B["qc_t"][:, 256:512],
                                        ALU.add).then_inc(VE.sem, 1)
                    P.em("vector", vbl)
                    VE.inc(5)
                    BLD_V.append(VE.n)

                if wi_ == len(wins) - 1 and t < NTT:
                    bank_c = 5 if (blend_cnt[0] % 2 == 0) else 7
                    vbc = VE.n
                    def vcr(e, rslot=rslot, vbc=vbc,
                            bv=(BLD_V[-1] if BLD_V else None)):
                        if bv is not None:
                            e.wait_ge(VE.sem, bv)
                        e.tensor_copy(B["mrow"][0:1, 0:256],
                                      B["rrow"][0:1, rslot * T + 256: rslot * T + 512]).then_inc(VE.sem, 1)
                    P.em("vector", vcr)
                    VE.inc(1)
                    vcr_m = VE.n

                    def tcr(e, bank_c=bank_c, vm=vcr_m,
                            bv=(BLD_V[-2] if len(BLD_V) >= 2 else None)):
                        e.wait_ge(VE.sem, vm)
                        if bv is not None:
                            e.wait_ge(VE.sem, bv)
                        e.matmul(B["p_bank"][bank_c][:, 0:256], B["ones_f"][:],
                                 B["mrow"][0:1, 0:256],
                                 start=True, stop=True).then_inc(PE.sem, 1)
                    P.em("tensor", tcr)
                    cr_pe = PE.inc(1)

                    def vcu(e, t=t, h=h, rslot=rslot, pm=cr_pe, bank_c=bank_c):
                        e.wait_ge(PE.sem, pm)
                        e.tensor_tensor(B["carry"][:, h * 256:(h + 1) * 256],
                                        B["w_sb"][rslot][:, 256:512],
                                        B["p_bank"][bank_c][:, 0:256],
                                        ALU.mult).then_inc(VE.sem, 1)
                    P.em("vector", vcu)
                    VE.inc(1)
                    BLD_V.append(VE.n)

        if t == 0:
            cn_, tok0 = 256, 0
        elif t < NTT:
            cn_, tok0 = 512, 512 * t - 256
        else:
            cn_, tok0 = 256, 1792
        for og in range(4):
            for hc in range(H):
                sl3 = QPN[0] % 2
                m3 = S["wqp"][sl3].inc(16)
                def pldo(e, hc=hc, sl3=sl3, pv=(QPM.get(QPN[0] - 2)), woT_d=woT_d):
                    if pv is not None:
                        e.wait_ge(PE.sem, pv)
                    e.dma_start(out=B["wpan"][sl3][:],
                                in_=woT_d[hc * 128:(hc + 1) * 128, :]).then_inc(S["wqp"][sl3].sem, 16)
                P.em("sync", pldo)

                def to_(e, t=t, og=og, hc=hc, sl3=sl3, m3=m3, cn_=cn_,
                        bv=BLD_V[-1], wv=(OP_EV[-1] if OP_EV else None)):
                    e.wait_ge(S["wqp"][sl3].sem, m3)
                    if hc == 0:
                        e.wait_ge(VE.sem, bv)
                        if wv is not None:
                            e.wait_ge(VE.sem, wv)
                    ins = None
                    for oc in range(8):
                        o = og * 8 + oc
                        ins = e.matmul(B["p_bank"][oc][:, 0:cn_],
                                       B["wpan"][sl3][:, o * 128:(o + 1) * 128],
                                       B["att_o"][:, hc * T: hc * T + cn_],
                                       start=(hc == 0), stop=(hc == H - 1))
                    ins.then_inc(PE.sem, 1)
                P.em("tensor", to_)
                QPM[QPN[0]] = PE.inc(1)
                QPN[0] += 1
            op_pe = PE.n

            for oc in range(8):
                dc0 = og * 8 + oc
                axslot = AXN[0] % 2
                m4 = S["axl"][axslot].inc(16)
                def gxa(e, tok0=tok0, dc0=dc0, axslot=axslot, cn_=cn_,
                        pv=(AXM.get(AXN[0] - 2))):
                    if pv is not None:
                        e.wait_ge(arsw[axslot].sem, pv)
                    e.dma_start(out=B["axr"][axslot][:, 0:cn_],
                                in_=xfm[dc0, :, tok0:tok0 + cn_]).then_inc(S["axl"][axslot].sem, 16)
                P.em("gpsimd", gxa)

                def vo(e, oc=oc, axslot=axslot, m4=m4, pm=op_pe, cn_=cn_):
                    e.wait_ge(PE.sem, pm)
                    e.wait_ge(S["axl"][axslot].sem, m4)
                    e.tensor_tensor(B["axr"][axslot][:, 0:cn_],
                                    B["axr"][axslot][:, 0:cn_],
                                    B["p_bank"][oc][:, 0:cn_], ALU.add).then_inc(VE.sem, 1)
                P.em("vector", vo)
                OP_EV.append(VE.inc(1))

                def gwa(e, tok0=tok0, dc0=dc0, axslot=axslot, cn_=cn_,
                        vm=OP_EV[-1]):
                    e.wait_ge(VE.sem, vm)
                    e.dma_start(out=xfm[dc0, :, tok0:tok0 + cn_],
                                in_=B["axr"][axslot][:, 0:cn_]).then_inc(arsw[axslot].sem, 16)
                P.em("gpsimd", gwa)
                AXM[AXN[0]] = arsw[axslot].inc(16)
                AXN[0] += 1

    return {"res_done": [(arsw[k].sem, arsw[k].n) for k in range(2)]}


AC2M = {}
WS2M = {}
WS4M = {}
WSKV = []
ROPE_VM = []
AROW_V = []
NORM_V = []
QPM = {}
QPN = [0]
QT_PE = []
QROPE_V = []
QR_WR = []
WIN_PE = []
SIG_V = []
WOUT_V = []
RRD_V = []
BLD_V = []
OP_EV = []
AXM = {}
AXN = [0]


def emit_readback(P, S, B, xfm, out_ap, dep_marks):
    rbl, rbs = S["rbl"], S["rbs"]
    st_marks = {}
    for dc in range(DC):
        m = rbl[dc % 2].inc(16)
        def sld(e, dc=dc, dep_marks=dep_marks,
                pv=(st_marks.get(dc - 2))):
            if dc == 0:
                for sm, vv_ in dep_marks:
                    e.wait_ge(sm, vv_)
            if dc >= 2:
                e.wait_ge(rbs[dc % 2].sem, st_marks[dc - 2])
            e.dma_start(out=B["ig_x"][dc % 2][:, 0:2048],
                        in_=xfm[dc]).then_inc(rbl[dc % 2].sem, 16)
        P.em("sync", sld)

        def gst(e, dc=dc, m=m):
            e.wait_ge(rbl[dc % 2].sem, m)
            e.dma_start(out=out_ap[dc * 128:(dc + 1) * 128, :],
                        in_=B["ig_x"][dc % 2][:, 0:2048]).then_inc(rbs[dc % 2].sem, 16)
        P.em("gpsimd", gst)
        st_marks[dc] = rbs[dc % 2].inc(16)

    def fin(e):
        e.wait_ge(rbs[0].sem, rbs[0].n)
        e.wait_ge(rbs[1].sem, rbs[1].n)
    P.em("gpsimd", fin)


def emit_egress(P, S, B, xfm, out_ap, dep_marks, p0m):
    PE, VE = S["PE"], S["VE"]
    igx, igst = S["igx"], S["igst"]
    NSUB = N // 128
    ev_marks = []
    st_marks = []
    ld_marks = {}
    gi = 0
    for s in range(NSUB):
        slot = s % 2
        xm = igx[slot].inc(16)
        def gld(e, s=s, slot=slot, deps=(dep_marks if s == 0 else None),
                pv=(ld_marks.get(s - 2))):
            if deps is not None:
                for sm, vv_ in deps:
                    e.wait_ge(sm, vv_)
            if pv is not None:
                e.wait_ge(PE.sem, pv)
            src_ = xfm.rearrange("dc p t -> p dc t")[:, :, s * 128:(s + 1) * 128]
            e.dma_start(out=B["ig_x"][slot][:].rearrange("p (dc j) -> p dc j", dc=32),
                        in_=src_).then_inc(igx[slot].sem, 16)
        P.em("gpsimd", gld)

        for grp in range(8):
            def t(e, s=s, slot=slot, grp=grp, gi=gi, xm=xm):
                if grp == 0:
                    e.wait_ge(igx[slot].sem, xm)
                if gi >= 2:
                    e.wait_ge(VE.sem, ev_marks[gi - 2])
                pg = B["p_ing"][gi % 2]
                ins = None
                for k in range(4):
                    dc = grp * 4 + k
                    ins = e.transpose(pg[:, k * 128:(k + 1) * 128],
                                      B["ig_x"][slot][:, dc * 128:(dc + 1) * 128],
                                      B["eye_f"][:])
                ins.then_inc(PE.sem, 1)
            P.em("tensor", t)
            PE.inc(1)
            if grp == 7:
                ld_marks[s] = PE.n

            def v(e, gi=gi, pm=PE.n,
                  stm=(st_marks[gi - 2] if gi >= 2 else None)):
                e.wait_ge(PE.sem, pm)
                if stm is not None:
                    e.wait_ge(igst[gi % 2].sem, stm)
                e.tensor_copy(B["ig_st"][gi % 2][:],
                              B["p_ing"][gi % 2][:, 0:512]).then_inc(VE.sem, 1)
            P.em("vector", v)
            ev_marks.append(VE.inc(1))

            def g2(e, s=s, grp=grp, gi=gi, vm=VE.n):
                e.wait_ge(VE.sem, vm)
                dst = out_ap[s * 128:(s + 1) * 128,
                             grp * 512:(grp + 1) * 512]
                e.dma_start(out=dst, in_=B["ig_st"][gi % 2][:]).then_inc(igst[gi % 2].sem, 16)
            P.em("gpsimd", g2)
            st_marks.append(igst[gi % 2].inc(16))
            gi += 1

    def fin(e):
        e.wait_ge(igst[0].sem, igst[0].n)
        e.wait_ge(igst[1].sem, igst[1].n)
    P.em("gpsimd", fin)


def _reset_emit_state():
    for d in (AC2M, WS2M, WS4M, QPM, AXM):
        d.clear()
    for l in (WSKV, ROPE_VM, AROW_V, NORM_V, QT_PE, QROPE_V, QR_WR, WIN_PE,
              SIG_V, WOUT_V, RRD_V, BLD_V, OP_EV):
        l.clear()
    QPN[0] = 0
    AXN[0] = 0


def build(mode="dequant", ncores=8):
    _reset_emit_state()
    nc = bass.Bass("TRN2", target_bir_lowering=False)
    x_ap = nc.dram_tensor("x", [N, D], F32, kind="ExternalInput").ap()
    wq_ap = nc.dram_tensor("wq", [D, D], F32, kind="ExternalInput").ap()
    wk_ap = nc.dram_tensor("wk", [128, D], F32, kind="ExternalInput").ap()
    wv_ap = nc.dram_tensor("wv", [128, D], F32, kind="ExternalInput").ap()
    wo_ap = nc.dram_tensor("wo", [D, D], F32, kind="ExternalInput").ap()
    tpos_ap = nc.dram_tensor("tpos", [1, EE], F32, kind="ExternalInput").ap()
    kmask_ap = nc.dram_tensor("kmask", [20, 128], F32, kind="ExternalInput").ap()
    flo_ap = nc.dram_tensor("flo", [1, 1], F32, kind="ExternalInput").ap()
    gq_ap = nc.dram_tensor("gq", [D, I], I32, kind="ExternalInput").ap()
    gs_ap = nc.dram_tensor("gs", [DC, I], F32, kind="ExternalInput").ap()
    gz_ap = nc.dram_tensor("gz", [DC, I], F32, kind="ExternalInput").ap()
    uq_ap = nc.dram_tensor("uq", [D, I], I32, kind="ExternalInput").ap()
    us_ap = nc.dram_tensor("us", [DC, I], F32, kind="ExternalInput").ap()
    uz_ap = nc.dram_tensor("uz", [DC, I], F32, kind="ExternalInput").ap()
    dq_ap = nc.dram_tensor("dq", [I, D], I32, kind="ExternalInput").ap()
    ds_ap = nc.dram_tensor("ds", [ICN, D], F32, kind="ExternalInput").ap()
    dz_ap = nc.dram_tensor("dz", [ICN, D], F32, kind="ExternalInput").ap()
    out_ap = nc.dram_tensor("out", [N, D], F32, kind="ExternalOutput").ap()

    xfm = nc.dram_tensor("xfm", [DC, 128, N], F32, kind="Internal").ap()
    h_scr = nc.dram_tensor("h_scr", [ICN, 128, N], BF16, kind="Internal").ap()
    ds2 = nc.dram_tensor("ds2", [ICN, D], F32, kind="Internal").ap()
    dz2 = nc.dram_tensor("dz2", [ICN, D], F32, kind="Internal").ap()
    d_eyef = nc.dram_tensor("d_eyef", [128, 128], F32, kind="Internal").ap()
    cache_g = nc.dram_tensor("cache_g", [ICN, 128, D], BF16, kind="Internal").ap()
    cache_u = nc.dram_tensor("cache_u", [ICN, 128, D], BF16, kind="Internal").ap()
    cache_d = nc.dram_tensor("cache_d", [DC, 128, I], BF16, kind="Internal").ap()
    q_scr = nc.dram_tensor("q_scr", [H, 128, N], BF16, kind="Internal").ap()
    wq_bf = nc.dram_tensor("wq_bf", [D, D], BF16, kind="Internal").ap()
    wo_bf = nc.dram_tensor("wo_bf", [D, D], BF16, kind="Internal").ap()
    wqT_d = nc.dram_tensor("wqT_d", [D, D], BF16, kind="Internal").ap()
    woT_d = nc.dram_tensor("woT_d", [D, D], BF16, kind="Internal").ap()
    wk_bf = nc.dram_tensor("wk_bf", [128, D], BF16, kind="Internal").ap()
    wv_bf = nc.dram_tensor("wv_bf", [128, D], BF16, kind="Internal").ap()
    kc_d = nc.dram_tensor("kc_d", [256, 256], BF16, kind="Internal").ap()
    vc_d = nc.dram_tensor("vc_d", [512, 128], BF16, kind="Internal").ap()
    kg_d = nc.dram_tensor("kg_d", [256 * ncores, 256], BF16, kind="Internal").ap()
    vg_d = nc.dram_tensor("vg_d", [512 * ncores, 128], BF16, kind="Internal").ap()
    caches = {"g": cache_g, "u": cache_u, "d": cache_d}
    qsrc = {"gq": gq_ap, "gs": gs_ap, "gz": gz_ap,
            "uq": uq_ap, "us": us_ap, "uz": uz_ap,
            "dq": dq_ap, "ds2": ds2, "dz2": dz2}

    B = alloc_buffers(nc)
    S = {k: Ctr(nc, k) for k in ("PE", "VE", "AC")}

    S["p0a"] = DS(nc, "p0a"); S["p0b"] = DS(nc, "p0b"); S["p0c"] = DS(nc, "p0c")
    S["igx"] = [DS(nc, f"igx{i}") for i in range(2)]
    S["igst"] = [DS(nc, f"igst{i}") for i in range(2)]
    S["lnx"] = [DS(nc, f"lnx{i}") for i in range(2)]
    S["dqs"] = DS(nc, "dqs")
    S["cwg"] = [DS(nc, f"cwg{i}") for i in range(2)]
    S["cwu"] = [DS(nc, f"cwu{i}") for i in range(2)]
    S["cwd"] = [DS(nc, f"cwd{i}") for i in range(2)]
    S["pang"] = [DS(nc, f"pang{i}") for i in range(2)]
    S["panu"] = [DS(nc, f"panu{i}") for i in range(2)]
    S["pand"] = [DS(nc, f"pand{i}") for i in range(2)]
    S["hstm"] = [DS(nc, f"hstm{i}") for i in range(4)]
    S["hld"] = [DS(nc, f"hld{i}") for i in range(2)]
    S["xrl"] = [DS(nc, f"xrl{i}") for i in range(4)]
    S["rsw"] = [DS(nc, f"rsw{i}") for i in range(4)]
    S["rbl"] = [DS(nc, f"rbl{i}") for i in range(2)]
    S["rbs"] = [DS(nc, f"rbs{i}") for i in range(2)]
    S["GP"] = Ctr(nc, "GP")
    S["ws1"] = [DS(nc, f"ws1{i}") for i in range(2)]
    S["ws2"] = [DS(nc, f"ws2{i}") for i in range(2)]
    S["ws3"] = [DS(nc, f"ws3{i}") for i in range(2)]
    S["ws4"] = [DS(nc, f"ws4{i}") for i in range(2)]
    S["ws5"] = DS(nc, "ws5")
    S["ac2"] = [Ctr(nc, f"ac2{i}") for i in range(2)]
    S["rp"] = DS(nc, "rp")
    S["qsw"] = [DS(nc, f"qsw{i}") for i in range(2)]
    S["qrl"] = [DS(nc, f"qrl{i}") for i in range(2)]
    S["wqp"] = [DS(nc, f"wqp{i}") for i in range(2)]
    S["ex"] = DS(nc, "ex")
    S["cc"] = Ctr(nc, "cc")
    S["axl"] = [DS(nc, f"axl{i}") for i in range(2)]
    S["arsw"] = [DS(nc, f"arsw{i}") for i in range(2)]
    P = Prog()

    p0m = emit_p0(P, S, B, ds_ap, dz_ap, ds2, dz2, d_eyef)
    global p0m_sz
    p0m_sz = (S["p0b"].sem, p0m["sz_p0b"])
    igm = emit_ingress(P, S, B, x_ap, xfm, p0m)
    S["wbfd"] = [wq_bf, wo_bf]
    stage = "kern"
    if stage == "ing":
        emit_readback(P, S, B, xfm, out_ap, igm["wr_done"])
    elif stage in ("attn", "mod", "kern"):
        ffm = emit_ffn(P, S, B, xfm, h_scr, caches, qsrc, mode,
                       dep_marks=igm["wr_done"], psum_free_ve=igm["last_ev"],
                       stage="full", dump_ap=None)
        wsm = emit_wstage(P, S, B, wq_ap, wk_ap, wv_ap, wo_ap, wqT_d, woT_d,
                          wk_bf, wv_bf, dep_marks=ffm["res_done"])
        rm = emit_rope_alpha(P, S, B, tpos_ap, kmask_ap, flo_ap, wsm["eyeb_ve"],
                             dep_yev=ffm["last_yev"])
        adeps = (ffm["res_done"] +
                 [(S["ws1"][k].sem, S["ws1"][k].n) for k in range(2)] +
                 [(S["ws2"][k].sem, S["ws2"][k].n) for k in range(2)])
        am_ = emit_attn(P, S, B, xfm, q_scr, wqT_d, woT_d, kg_d, vg_d, kc_d, vc_d,
                        dep_marks=adeps, wsm=wsm, rm=rm, ncores=ncores)
        if stage == "mod":
            ffm2 = emit_ffn(P, S, B, xfm, h_scr, caches, qsrc, "cached",
                            dep_marks=am_["res_done"], psum_free_ve=OP_EV[-1],
                            stage="full", dump_ap=None, pan_gate_pe=QPM[QPN[0] - 1])
            emit_readback(P, S, B, xfm, out_ap, ffm2["res_done"])
        elif stage == "kern":
            ffm2 = emit_ffn(P, S, B, xfm, h_scr, caches, qsrc, "cached",
                            dep_marks=am_["res_done"], psum_free_ve=OP_EV[-1],
                            stage="full", dump_ap=None, pan_gate_pe=QPM[QPN[0] - 1])
            emit_egress(P, S, B, xfm, out_ap, ffm2["res_done"], p0m)
        else:
            emit_readback(P, S, B, xfm, out_ap, am_["res_done"])
    else:
        ffm = emit_ffn(P, S, B, xfm, h_scr, caches, qsrc, mode,
                       dep_marks=igm["wr_done"], psum_free_ve=igm["last_ev"],
                       stage=stage, dump_ap=out_ap)
        if stage == "full":
            emit_readback(P, S, B, xfm, out_ap, ffm["res_done"])
        else:
            def fin2(e, dm=ffm["res_done"]):
                for sm, vv_ in dm:
                    e.wait_ge(sm, vv_)
            P.em("gpsimd", fin2)

    P.install(nc)
    if os.environ.get("MKAUDIT"):
        import collections
        tally = collections.Counter()
        for ins in nc.all_instructions():
            si = ins.sync_info
            if si is None:
                continue
            for su in si.on_update:
                if su.update_mode in ("sem-inc", "sem-add-imm"):
                    tally[su.id] += su.update_value
        def flat(d):
            for k, v in d.items():
                if isinstance(v, list):
                    for i, x in enumerate(v):
                        yield f"{k}{i}", x
                else:
                    yield k, v
        for k, c in flat(S):
            if tally.get(c.sem.num, 0) != c.n:
                print(f"AUDIT MISMATCH {k}: ledger={c.n} emitted={tally.get(c.sem.num)}")
        print("AUDIT done")
    return nc


def emulate(x, gq, gs, gz, uq, us, uz, dq, ds, dz):
    f32 = np.float32
    xbf = x.astype(bfd).astype(f32)
    mu = xbf.mean(axis=1, keepdims=True)
    sq = (xbf ** 2).astype(bfd).astype(f32)
    var = sq.mean(axis=1, keepdims=True) - mu ** 2
    r = np.exp(-0.5 * np.log(var + EPS))
    xn = ((xbf - mu) * r).astype(bfd).astype(f32)

    def deq(q, s, z, half=False):
        G = q.shape[0] // s.shape[0]
        sf = np.repeat(s, G, axis=0).astype(bfd).astype(f32)
        zf = np.repeat(z, G, axis=0).astype(bfd).astype(f32)
        if half:
            sf = np.repeat(0.5 * s, G, axis=0).astype(bfd).astype(f32)
            zf = np.repeat(0.5 * z, G, axis=0).astype(bfd).astype(f32)
        return ((q.astype(f32) - 8.0) * sf + zf).astype(bfd).astype(f32)

    Wg = deq(gq, gs, gz)
    Wu = deq(uq, us, uz)
    Wd = deq(dq, ds, dz, half=True)
    gate = xn @ Wg
    up = xn @ Wu
    sil = gate / (1.0 + np.exp(-gate))
    h = (sil * up).astype(bfd).astype(f32)
    y = h @ Wd
    return x + y


_NC_CACHE = {}


def kernel(L0_attn_x, L0_attn_Wq, L0_attn_Wk, L0_attn_Wv, L0_attn_Wo,
           L0_ffn_gate_q, L0_ffn_gate_s, L0_ffn_gate_z,
           L0_ffn_up_q, L0_ffn_up_s, L0_ffn_up_z,
           L0_ffn_down_q, L0_ffn_down_s, L0_ffn_down_z):
    x = np.ascontiguousarray(np.asarray(L0_attn_x, dtype=np.float32))
    BB, SS, _ = x.shape
    NCORES = 8
    CHUNKS = 4

    if "nc" not in _NC_CACHE:
        _NC_CACHE["nc"] = build("dequant", ncores=NCORES)
    nc = _NC_CACHE["nc"]

    wq = np.ascontiguousarray(np.asarray(L0_attn_Wq, dtype=np.float32))
    wk = np.ascontiguousarray(np.asarray(L0_attn_Wk, dtype=np.float32))
    wv = np.ascontiguousarray(np.asarray(L0_attn_Wv, dtype=np.float32))
    wo = np.ascontiguousarray(np.asarray(L0_attn_Wo, dtype=np.float32))
    gq = np.ascontiguousarray(np.asarray(L0_ffn_gate_q, dtype=np.int32))
    gs = np.ascontiguousarray(np.asarray(L0_ffn_gate_s, dtype=np.float32))
    gz = np.ascontiguousarray(np.asarray(L0_ffn_gate_z, dtype=np.float32))
    uq = np.ascontiguousarray(np.asarray(L0_ffn_up_q, dtype=np.int32))
    us = np.ascontiguousarray(np.asarray(L0_ffn_up_s, dtype=np.float32))
    uz = np.ascontiguousarray(np.asarray(L0_ffn_up_z, dtype=np.float32))
    dq_ = np.ascontiguousarray(np.asarray(L0_ffn_down_q, dtype=np.int32))
    dsx = np.ascontiguousarray(np.asarray(L0_ffn_down_s, dtype=np.float32))
    dzx = np.ascontiguousarray(np.asarray(L0_ffn_down_z, dtype=np.float32))

    in_maps = []
    for c in range(NCORES):
        b, k = c // CHUNKS, c % CHUNKS
        t0 = k * N
        tpos = np.arange(t0 - 256, t0 + N + 256, dtype=np.float32).reshape(1, EE)
        km = np.zeros(EE, np.float32)
        if k == 0:
            km[0:256] = -1e30
        if k == CHUNKS - 1:
            km[EE - 256:EE] = -1e30
        km = km.reshape(20, 128)
        flo = np.array([[0.0 if k == 0 else 1.0]], dtype=np.float32)
        in_maps.append({
            "x": np.ascontiguousarray(x[b, t0:t0 + N, :]),
            "wq": wq, "wk": wk, "wv": wv, "wo": wo,
            "gq": gq, "gs": gs, "gz": gz,
            "uq": uq, "us": us, "uz": uz,
            "dq": dq_, "ds": dsx, "dz": dzx,
            "tpos": tpos, "kmask": km, "flo": flo,
        })

    res = run_bass_kernel_spmd(nc, in_maps, core_ids=list(range(NCORES)))
    out = np.empty((BB, SS, D), dtype=np.float32)
    for c in range(NCORES):
        b, k = c // CHUNKS, c % CHUNKS
        out[b, k * N:(k + 1) * N, :] = res.results[c]["out"]
    return out


# version 02
# speedup vs baseline: 1.1097x; 1.1097x over previous
import sys, os, time
sys.path.insert(0, "/opt/trn_rl_repo")
import numpy as np
import ml_dtypes
import concourse.bass as bass
import concourse.mybir as mybir
from concourse.bass_utils import run_bass_kernel_spmd

F32 = mybir.dt.float32
BF16 = mybir.dt.bfloat16
I32 = mybir.dt.int32
AF = mybir.ActivationFunctionType
ALU = mybir.AluOpType
bfd = ml_dtypes.bfloat16

D, I = 4096, 11008
DC, ICN = 32, 86
T, NT = 512, 4
N = T * NT
EPS = 1e-5
KB = 1024

DWN_SLICES = [(k * 1024, 1024) for k in range(10)] + [(10240, 768)]
GUP_SLICES = [(k * 1024, 1024) for k in range(4)]


class Ctr:
    def __init__(self, nc, name):
        self.sem = nc.alloc_semaphore(name)
        self.n = 0

    def inc(self, k=1):
        self.n += k
        return self.n


class DS(Ctr):
    pass


class Prog:
    ENGS = ("sync", "gpsimd", "vector", "scalar", "tensor")

    def __init__(self):
        self.streams = {e: [] for e in self.ENGS}

    def em(self, eng, fn):
        self.streams[eng].append(fn)

    def install(self, nc):
        with nc.Block() as block:
            @block.sync
            def _(eng):
                for fn in self.streams["sync"]:
                    fn(eng)

            @block.gpsimd
            def _(eng):
                for fn in self.streams["gpsimd"]:
                    fn(eng)

            @block.vector
            def _(eng):
                for fn in self.streams["vector"]:
                    fn(eng)

            @block.scalar
            def _(eng):
                for fn in self.streams["scalar"]:
                    fn(eng)

            @block.tensor
            def _(eng):
                for fn in self.streams["tensor"]:
                    fn(eng)


def alloc_buffers(nc):
    B = {}
    BASE = 16512

    def A(name, shape, dtype, offset):
        return nc.alloc_sbuf_tensor_at(name, shape, dtype, offset=BASE + offset)

    B["xn"] = A("xn", [128, DC * N], BF16, offset=0)
    B["h_half"] = [A(f"hh{i}", [128, 43 * T], BF16, offset=i * 43 * KB) for i in range(2)]
    B["xr"] = [A(f"xr{i}", [128, T], F32, offset=86 * KB + i * 2 * KB) for i in range(4)]
    B["dnq"] = [A(f"dnq{i}", [128, 1024], BF16, offset=94 * KB + i * 2 * KB)
                for i in range(2)]
    B["dns"] = [A(f"dns{i}", [128, 1024], BF16, offset=98 * KB + i * 2 * KB)
                for i in range(2)]
    B["dnz"] = [A(f"dnz{i}", [128, 1024], BF16, offset=102 * KB + i * 2 * KB)
                for i in range(2)]
    B["dnt"] = A("dnt", [128, 1024], F32, offset=106 * KB)
    B["ig_x"] = [A(f"igx{i}", [128, D], F32, offset=i * 16 * KB) for i in range(2)]
    B["ig_st"] = [A(f"igst{i}", [128, 512], F32, offset=32 * KB + i * 4 * KB)
                  for i in range(2)]
    B["szrow"] = A("szrow", [86, D], F32, offset=64 * KB)
    B["szrow2"] = A("szrow2", [86, D], F32, offset=80 * KB)

    B["pan_g"] = [A(f"pang{i}", [128, D], BF16, offset=128 * KB + i * 8 * KB)
                  for i in range(2)]
    B["pan_u"] = [A(f"panu{i}", [128, D], BF16, offset=144 * KB + i * 8 * KB)
                  for i in range(2)]
    B["pan_d"] = [A(f"pand{i}", [128, I], BF16, offset=128 * KB + i * 22 * KB)
                  for i in range(2)]
    B["guq"] = [A(f"guq{i}", [128, 1024], BF16, offset=160 * KB + i * 2 * KB)
                for i in range(2)]
    B["gus"] = [A(f"gus{i}", [128, 1024], BF16, offset=164 * KB + i * 2 * KB)
                for i in range(2)]
    B["guz"] = [A(f"guz{i}", [128, 1024], BF16, offset=168 * KB + i * 2 * KB)
                for i in range(2)]
    B["gut"] = A("gut", [128, 1024], F32, offset=172 * KB)

    B["ln_x"] = [A(f"lnx{i}", [128, T], F32, offset=180 * KB + i * 2 * KB)
                 for i in range(2)]
    B["ln_sq"] = [A(f"lnsq{i}", [128, T], BF16, offset=184 * KB + i * KB)
                  for i in range(2)]
    B["ln_t"] = A("ln_t", [128, T], F32, offset=186 * KB)
    B["silu"] = [A(f"silu{i}", [128, T], F32, offset=188 * KB + i * 2 * KB)
                 for i in range(2)]
    B["h_out"] = [A(f"hout{i}", [128, T], BF16, offset=192 * KB + i * KB)
                  for i in range(4)]
    B["rows"] = A("rows", [1, 4 * T], F32, offset=196 * KB)
    B["ones_f"] = A("ones_f", [1, 128], F32, offset=204 * KB)
    B["ones_b"] = A("ones_b", [128, 1], BF16, offset=204 * KB + 512)
    B["eye_f"] = A("eye_f", [128, 128], F32, offset=204 * KB + 544)
    B["zeros128"] = A("zeros128", [128, 128], F32, offset=205 * KB + 64)

    B["xn_t"] = A("xn_t", [128, DC * T], BF16, offset=0)
    B["qr_t"] = A("qr_t", [128, 32 * T], BF16, offset=32 * KB)
    B["q_roll"] = A("q_roll", [128, 32 * 768], BF16, offset=0)
    B["att_o"] = A("att_o", [128, 32 * T], BF16, offset=48 * KB)
    B["carry"] = A("carry", [128, 32 * 256], BF16, offset=80 * KB)
    B["ab"] = A("ab", [128, 4 * 256], BF16, offset=96 * KB)
    B["mrow"] = A("mrow", [1, 512], F32, offset=98 * KB)
    B["rrow"] = A("rrow", [1, 1024], F32, offset=100 * KB)
    B["jrow"] = A("jrow", [1, 128], F32, offset=104 * KB)
    B["flo"] = A("flo", [1, 8], F32, offset=104 * KB + 512)
    B["tpos"] = A("tpos", [1, 2560], F32, offset=32 * KB)
    B["arows"] = A("arows", [1, 1024], F32, offset=123 * KB)
    B["wpan"] = [A(f"wpan{i}", [128, D], BF16, offset=128 * KB + i * 8 * KB)
                 for i in range(6)]
    B["k_ext"] = A("k_ext", [128, 2560], BF16, offset=115 * KB)
    B["v_ext"] = A("v_ext", [128, 2560], BF16, offset=120 * KB)
    B["cos_e"] = A("cos_e", [128, 2560], BF16, offset=64 * KB)
    B["sinr_e"] = A("sinr_e", [128, 2560], BF16, offset=69 * KB)
    B["wkT"] = A("wkT", [128, D], BF16, offset=74 * KB)
    B["wvT"] = A("wvT", [128, D], BF16, offset=82 * KB)
    B["axr"] = [A(f"axr{i}", [128, T], F32, offset=111 * KB + i * 2 * KB)
                for i in range(2)]
    B["sig"] = [A(f"sig{i}", [128, T], BF16, offset=105 * KB + i * KB)
                for i in range(4)]
    B["w_sb"] = [A(f"wsb{i}", [128, T], BF16, offset=109 * KB + i * KB)
                 for i in range(2)]
    B["rope_f"] = A("rope_f", [128, T], F32, offset=188 * KB)
    B["rope_g"] = A("rope_g", [128, T], F32, offset=190 * KB)
    B["rope_i"] = A("rope_i", [128, T], I32, offset=192 * KB)
    B["qc_t"] = A("qc_t", [128, T], F32, offset=188 * KB)
    B["qs_t"] = A("qs_t", [128, T], F32, offset=190 * KB)
    B["kmask"] = A("kmask", [20, 128], F32, offset=205 * KB + 640)
    B["kmask_c"] = A("kmask_c", [128, 20], F32, offset=206 * KB + 128)
    B["eye_b"] = A("eye_b", [128, 128], BF16, offset=206 * KB + 256)

    def row(i):
        return B["rows"][0:1, i * T:(i + 1) * T]
    B["r_mu"], B["r_var"], B["r_t1"], B["r_r"] = row(0), row(1), row(2), row(3)

    B["p_bank"] = [nc.alloc_psum_tensor(f"pb{i}", [128, T], F32) for i in range(8)]
    B["p_g"] = B["p_bank"][0:2]
    B["p_u"] = B["p_bank"][2:4]
    B["p_sx"] = B["p_bank"][4][0:1, :]
    B["p_sq"] = B["p_bank"][5][0:1, :]
    B["p_mu"] = B["p_bank"][6]
    B["p_r"] = B["p_bank"][7]
    B["p_y"] = B["p_g"]
    B["p_ing"] = [B["p_g"][0], B["p_u"][0]]
    return B


def emit_p0(P, S, B, ds_ap, dz_ap, ds2, dz2, d_eyef):
    VE = S["VE"]
    p0a, p0b = S["p0a"], S["p0b"]
    a0 = p0a.inc(32)
    def sld(e):
        e.dma_start(out=B["szrow"][:], in_=ds_ap).then_inc(p0a.sem, 16)
        e.dma_start(out=B["szrow2"][:], in_=dz_ap).then_inc(p0a.sem, 16)
    P.em("sync", sld)

    def v(e, a0=a0):
        e.memset(B["ones_f"][:], 1.0)
        e.memset(B["ones_b"][:], 1.0)
        e.memset(B["zeros128"][:], 0.0)
        e.wait_ge(p0a.sem, a0)
        e.tensor_scalar(B["szrow"][:], B["szrow"][:], 0.5, None, ALU.mult)
        e.tensor_scalar(B["szrow2"][:], B["szrow2"][:], 0.5, None,
                        ALU.mult).then_inc(VE.sem, 1)
    P.em("vector", v)
    v0 = VE.inc(1)

    b0 = p0b.inc(48)
    def g(e, v0=v0, b0=b0):
        e.wait_ge(VE.sem, v0)
        e.dma_start(out=ds2, in_=B["szrow"][:]).then_inc(p0b.sem, 16)
        e.dma_start(out=dz2, in_=B["szrow2"][:]).then_inc(p0b.sem, 16)
        e.dma_start(out=d_eyef, in_=B["zeros128"][:]).then_inc(p0b.sem, 16)
        e.wait_ge(p0b.sem, b0)
        diag = bass.AP(d_eyef.tensor, 0, [[129, 128]])
        e.dma_start(out=diag, in_=B["ones_f"][:],
                    allow_slow_non_contiguous=True).then_inc(p0b.sem, 16)
    P.em("gpsimd", g)
    b1 = p0b.inc(16)

    p0c = S["p0c"]
    b2 = p0c.inc(16)
    def sld2(e, b1=b1):
        e.wait_ge(p0b.sem, b1)
        e.dma_start(out=B["eye_f"][:], in_=d_eyef).then_inc(p0c.sem, 16)
    P.em("sync", sld2)
    return {"sz_p0b": b1, "eye_p0b": b2}


def emit_ingress(P, S, B, x_in, xfm, p0m):
    PE, VE = S["PE"], S["VE"]
    igx, igst = S["igx"], S["igst"]
    NSUB = N // 128
    pe_sub = {}
    ev_marks = []
    st_marks = []
    gi = 0
    for s in range(NSUB):
        slot = s % 2
        xm = igx[slot].inc(16)
        def gld(e, s=s, slot=slot):
            if s >= 2:
                e.wait_ge(PE.sem, pe_sub[s - 2])
            e.dma_start(out=B["ig_x"][slot][:],
                        in_=x_in[s * 128:(s + 1) * 128, :]).then_inc(igx[slot].sem, 16)
        P.em("gpsimd", gld)

        for grp in range(8):
            def t(e, s=s, slot=slot, grp=grp, gi=gi, xm=xm):
                if grp == 0:
                    e.wait_ge(igx[slot].sem, xm)
                    if s == 0:
                        e.wait_ge(S["p0c"].sem, p0m["eye_p0b"])
                if gi >= 2:
                    e.wait_ge(VE.sem, ev_marks[gi - 2])
                pg = B["p_ing"][gi % 2]
                ins = None
                for k in range(4):
                    dc = grp * 4 + k
                    ins = e.transpose(pg[:, k * 128:(k + 1) * 128],
                                      B["ig_x"][slot][:, dc * 128:(dc + 1) * 128],
                                      B["eye_f"][:])
                ins.then_inc(PE.sem, 1)
            P.em("tensor", t)
            PE.inc(1)
            if grp == 7:
                pe_sub[s] = PE.n

            def v(e, gi=gi, pm=PE.n,
                  stm=(st_marks[gi - 2] if gi >= 2 else None)):
                e.wait_ge(PE.sem, pm)
                if stm is not None:
                    e.wait_ge(igst[gi % 2].sem, stm)
                e.tensor_copy(B["ig_st"][gi % 2][:],
                              B["p_ing"][gi % 2][:, 0:512]).then_inc(VE.sem, 1)
            P.em("vector", v)
            ev_marks.append(VE.inc(1))

            def g2(e, s=s, grp=grp, gi=gi, vm=VE.n):
                e.wait_ge(VE.sem, vm)
                dst = xfm.rearrange("dc p t -> p dc t")[:, grp * 4:(grp + 1) * 4,
                                                        s * 128:(s + 1) * 128]
                src = B["ig_st"][gi % 2][:].rearrange("p (dc j) -> p dc j", dc=4)
                e.dma_start(out=dst, in_=src).then_inc(igst[gi % 2].sem, 16)
            P.em("gpsimd", g2)
            st_marks.append(igst[gi % 2].inc(16))
            gi += 1
    return {"wr_done": [(igst[0].sem, igst[0].n), (igst[1].sem, igst[1].n)],
            "last_ev": ev_marks[-1]}


def emit_ffn(P, S, B, xfm, h_scr, caches, qsrc, mode, dep_marks, psum_free_ve,
             stage="full", dump_ap=None, pan_gate_pe=None):
    PE, VE, AC = S["PE"], S["VE"], S["AC"]
    lnx, dqs = S["lnx"], S["dqs"]

    ln_parked, ln_mm, ln_norm = [], [], []
    vrow_marks = []
    for t in range(NT):
        for dc in range(DC):
            i = t * DC + dc
            xm = lnx[i % 2].inc(16)
            def gld(e, t=t, dc=dc, i=i):
                if i == 0:
                    for sm, vv_ in dep_marks:
                        e.wait_ge(sm, vv_)
                if i >= 2:
                    e.wait_ge(VE.sem, ln_parked[i - 2])
                e.dma_start(out=B["ln_x"][i % 2][:],
                            in_=xfm[dc, :, t * T:(t + 1) * T]).then_inc(lnx[i % 2].sem, 16)
            P.em("gpsimd", gld)

            def vpark(e, t=t, dc=dc, i=i, xm=xm):
                e.wait_ge(lnx[i % 2].sem, xm)
                e.tensor_copy(B["xn"][:, dc * N + t * T: dc * N + t * T + T],
                              B["ln_x"][i % 2][:]).then_inc(VE.sem, 1)
            P.em("vector", vpark)
            ln_parked.append(VE.inc(1))

            def apark(e, t=t, dc=dc, i=i, vm=VE.n):
                e.wait_ge(VE.sem, vm)
                if i >= 2:
                    e.wait_ge(PE.sem, ln_mm[i - 2])
                e.activation(B["ln_sq"][i % 2][:],
                             B["xn"][:, dc * N + t * T: dc * N + t * T + T],
                             AF.Square).then_inc(AC.sem, 1)
            P.em("scalar", apark)
            AC.inc(1)

            vrm = vrow_marks[t - 1] if (t >= 1 and dc == 0) else None

            def tmm(e, t=t, dc=dc, i=i, am=AC.n, vrm=vrm):
                e.wait_ge(AC.sem, am)
                if vrm is not None:
                    e.wait_ge(VE.sem, vrm)
                if i == 0:
                    e.wait_ge(VE.sem, psum_free_ve)
                e.matmul(B["p_sx"], B["ones_b"][:],
                         B["xn"][:, dc * N + t * T: dc * N + t * T + T],
                         start=(dc == 0), stop=(dc == DC - 1))
                e.matmul(B["p_sq"], B["ones_b"][:], B["ln_sq"][i % 2][:],
                         start=(dc == 0), stop=(dc == DC - 1)).then_inc(PE.sem, 1)
            P.em("tensor", tmm)
            ln_mm.append(PE.inc(1))

        v_base = VE.n
        arow_prev = AC.n if t >= 1 else None

        def vrow(e, t=t, pm=PE.n, v_base=v_base, arow_prev=arow_prev):
            e.wait_ge(PE.sem, pm)
            if arow_prev is not None:
                e.wait_ge(AC.sem, arow_prev)
            e.tensor_scalar(B["r_mu"], B["p_sx"], 1.0 / D, None,
                            ALU.mult).then_inc(VE.sem, 1)
            e.wait_ge(VE.sem, v_base + 1)
            e.tensor_tensor(B["r_t1"], B["r_mu"], B["r_mu"],
                            ALU.mult).then_inc(VE.sem, 1)
            e.wait_ge(VE.sem, v_base + 2)
            e.tensor_scalar(B["r_t1"], B["r_t1"], EPS, None,
                            ALU.subtract).then_inc(VE.sem, 1)
            e.wait_ge(VE.sem, v_base + 3)
            e.scalar_tensor_tensor(B["r_var"], B["p_sq"], 1.0 / D,
                                   B["r_t1"], ALU.mult,
                                   ALU.subtract).then_inc(VE.sem, 1)
        P.em("vector", vrow)
        VE.inc(4)
        vrow_marks.append(VE.n)

        a_base = AC.n

        def arow(e, t=t, vm=VE.n, a_base=a_base):
            e.wait_ge(VE.sem, vm)
            e.activation(B["r_t1"], B["r_var"], AF.Ln).then_inc(AC.sem, 1)
            e.wait_ge(AC.sem, a_base + 1)
            e.activation(B["r_r"], B["r_t1"], AF.Exp,
                         scale=-0.5).then_inc(AC.sem, 1)
        P.em("scalar", arow)
        AC.inc(2)

        def tbc(e, t=t, am=AC.n):
            e.wait_ge(AC.sem, am)
            if t >= 1:
                e.wait_ge(VE.sem, ln_norm[t * DC - 1])
            e.matmul(B["p_mu"][:], B["ones_f"][:], B["r_mu"], start=True, stop=True)
            e.matmul(B["p_r"][:], B["ones_f"][:], B["r_r"],
                     start=True, stop=True).then_inc(PE.sem, 1)
        P.em("tensor", tbc)
        bc_pe = PE.inc(1)

        for dc in range(DC):
            v_base2 = VE.n

            def vn(e, t=t, dc=dc, pm=bc_pe, v_base2=v_base2):
                if dc == 0:
                    e.wait_ge(PE.sem, pm)
                e.wait_ge(VE.sem, v_base2)
                sl = slice(dc * N + t * T, dc * N + t * T + T)
                e.tensor_tensor(B["ln_t"][:], B["xn"][:, sl], B["p_mu"][:],
                                ALU.subtract).then_inc(VE.sem, 1)
                e.wait_ge(VE.sem, v_base2 + 1)
                e.tensor_tensor(B["xn"][:, sl], B["ln_t"][:], B["p_r"][:],
                                ALU.mult).then_inc(VE.sem, 1)
            P.em("vector", vn)
            VE.inc(2)
            ln_norm.append(VE.n)

    if stage == "ln":
        marks = []
        rs = S["rsw"][0]
        for dc in range(DC):
            def gdump(e, dc=dc, vm=ln_norm[-1], base=rs.n):
                e.wait_ge(VE.sem, vm)
                e.wait_ge(rs.sem, base)
                e.dma_start(out=dump_ap[dc * 128:(dc + 1) * 128, :],
                            in_=B["xn"][:, dc * N:(dc + 1) * N]).then_inc(rs.sem, 16)
            P.em("gpsimd", gdump)
            marks.append(rs.inc(16))
        return {"res_done": [(rs.sem, rs.n)], "last_yev": 0}

    pan_ready = {}
    gu_done = []
    dq_hist = []
    gu_hist, dn_hist = [], []
    gu_cnt, dn_cnt = [0], [0]
    dqs2 = S["dqs2"]
    cw_marks = {}

    def produce_gu(kind, idx):
        slot = idx % 2
        prev_pe = None
        if idx >= 2:
            prev_pe = gu_done[(idx - 2) * NT + NT - 1]
        psem = S["pang"][slot] if kind == "g" else S["panu"][slot]
        if mode == "cached":
            m = psem.inc(16)
            def sld(e, kind=kind, idx=idx, slot=slot, prev_pe=prev_pe, psem=psem,
                    pg=(pan_gate_pe if idx < 2 else None)):
                if pg is not None:
                    e.wait_ge(PE.sem, pg)
                if prev_pe is not None:
                    e.wait_ge(PE.sem, prev_pe)
                e.dma_start(out=B["pan_" + kind][slot][:],
                            in_=caches[kind][idx]).then_inc(psem.sem, 16)
            P.em("sync", sld)
            pan_ready[(kind, idx)] = (psem.sem, m)
            return

        qap = qsrc[kind + "q"]
        sap = qsrc[kind + "s"]
        zap = qsrc[kind + "z"]
        cwsem = (S["cwg"] if kind == "g" else S["cwu"])[slot]
        cwm_prev = cw_marks.get((kind, idx - 2))
        for sl, (c0, cn) in enumerate(GUP_SLICES):
            gl0, gn = c0 // 128, cn // 128
            q_in = qap.rearrange("(g p) (ic j) -> ic p g j", p=128, j=128)[idx, :, gl0:gl0 + gn, :]
            s_in = sap.rearrange("g (ic j) -> ic g j", j=128)[idx, gl0:gl0 + gn, :]\
                      .unsqueeze(0).partition_broadcast(128)
            z_in = zap.rearrange("g (ic j) -> ic g j", j=128)[idx, gl0:gl0 + gn, :]\
                      .unsqueeze(0).partition_broadcast(128)
            st = gu_cnt[0] % 2
            stg_free = gu_hist[-2] if len(gu_hist) >= 2 else None
            gu_cnt[0] += 1

            d0 = dqs[st].inc(48)
            def gq(e, q_in=q_in, s_in=s_in, z_in=z_in, stg_free=stg_free, st=st):
                if stg_free is not None:
                    e.wait_ge(VE.sem, stg_free)
                e.dma_start(out=B["guq"][st][:], in_=q_in).then_inc(dqs[st].sem, 16)
                e.dma_start(out=B["gus"][st][:], in_=s_in).then_inc(dqs[st].sem, 16)
                e.dma_start(out=B["guz"][st][:], in_=z_in).then_inc(dqs[st].sem, 16)
            P.em("gpsimd", gq)

            v_base3 = VE.n

            def vdq(e, kind=kind, slot=slot, c0=c0, d0=d0, prev_pe=prev_pe, sl=sl,
                    v_base3=v_base3, st=st):
                e.wait_ge(dqs[st].sem, d0)
                e.wait_ge(VE.sem, v_base3)
                if sl == 0 and prev_pe is not None:
                    e.wait_ge(PE.sem, prev_pe)
                if sl == 0 and cwm_prev is not None:
                    e.wait_ge(cwsem.sem, cwm_prev)
                e.scalar_tensor_tensor(B["gut"][:], B["guq"][st][:], -8.0,
                                       B["gus"][st][:], ALU.add,
                                       ALU.mult).then_inc(VE.sem, 1)
                e.wait_ge(VE.sem, v_base3 + 1)
                e.tensor_tensor(B["pan_" + kind][slot][:, c0:c0 + 1024],
                                B["gut"][:], B["guz"][st][:],
                                ALU.add).then_inc(VE.sem, 1)
            P.em("vector", vdq)
            VE.inc(2)
            gu_hist.append(VE.n)
            dq_hist.append(VE.n)
        pan_ready[(kind, idx)] = (VE.sem, VE.n)

        def gw(e, kind=kind, idx=idx, slot=slot, vm=VE.n, cwsem=cwsem):
            e.wait_ge(VE.sem, vm)
            e.dma_start(out=caches[kind][idx],
                        in_=B["pan_" + kind][slot][:]).then_inc(cwsem.sem, 16)
        P.em("gpsimd", gw)
        cw_marks[(kind, idx)] = cwsem.inc(16)

    down_mm = {}

    def produce_d(idx, pe_gate=None):
        slot = idx % 2
        prev_pe = down_mm.get(idx - 2)
        psem = S["pand"][slot]
        gu_cw_finals = ([(S["cwg"][k].sem, S["cwg"][k].n) for k in range(2)] +
                        [(S["cwu"][k].sem, S["cwu"][k].n) for k in range(2)]) \
            if (mode == "dequant" and idx < 2) else None
        if mode == "cached":
            m = psem.inc(16)
            def sld(e, idx=idx, slot=slot, prev_pe=prev_pe, pe_gate=pe_gate, psem=psem):
                if pe_gate is not None:
                    e.wait_ge(PE.sem, pe_gate)
                if prev_pe is not None:
                    e.wait_ge(PE.sem, prev_pe)
                e.dma_start(out=B["pan_d"][slot][:],
                            in_=caches["d"][idx]).then_inc(psem.sem, 16)
            P.em("sync", sld)
            pan_ready[("d", idx)] = (psem.sem, m)
            return
        qap, sap, zap = qsrc["dq"], qsrc["ds2"], qsrc["dz2"]
        cwsem = S["cwd"][slot]
        cwm_prev = cw_marks.get(("d", idx - 2))
        for sl, (c0, cn) in enumerate(DWN_SLICES):
            gl0, gn = c0 // 128, cn // 128
            q_in = qap.rearrange("(g p) (ic j) -> ic p g j", p=128, j=128)[idx, :, gl0:gl0 + gn, :]
            s_in = sap.rearrange("g (ic j) -> ic g j", j=128)[idx, gl0:gl0 + gn, :]\
                      .unsqueeze(0).partition_broadcast(128)
            z_in = zap.rearrange("g (ic j) -> ic g j", j=128)[idx, gl0:gl0 + gn, :]\
                      .unsqueeze(0).partition_broadcast(128)
            st = dn_cnt[0] % 2
            stg_free = dn_hist[-2] if len(dn_hist) >= 2 else None
            dn_cnt[0] += 1

            d0 = dqs2[st].inc(48)
            def gq(e, q_in=q_in, s_in=s_in, z_in=z_in, stg_free=stg_free, cn=cn,
                   pe_gate=pe_gate, sz=p0m_sz, st=st):
                e.wait_ge(sz[0], sz[1])
                if pe_gate is not None:
                    e.wait_ge(PE.sem, pe_gate)
                if stg_free is not None:
                    e.wait_ge(VE.sem, stg_free)
                e.dma_start(out=B["dnq"][st][:, 0:cn], in_=q_in).then_inc(dqs2[st].sem, 16)
                e.dma_start(out=B["dns"][st][:, 0:cn], in_=s_in).then_inc(dqs2[st].sem, 16)
                e.dma_start(out=B["dnz"][st][:, 0:cn], in_=z_in).then_inc(dqs2[st].sem, 16)
            P.em("gpsimd", gq)

            v_base4 = VE.n

            def vdq(e, slot=slot, c0=c0, cn=cn, d0=d0, prev_pe=prev_pe, sl=sl,
                    v_base4=v_base4, gu_cw_finals=gu_cw_finals, st=st):
                e.wait_ge(dqs2[st].sem, d0)
                e.wait_ge(VE.sem, v_base4)
                if sl == 0 and prev_pe is not None:
                    e.wait_ge(PE.sem, prev_pe)
                if sl == 0 and cwm_prev is not None:
                    e.wait_ge(cwsem.sem, cwm_prev)
                if sl == 0 and gu_cw_finals is not None:
                    for sm, vv_ in gu_cw_finals:
                        e.wait_ge(sm, vv_)
                e.scalar_tensor_tensor(B["dnt"][:, 0:cn], B["dnq"][st][:, 0:cn], -8.0,
                                       B["dns"][st][:, 0:cn], ALU.add,
                                       ALU.mult).then_inc(VE.sem, 1)
                e.wait_ge(VE.sem, v_base4 + 1)
                e.tensor_tensor(B["pan_d"][slot][:, c0:c0 + cn],
                                B["dnt"][:, 0:cn], B["dnz"][st][:, 0:cn],
                                ALU.add).then_inc(VE.sem, 1)
            P.em("vector", vdq)
            VE.inc(2)
            dn_hist.append(VE.n)
            dq_hist.append(VE.n)
        pan_ready[("d", idx)] = (VE.sem, VE.n)

        def gw(e, idx=idx, slot=slot, vm=VE.n, cwsem=cwsem):
            e.wait_ge(VE.sem, vm)
            e.dma_start(out=caches["d"][idx],
                        in_=B["pan_d"][slot][:]).then_inc(cwsem.sem, 16)
        P.em("gpsimd", gw)
        cw_marks[("d", idx)] = cwsem.inc(16)

    if stage == "deq":
        produce_gu("g", 0)
        produce_gu("u", 0)
        rs = S["rsw"][0]
        semg, vg = pan_ready[("g", 0)]
        semu, vu = pan_ready[("u", 0)]
        def gdumpg(e, vg=vg, semg=semg):
            e.wait_ge(semg, vg)
            e.dma_start(out=dump_ap[0:256, 0:2048].rearrange("(a p) c -> p a c", p=128),
                        in_=B["pan_g"][0][:].rearrange("p (a c) -> p a c", a=2)).then_inc(rs.sem, 16)
        P.em("gpsimd", gdumpg)
        rs.inc(16)
        def gdumpu(e, vu=vu, semu=semu):
            e.wait_ge(semu, vu)
            e.dma_start(out=dump_ap[256:512, 0:2048].rearrange("(a p) c -> p a c", p=128),
                        in_=B["pan_u"][0][:].rearrange("p (a c) -> p a c", a=2)).then_inc(rs.sem, 16)
        P.em("gpsimd", gdumpu)
        return {"res_done": [(rs.sem, rs.inc(16))], "last_yev": 0}

    hep, hst_marks = [], []
    hstm = S["hstm"]
    for ic in range(min(2, ICN)):
        produce_gu("g", ic)
        produce_gu("u", ic)

    for ic in range(ICN):
        semg, vg = pan_ready[("g", ic)]
        semu, vu = pan_ready[("u", ic)]
        for t in range(NT):
            gi = ic * NT + t
            def tmm(e, ic=ic, t=t, gi=gi, semg=semg, vg=vg, semu=semu, vu=vu):
                if t == 0:
                    e.wait_ge(semg, vg)
                    e.wait_ge(semu, vu)
                if ic == 0:
                    e.wait_ge(VE.sem, ln_norm[(t + 1) * DC - 1])
                if gi >= 2:
                    e.wait_ge(VE.sem, hep[gi - 2])
                pg, pu = B["p_g"][t % 2], B["p_u"][t % 2]
                for dc in range(DC):
                    rhs = B["xn"][:, dc * N + t * T: dc * N + t * T + T]
                    e.matmul(pg[:], B["pan_g"][ic % 2][:, dc * 128:(dc + 1) * 128], rhs,
                             start=(dc == 0), stop=(dc == DC - 1))
                ins = None
                for dc in range(DC):
                    rhs = B["xn"][:, dc * N + t * T: dc * N + t * T + T]
                    ins = e.matmul(pu[:], B["pan_u"][ic % 2][:, dc * 128:(dc + 1) * 128],
                                   rhs, start=(dc == 0), stop=(dc == DC - 1))
                ins.then_inc(PE.sem, 1)
            P.em("tensor", tmm)
            gu_done.append(PE.inc(1))

            def aep(e, t=t, gi=gi, pm=PE.n,
                    hv=(hep[gi - 2] if gi >= 2 else None)):
                e.wait_ge(PE.sem, pm)
                if hv is not None:
                    e.wait_ge(VE.sem, hv)
                e.activation(B["silu"][gi % 2][:], B["p_g"][t % 2][:],
                             AF.Silu).then_inc(AC.sem, 1)
            P.em("scalar", aep)
            AC.inc(1)

            def vep(e, t=t, gi=gi, am=AC.n,
                    hsm=(hst_marks[gi - 4] if gi >= 4 else None)):
                e.wait_ge(AC.sem, am)
                if hsm is not None:
                    e.wait_ge(hstm[gi % 4].sem, hsm)
                e.tensor_tensor(B["h_out"][gi % 4][:], B["silu"][gi % 2][:],
                                B["p_u"][t % 2][:], ALU.mult).then_inc(VE.sem, 1)
            P.em("vector", vep)
            hep.append(VE.inc(1))

            def ghw(e, ic=ic, t=t, gi=gi, vm=VE.n):
                e.wait_ge(VE.sem, vm)
                e.dma_start(out=h_scr[ic, :, t * T:(t + 1) * T],
                            in_=B["h_out"][gi % 4][:]).then_inc(hstm[gi % 4].sem, 16)
            P.em("gpsimd", ghw)
            hst_marks.append(hstm[gi % 4].inc(16))

        if ic + 2 < ICN:
            produce_gu("g", ic + 2)
            produce_gu("u", ic + 2)

    if stage == "gate":
        marks = []
        for ic in range(DC):
            rs = S["rbl"][ic % 2]
            def gdump(e, ic=ic, rs=rs,
                      hm=[(hstm[k].sem, hstm[k].n) for k in range(4)],
                      pv=(marks[ic - 2] if ic >= 2 else None), base=rs.n):
                for sm, vv_ in hm:
                    e.wait_ge(sm, vv_)
                if pv is not None:
                    e.wait_ge(rs.sem, pv)
                e.dma_start(out=B["ln_x"][ic % 2][:],
                            in_=h_scr[ic, :, 0:T]).then_inc(rs.sem, 16)
                e.wait_ge(rs.sem, base + 16)
                e.dma_start(out=dump_ap[ic * 128:(ic + 1) * 128, 0:T],
                            in_=B["ln_x"][ic % 2][:]).then_inc(rs.sem, 16)
            P.em("gpsimd", gdump)
            marks.append(rs.inc(32))
        return {"res_done": [(S["rbl"][0].sem, S["rbl"][0].n),
                             (S["rbl"][1].sem, S["rbl"][1].n)], "last_yev": 0}

    if mode == "dequant":
        produce_d(0, pe_gate=gu_done[-1])
        produce_d(1, pe_gate=gu_done[-1])
    yev = []
    hld = S["hld"]
    xrl = S["xrl"]
    rsw = S["rsw"]
    res_marks = []
    hload = {}
    down_tile_last = {}
    hst_all = [(hstm[k].sem, hstm[k].n) for k in range(4)]
    for t in range(NT):
        for half in range(2):
            lo = 0 if half == 0 else 43
            def ghl(e, t=t, half=half, lo=lo):
                if t >= 1:
                    e.wait_ge(PE.sem, down_tile_last[t - 1])
                if t == 0 and half == 0:
                    for sm, vv_ in hst_all:
                        e.wait_ge(sm, vv_)
                src = h_scr[lo:lo + 43, :, t * T:(t + 1) * T].rearrange("ic p t -> p ic t")
                e.dma_start(out=B["h_half"][half][:, 0:21 * T],
                            in_=src[:, 0:21, :]).then_inc(hld[half].sem, 16)
                e.dma_start(out=B["h_half"][half][:, 21 * T:43 * T],
                            in_=src[:, 21:43, :]).then_inc(hld[half].sem, 16)
            P.em("gpsimd", ghl)
            hload[(t, half)] = hld[half].inc(32)

        for dc in range(DC):
            di = t * DC + dc
            if mode == "dequant":
                if t == 0:
                    pass
                else:
                    prev_pe = down_mm.get(di - 2)
                    psem = S["pand"][dc % 2]
                    m = psem.inc(16)
                    def sld(e, dc=dc, prev_pe=prev_pe, psem=psem,
                            cwm=[(S["cwd"][k].sem, S["cwd"][k].n) for k in range(2)]):
                        if prev_pe is not None:
                            e.wait_ge(PE.sem, prev_pe)
                        for sm, vv_ in cwm:
                            e.wait_ge(sm, vv_)
                        e.dma_start(out=B["pan_d"][dc % 2][:],
                                    in_=caches["d"][dc]).then_inc(psem.sem, 16)
                    P.em("sync", sld)
                    pan_ready[("d", dc)] = (psem.sem, m)
            else:
                prev_pe = down_mm.get(di - 2)
                psem = S["pand"][dc % 2]
                m = psem.inc(16)
                pe_gate2 = gu_done[-1] if di < 2 else None
                def sldc(e, dc=dc, prev_pe=prev_pe, psem=psem, pe_gate2=pe_gate2):
                    if pe_gate2 is not None:
                        e.wait_ge(PE.sem, pe_gate2)
                    if prev_pe is not None:
                        e.wait_ge(PE.sem, prev_pe)
                    e.dma_start(out=B["pan_d"][dc % 2][:],
                                in_=caches["d"][dc]).then_inc(psem.sem, 16)
                P.em("sync", sldc)
                pan_ready[("d", dc)] = (psem.sem, m)

            semd, vd = pan_ready[("d", dc)]
            xm = xrl[di % 4].inc(16)
            def gxr(e, t=t, dc=dc, di=di,
                    rm=(res_marks[di - 4] if di >= 4 else None)):
                if di >= 4:
                    e.wait_ge(VE.sem, yev[di - 4])
                if rm is not None:
                    e.wait_ge(rsw[di % 4].sem, rm)
                e.dma_start(out=B["xr"][di % 4][:],
                            in_=xfm[dc, :, t * T:(t + 1) * T]).then_inc(xrl[di % 4].sem, 16)
            P.em("gpsimd", gxr)

            def tdn(e, t=t, dc=dc, di=di, semd=semd, vd=vd,
                    hm0=hload[(t, 0)], hm1=hload[(t, 1)]):
                e.wait_ge(semd, vd)
                e.wait_ge(hld[0].sem, hm0)
                e.wait_ge(hld[1].sem, hm1)
                if di >= 2:
                    e.wait_ge(VE.sem, yev[di - 2])
                if di == 0:
                    e.wait_ge(VE.sem, hep[-1])
                py = B["p_y"][dc % 2]
                ins = None
                for ic in range(ICN):
                    half = 0 if ic < 43 else 1
                    ofs = (ic - (0 if half == 0 else 43)) * T
                    ins = e.matmul(py[:], B["pan_d"][dc % 2][:, ic * 128:(ic + 1) * 128],
                                   B["h_half"][half][:, ofs:ofs + T],
                                   start=(ic == 0), stop=(ic == ICN - 1))
                ins.then_inc(PE.sem, 1)
            P.em("tensor", tdn)
            down_mm[di] = PE.inc(1)
            if dc == DC - 1:
                down_tile_last[t] = PE.n

            def vy(e, t=t, dc=dc, di=di, pm=PE.n, xm=xm):
                e.wait_ge(PE.sem, pm)
                e.wait_ge(xrl[di % 4].sem, xm)
                e.tensor_tensor(B["xr"][di % 4][:], B["xr"][di % 4][:],
                                B["p_y"][dc % 2][:], ALU.add).then_inc(VE.sem, 1)
            P.em("vector", vy)
            yev.append(VE.inc(1))

            def gyw(e, t=t, dc=dc, di=di, vm=VE.n):
                e.wait_ge(VE.sem, vm)
                e.dma_start(out=xfm[dc, :, t * T:(t + 1) * T],
                            in_=B["xr"][di % 4][:]).then_inc(rsw[di % 4].sem, 16)
            P.em("gpsimd", gyw)
            res_marks.append(rsw[di % 4].inc(16))
            if mode == "dequant" and t == 0 and dc + 2 < DC:
                produce_d(dc + 2)

    return {"res_done": [(rsw[k].sem, rsw[k].n) for k in range(4)],
            "last_yev": yev[-1]}


H = 32
DH = 128
SCALE = 1.0 / float(np.sqrt(128.0))
EE = 2560
L2C = float(np.log2(10000.0) / 64.0)
L_HI = float(np.float32(round(L2C * (1 << 17)) / (1 << 17)))
L_LO = float(np.float32(L2C - L_HI))
LN2 = float(np.log(2.0))
INV2PI = float(1.0 / (2 * np.pi))
C1 = float(np.float32(6.28125))
C2 = float(np.float32(2 * np.pi - 6.28125))
C3 = float(np.float32(2 * np.pi - C1 - float(np.float32(2 * np.pi - 6.28125))))
PI = float(np.pi)


def emit_wstage(P, S, B, wq_ap, wk_ap, wv_ap, wo_ap, wqT_d, woT_d, wk_bf_d, wv_bf_d,
                dep_marks):
    PE, VE, AC = S["PE"], S["VE"], S["AC"]
    ws1, ws2, ws3 = S["ws1"], S["ws2"], S["ws3"]
    def vcast(e):
        e.tensor_copy(B["eye_b"][:], B["eye_f"][:]).then_inc(VE.sem, 1)
    P.em("vector", vcast)
    eyeb_ve = VE.inc(1)

    for wi, (w_ap, wT_d) in enumerate([(wq_ap, wqT_d), (wo_ap, woT_d)]):
        for ch in range(32):
            wch = wi * 32 + ch
            sl = wch % 2
            m = ws1[sl].inc(16)
            def ld(e, w_ap=w_ap, ch=ch, sl=sl, wch=wch,
                   pv=(AC2M.get(wch - 2)), dep_marks=dep_marks):
                if wch == 0:
                    for sm, vv_ in dep_marks:
                        e.wait_ge(sm, vv_)
                if pv is not None:
                    e.wait_ge(S["ac2"][sl].sem, pv)
                e.dma_start(out=B["ig_x"][sl][:],
                            in_=w_ap[ch * 128:(ch + 1) * 128, :]).then_inc(ws1[sl].sem, 16)
            P.em("sync", ld)

            def cst(e, sl=sl, m=m, wch=wch, wv=(WS2M.get(wch - 2))):
                e.wait_ge(ws1[sl].sem, m)
                if wv is not None:
                    e.wait_ge(ws2[sl].sem, wv)
                e.activation(B["wpan"][sl][:], B["ig_x"][sl][:],
                             AF.Copy).then_inc(S["ac2"][sl].sem, 1)
            P.em("scalar", cst)
            AC2M[wch] = S["ac2"][sl].inc(1)

            def st(e, ch=ch, sl=sl, wch=wch, am=AC2M[wch], wbf=(S["wbfd"][wi])):
                e.wait_ge(S["ac2"][sl].sem, am)
                e.dma_start(out=wbf[ch * 128:(ch + 1) * 128, :],
                            in_=B["wpan"][sl][:]).then_inc(ws2[sl].sem, 16)
            P.em("gpsimd", st)
            WS2M[wch] = ws2[sl].inc(16)

    for wi, wT_d in enumerate([wqT_d, woT_d]):
        for dc in range(32):
            tch = wi * 32 + dc
            sl = tch % 2
            m = ws3[sl].inc(16)
            def tld(e, wi=wi, dc=dc, sl=sl, tch=tch,
                    wm=[(ws2[k].sem, ws2[k].n) for k in range(2)],
                    pv=(WS4M.get(tch - 2)), wbf=(S["wbfd"][wi])):
                for sm, vv_ in wm:
                    e.wait_ge(sm, vv_)
                if pv is not None:
                    e.wait_ge(S["ws4"][sl].sem, pv)
                e.dma_start_transpose(B["wpan"][sl][:],
                                      wbf[:, dc * 128:(dc + 1) * 128]).then_inc(ws3[sl].sem, 16)
            P.em("sync", tld)

            def tst(e, dc=dc, sl=sl, m=m, wT_d=wT_d):
                e.wait_ge(ws3[sl].sem, m)
                e.dma_start(out=wT_d[dc * 128:(dc + 1) * 128, :],
                            in_=B["wpan"][sl][:]).then_inc(S["ws4"][sl].sem, 16)
            P.em("gpsimd", tst)
            WS4M[tch] = S["ws4"][sl].inc(16)

    for wi2, (w_ap, wbf) in enumerate([(wk_ap, wk_bf_d), (wv_ap, wv_bf_d)]):
        sl = wi2 % 2
        m = ws1[sl].inc(16)
        def ldk(e, w_ap=w_ap, sl=sl,
                kv=[(S["ws4"][k].sem, S["ws4"][k].n) for k in range(2)]):
            for sm, vv_ in kv:
                e.wait_ge(sm, vv_)
            e.dma_start(out=B["ig_x"][sl][:], in_=w_ap).then_inc(ws1[sl].sem, 16)
        P.em("sync", ldk)

        def cstk(e, sl=sl, m=m):
            e.wait_ge(ws1[sl].sem, m)
            e.activation(B["wpan"][sl][:], B["ig_x"][sl][:],
                         AF.Copy).then_inc(S["ac2"][sl].sem, 1)
        P.em("scalar", cstk)
        am = S["ac2"][sl].inc(1)

        def stk(e, sl=sl, am=am, wbf=wbf):
            e.wait_ge(S["ac2"][sl].sem, am)
            e.dma_start(out=wbf, in_=B["wpan"][sl][:]).then_inc(ws2[sl].sem, 16)
        P.em("gpsimd", stk)
        WSKV.append(ws2[sl].inc(16))

    for wi, (wbf, dst) in enumerate([(wk_bf_d, "wkT"), (wv_bf_d, "wvT")]):
        for dc in range(32):
            m2 = S["ws5"].inc(16)
            def tl2(e, wi=wi, dc=dc, dst=dst, wbf=wbf,
                    kv=[(ws2[k].sem, ws2[k].n) for k in range(2)]):
                if dc == 0:
                    for sm, vv_ in kv:
                        e.wait_ge(sm, vv_)
                e.dma_start_transpose(B[dst][:, dc * 128:(dc + 1) * 128],
                                      wbf[:, dc * 128:(dc + 1) * 128]).then_inc(S["ws5"].sem, 16)
            P.em("sync", tl2)
    return {"wT_done": [(S["ws4"][0].sem, S["ws4"][0].n), (S["ws4"][1].sem, S["ws4"][1].n),
                        (ws2[0].sem, ws2[0].n), (ws2[1].sem, ws2[1].n)],
            "kvT_done": [(S["ws5"].sem, S["ws5"].n)], "eyeb_ve": eyeb_ve}


def emit_rope_alpha(P, S, B, tpos_ap, kmask_ap, flo_ap, eyeb_ve, dep_yev):
    PE, VE, AC = S["PE"], S["VE"], S["AC"]
    rp = S["rp"]
    m0 = rp.inc(48)
    def ld(e):
        e.dma_start(out=B["tpos"][:], in_=tpos_ap).then_inc(rp.sem, 16)
        e.dma_start(out=B["kmask"][:], in_=kmask_ap).then_inc(rp.sem, 16)
        e.dma_start(out=B["flo"][0:1, 0:1], in_=flo_ap).then_inc(rp.sem, 16)
    P.em("gpsimd", ld)

    def gio(e, dy=dep_yev):
        e.wait_ge(S["VE"].sem, dy)
        e.iota(B["jrow"][0:1, :].rearrange("p (a b) -> p a b", b=2), [[1, 64], [0, 2]],
               base=0, channel_multiplier=0, allow_small_or_imprecise_dtypes=True)
        e.iota(B["arows"][0:1, 256:512], [[1, 256]], base=0, channel_multiplier=0,
               allow_small_or_imprecise_dtypes=True).then_inc(S["GP"].sem, 1)
    P.em("gpsimd", gio)
    gp0 = S["GP"].inc(1)

    vb = VE.n
    def vif(e, gp0=gp0, m0=m0, vb=vb):
        e.wait_ge(S["GP"].sem, gp0)
        e.wait_ge(rp.sem, m0)
        J, G_, NF, NI, PH = (B["jrow"][0:1, :], B["rope_g"][0:1, 0:128],
                            B["rope_f"][0:1, 0:128], B["rope_i"][0:1, 0:128],
                            B["rope_f"][0:1, 128:256])
        e.tensor_scalar(G_, J, L_HI, None, ALU.mult).then_inc(VE.sem, 1)
        e.wait_ge(VE.sem, vb + 1)
        e.scalar_tensor_tensor(G_, J, L_LO, G_, ALU.mult, ALU.add).then_inc(VE.sem, 1)
        e.wait_ge(VE.sem, vb + 2)
        e.tensor_scalar(NF, G_, -0.5, 12582912.0, ALU.add, ALU.add).then_inc(VE.sem, 1)
        e.wait_ge(VE.sem, vb + 3)
        e.tensor_scalar(NF, NF, 12582912.0, None, ALU.subtract).then_inc(VE.sem, 1)
        e.wait_ge(VE.sem, vb + 4)
        e.nop().then_inc(VE.sem, 1)
        e.wait_ge(VE.sem, vb + 5)
        e.tensor_tensor(PH, G_, NF, ALU.subtract).then_inc(VE.sem, 1)
        e.wait_ge(VE.sem, vb + 6)
        e.tensor_scalar(PH, PH, -LN2, None, ALU.mult).then_inc(VE.sem, 1)
        e.wait_ge(VE.sem, vb + 7)
        e.memset(G_, 1.0).then_inc(VE.sem, 1)
        k_ = vb + 8
        for kk in range(10, 0, -1):
            e.wait_ge(VE.sem, k_)
            e.tensor_tensor(B["rope_g"][0:1, 128:256], G_, PH, ALU.mult).then_inc(VE.sem, 1)
            k_ += 1
            e.wait_ge(VE.sem, k_)
            e.tensor_scalar(G_, B["rope_g"][0:1, 128:256], 1.0 / kk, 1.0,
                            ALU.mult, ALU.add).then_inc(VE.sem, 1)
            k_ += 1
        e.wait_ge(VE.sem, k_)
        e.tensor_scalar(NF, NF, -8388608.0, 1065353216.0, ALU.mult, ALU.add).then_inc(VE.sem, 1)
        k_ += 1
        e.wait_ge(VE.sem, k_)
        e.tensor_copy(NI, NF).then_inc(VE.sem, 1)
        k_ += 1
        e.wait_ge(VE.sem, k_)
        e.tensor_tensor(B["rope_f"][0:1, 0:128], G_, NI.bitcast(F32),
                        ALU.mult).then_inc(VE.sem, 1)
    P.em("vector", vif)
    VE.inc(8 + 20 + 3)
    vif_m = VE.n

    vb2 = VE.n
    def val(e, vb2=vb2):
        AR = B["arows"]
        e.tensor_scalar(AR[0:1, 256:512], AR[0:1, 256:512], 1.0 / 255.0, None,
                        ALU.mult).then_inc(VE.sem, 1)
        e.wait_ge(VE.sem, vb2 + 1)
        e.tensor_scalar(AR[0:1, 0:256], AR[0:1, 256:512], -1.0, 1.0,
                        ALU.mult, ALU.add).then_inc(VE.sem, 1)
        e.wait_ge(VE.sem, vb2 + 2)
        e.tensor_scalar(AR[0:1, 512:768], AR[0:1, 0:256], B["flo"][0:1, 0:1], None,
                        ALU.mult).then_inc(VE.sem, 1)
        e.wait_ge(VE.sem, vb2 + 3)
        e.tensor_scalar(AR[0:1, 768:1024], AR[0:1, 512:768], -1.0, 1.0,
                        ALU.mult, ALU.add).then_inc(VE.sem, 1)
    P.em("vector", val)
    VE.inc(4)
    val_m = VE.n

    def tkm(e, vm=val_m, dy=dep_yev):
        e.wait_ge(VE.sem, max(vm, dy))
        e.transpose(B["p_u"][0][:, 0:20], B["kmask"][:], B["eye_f"][0:20, 0:20]).then_inc(PE.sem, 1)
    P.em("tensor", tkm)
    tk_m = PE.inc(1)

    def vkm(e, pm=tk_m):
        e.wait_ge(PE.sem, pm)
        e.tensor_copy(B["kmask_c"][:], B["p_u"][0][:, 0:20]).then_inc(VE.sem, 1)
    P.em("vector", vkm)
    vkm_m = VE.inc(1)

    def tab(e, vm=vkm_m, dy=dep_yev):
        e.wait_ge(VE.sem, max(vm, dy))
        e.matmul(B["p_bank"][2][:, 0:512], B["ones_f"][:], B["arows"][0:1, 0:512],
                 start=True, stop=True)
        e.matmul(B["p_bank"][3][:, 0:512], B["ones_f"][:], B["arows"][0:1, 512:1024],
                 start=True, stop=True).then_inc(PE.sem, 1)
    P.em("tensor", tab)
    ab_pe = PE.inc(1)

    def vab(e, pm=ab_pe):
        e.wait_ge(PE.sem, pm)
        e.tensor_copy(B["ab"][:, 0:512], B["p_bank"][2][:, 0:512]).then_inc(VE.sem, 1)
        e.tensor_copy(B["ab"][:, 512:1024], B["p_bank"][3][:, 0:512]).then_inc(VE.sem, 1)
    P.em("vector", vab)
    VE.inc(2)
    ab_ve = VE.n

    for ss in range(EE // T):
        def tmf(e, ss=ss, vm=vif_m, vk=vkm_m, dy=dep_yev):
            if ss == 0:
                e.wait_ge(VE.sem, max(vm, vk, dy))
            else:
                e.wait_ge(VE.sem, ROPE_VM[ss - 1])
            e.matmul(B["p_g"][0][:], B["rope_f"][0:1, 0:128],
                     B["tpos"][0:1, ss * T:(ss + 1) * T],
                     start=True, stop=True).then_inc(PE.sem, 1)
        P.em("tensor", tmf)
        pm = PE.inc(1)

        vb3 = VE.n
        def vrope(e, ss=ss, pm=pm, vb3=vb3):
            e.wait_ge(PE.sem, pm)
            FF, TT_, KI = B["rope_f"], B["rope_g"], B["rope_i"]
            e.tensor_copy(FF[:], B["p_g"][0][:]).then_inc(VE.sem, 1)
            e.wait_ge(VE.sem, vb3 + 1)
            e.tensor_scalar(TT_[:], FF[:], INV2PI, 12582912.0, ALU.mult,
                            ALU.add).then_inc(VE.sem, 1)
            e.wait_ge(VE.sem, vb3 + 2)
            e.tensor_scalar(TT_[:], TT_[:], 12582912.0, None,
                            ALU.subtract).then_inc(VE.sem, 1)
            e.wait_ge(VE.sem, vb3 + 3)
            e.nop().then_inc(VE.sem, 1)
            e.wait_ge(VE.sem, vb3 + 4)
            e.scalar_tensor_tensor(FF[:], TT_[:], -C1, FF[:], ALU.mult, ALU.add).then_inc(VE.sem, 1)
            e.wait_ge(VE.sem, vb3 + 5)
            e.scalar_tensor_tensor(FF[:], TT_[:], -C2, FF[:], ALU.mult, ALU.add).then_inc(VE.sem, 1)
            e.wait_ge(VE.sem, vb3 + 6)
            e.scalar_tensor_tensor(FF[:], TT_[:], -C3, FF[:], ALU.mult, ALU.add).then_inc(VE.sem, 1)
        P.em("vector", vrope)
        VE.inc(7)
        vr_m = VE.n

        def arope(e, ss=ss, vm=vr_m):
            e.wait_ge(VE.sem, vm)
            e.activation(B["sinr_e"][:, ss * T:(ss + 1) * T], B["rope_f"][:],
                         AF.Sin).then_inc(AC.sem, 1)
        P.em("scalar", arope)
        a1 = AC.inc(1)

        vb4 = VE.n
        def vwrap(e, ss=ss, am=a1, vb4=vb4):
            e.wait_ge(AC.sem, am)
            FF, TT_ = B["rope_f"], B["rope_g"]
            e.tensor_scalar(FF[:], FF[:], PI / 2, None, ALU.add).then_inc(VE.sem, 1)
            e.wait_ge(VE.sem, vb4 + 1)
            e.tensor_single_scalar(TT_[:], FF[:], PI, ALU.is_gt).then_inc(VE.sem, 1)
            e.wait_ge(VE.sem, vb4 + 2)
            e.scalar_tensor_tensor(FF[:], TT_[:], -2 * PI, FF[:], ALU.mult,
                                   ALU.add).then_inc(VE.sem, 1)
        P.em("vector", vwrap)
        VE.inc(3)
        vw_m = VE.n

        def arope2(e, ss=ss, vm=vw_m):
            e.wait_ge(VE.sem, vm)
            e.activation(B["cos_e"][:, ss * T:(ss + 1) * T], B["rope_f"][:],
                         AF.Sin).then_inc(AC.sem, 1)
        P.em("scalar", arope2)
        a2 = AC.inc(1)

        vb5 = VE.n
        def vsr(e, ss=ss, am=a2, vb5=vb5):
            e.wait_ge(AC.sem, am)
            sl_ = slice(ss * T, (ss + 1) * T)
            e.tensor_scalar(B["sinr_e"][0:64, sl_], B["sinr_e"][0:64, sl_],
                            -1.0, None, ALU.mult).then_inc(VE.sem, 1)
        P.em("vector", vsr)
        ROPE_VM.append(VE.inc(1))
    return {"rope_ve": max(ROPE_VM[-1], ab_ve), "alpha_ve": val_m}


def emit_attn(P, S, B, xfm, q_scr, wqT_d, woT_d, kg_d, vg_d, kc_d, vc_d,
              dep_marks, wsm, rm, ncores):
    PE, VE, AC, GP = S["PE"], S["VE"], S["AC"], S["GP"]
    lnx = S["lnx"]
    NTT = NT

    kv_ev = []
    q_w = []
    for t in range(NTT):
        parked, mm_m = [], []
        for dc in range(DC):
            i = t * DC + dc
            xm = lnx[i % 2].inc(16)
            def gld(e, t=t, dc=dc, i=i,
                    deps=(dep_marks if (t == 0 and dc == 0) else None),
                    pk=(parked[dc - 2] if dc >= 2 else None),
                    qpe=(QT_PE[t - 1] if (t >= 1 and dc == 0) else None)):
                if deps is not None:
                    for sm, vv_ in deps:
                        e.wait_ge(sm, vv_)
                if pk is not None:
                    e.wait_ge(VE.sem, pk)
                if qpe is not None:
                    e.wait_ge(PE.sem, qpe)
                e.dma_start(out=B["ln_x"][i % 2][:],
                            in_=xfm[dc, :, t * T:(t + 1) * T]).then_inc(lnx[i % 2].sem, 16)
            P.em("gpsimd", gld)

            def vpark(e, dc=dc, i=i, xm=xm):
                e.wait_ge(lnx[i % 2].sem, xm)
                e.tensor_copy(B["xn_t"][:, dc * T:(dc + 1) * T],
                              B["ln_x"][i % 2][:]).then_inc(VE.sem, 1)
            P.em("vector", vpark)
            parked.append(VE.inc(1))

            def apark(e, dc=dc, i=i, vm=VE.n,
                      pmm=(mm_m[dc - 2] if dc >= 2 else None)):
                e.wait_ge(VE.sem, vm)
                if pmm is not None:
                    e.wait_ge(PE.sem, pmm)
                e.activation(B["ln_sq"][i % 2][:], B["xn_t"][:, dc * T:(dc + 1) * T],
                             AF.Square).then_inc(AC.sem, 1)
            P.em("scalar", apark)
            AC.inc(1)

            def tmm(e, t=t, dc=dc, i=i, am=AC.n,
                    rope_dep=(rm["rope_ve"] if (t == 0 and dc == 0) else None),
                    qev=(QROPE_V[-1] if (t >= 1 and dc == 0 and QROPE_V) else None)):
                e.wait_ge(AC.sem, am)
                if rope_dep is not None:
                    e.wait_ge(VE.sem, rope_dep)
                if qev is not None:
                    e.wait_ge(VE.sem, qev)
                if dc == 0 and t >= 1:
                    e.wait_ge(VE.sem, AROW_V[t - 1])
                e.matmul(B["p_sx"], B["ones_b"][:], B["xn_t"][:, dc * T:(dc + 1) * T],
                         start=(dc == 0), stop=(dc == DC - 1))
                e.matmul(B["p_sq"], B["ones_b"][:], B["ln_sq"][i % 2][:],
                         start=(dc == 0), stop=(dc == DC - 1)).then_inc(PE.sem, 1)
            P.em("tensor", tmm)
            mm_m.append(PE.inc(1))

        vb = VE.n
        def vrow(e, pm=PE.n, vb=vb):
            e.wait_ge(PE.sem, pm)
            e.tensor_scalar(B["r_mu"], B["p_sx"], 1.0 / D, None,
                            ALU.mult).then_inc(VE.sem, 1)
            e.wait_ge(VE.sem, vb + 1)
            e.tensor_tensor(B["r_t1"], B["r_mu"], B["r_mu"], ALU.mult).then_inc(VE.sem, 1)
            e.wait_ge(VE.sem, vb + 2)
            e.tensor_scalar(B["r_t1"], B["r_t1"], EPS, None, ALU.subtract).then_inc(VE.sem, 1)
            e.wait_ge(VE.sem, vb + 3)
            e.scalar_tensor_tensor(B["r_var"], B["p_sq"], 1.0 / D, B["r_t1"],
                                   ALU.mult, ALU.subtract).then_inc(VE.sem, 1)
        P.em("vector", vrow)
        VE.inc(4)
        AROW_V.append(VE.n)
        vrow_m = VE.n

        ab = AC.n
        def arow(e, vm=vrow_m, ab=ab):
            e.wait_ge(VE.sem, vm)
            e.activation(B["r_t1"], B["r_var"], AF.Ln).then_inc(AC.sem, 1)
            e.wait_ge(AC.sem, ab + 1)
            e.activation(B["r_r"], B["r_t1"], AF.Exp, scale=-0.5).then_inc(AC.sem, 1)
        P.em("scalar", arow)
        AC.inc(2)

        def tbc(e, t=t, am=AC.n, nv=(NORM_V[t - 1] if t >= 1 else None)):
            e.wait_ge(AC.sem, am)
            if nv is not None:
                e.wait_ge(VE.sem, nv)
            e.matmul(B["p_mu"][:], B["ones_f"][:], B["r_mu"], start=True, stop=True)
            e.matmul(B["p_r"][:], B["ones_f"][:], B["r_r"],
                     start=True, stop=True).then_inc(PE.sem, 1)
        P.em("tensor", tbc)
        bc_pe = PE.inc(1)

        norm_m = []
        for dc in range(DC):
            vb2 = VE.n
            def vn(e, dc=dc, pm=bc_pe, vb2=vb2):
                if dc == 0:
                    e.wait_ge(PE.sem, pm)
                e.wait_ge(VE.sem, vb2)
                sl_ = slice(dc * T, (dc + 1) * T)
                e.tensor_tensor(B["ln_t"][:], B["xn_t"][:, sl_], B["p_mu"][:],
                                ALU.subtract).then_inc(VE.sem, 1)
                e.wait_ge(VE.sem, vb2 + 1)
                e.tensor_tensor(B["xn_t"][:, sl_], B["ln_t"][:], B["p_r"][:],
                                ALU.mult).then_inc(VE.sem, 1)
            P.em("vector", vn)
            VE.inc(2)
            norm_m.append(VE.n)
        NORM_V.append(VE.n)

        def tkv(e, t=t, nm=norm_m[-1], kvm=(wsm["kvT_done"])):
            e.wait_ge(VE.sem, nm)
            if t == 0:
                for sm, vv_ in kvm:
                    e.wait_ge(sm, vv_)
            for dc in range(DC):
                e.matmul(B["p_g"][0][:], B["wkT"][:, dc * 128:(dc + 1) * 128],
                         B["xn_t"][:, dc * T:(dc + 1) * T],
                         start=(dc == 0), stop=(dc == DC - 1))
            ins = None
            for dc in range(DC):
                ins = e.matmul(B["p_g"][1][:], B["wvT"][:, dc * 128:(dc + 1) * 128],
                               B["xn_t"][:, dc * T:(dc + 1) * T],
                               start=(dc == 0), stop=(dc == DC - 1))
            ins.then_inc(PE.sem, 1)
        P.em("tensor", tkv)
        kv_pe = PE.inc(1)

        ofs = 256 + t * T
        vb3 = VE.n
        def vkr(e, t=t, pm=kv_pe, ofs=ofs, vb3=vb3):
            e.wait_ge(PE.sem, pm)
            e.wait_ge(VE.sem, vb3)
            sl_ = slice(ofs, ofs + T)
            e.tensor_tensor(B["qc_t"][:], B["p_g"][0][:], B["cos_e"][:, sl_],
                            ALU.mult).then_inc(VE.sem, 1)
            e.tensor_tensor(B["qs_t"][0:64, :], B["p_g"][0][64:128, :],
                            B["sinr_e"][0:64, sl_], ALU.mult).then_inc(VE.sem, 1)
            e.tensor_tensor(B["qs_t"][64:128, :], B["p_g"][0][0:64, :],
                            B["sinr_e"][64:128, sl_], ALU.mult).then_inc(VE.sem, 1)
            e.wait_ge(VE.sem, vb3 + 3)
            e.tensor_tensor(B["k_ext"][:, sl_], B["qc_t"][:], B["qs_t"][:],
                            ALU.add).then_inc(VE.sem, 1)
            e.wait_ge(VE.sem, vb3 + 4)
            e.tensor_copy(B["w_sb"][0][:], B["p_g"][1][:]).then_inc(VE.sem, 1)
        P.em("vector", vkr)
        VE.inc(5)
        vkr_m = VE.n

        def tvt(e, t=t, vm=vkr_m, ev=wsm["eyeb_ve"]):
            e.wait_ge(VE.sem, vm)
            pview = B["p_bank"][2][:].bitcast(BF16)
            ins = None
            for k in range(4):
                ins = e.transpose(pview[:, k * 128:(k + 1) * 128],
                                  B["w_sb"][0][:, k * 128:(k + 1) * 128],
                                  B["eye_b"][:])
            ins.then_inc(PE.sem, 1)
        P.em("tensor", tvt)
        vt_pe = PE.inc(1)

        def vvt(e, t=t, pm=vt_pe, ofs=ofs):
            e.wait_ge(PE.sem, pm)
            e.tensor_copy(B["v_ext"][:, ofs:ofs + T],
                          B["p_bank"][2][:].bitcast(BF16)[:, 0:512]).then_inc(VE.sem, 1)
        P.em("vector", vvt)
        kv_ev.append(VE.inc(1))

        for og in range(4):
            gidx = t * 4 + og
            for dc in range(DC):
                sl2 = QPN[0] % 6
                m = S["wqp"][sl2].inc(16)
                def pld(e, dc=dc, sl2=sl2,
                        pv=(QPM.get(QPN[0] - 6)), wtd=wsm["wT_done"],
                        first=(QPN[0] < 6), wqT_d=wqT_d):
                    if first:
                        for sm, vv_ in wtd:
                            e.wait_ge(sm, vv_)
                    if pv is not None:
                        e.wait_ge(PE.sem, pv)
                    e.dma_start(out=B["wpan"][sl2][:],
                                in_=wqT_d[dc * 128:(dc + 1) * 128, :]).then_inc(S["wqp"][sl2].sem, 16)
                P.em("sync" if QPN[0] % 2 == 0 else "scalar", pld)

                def tq(e, t=t, og=og, dc=dc, sl2=sl2, m=m, gidx=gidx,
                       kv=kv_ev[-1]):
                    e.wait_ge(S["wqp"][sl2].sem, m)
                    if dc == 0:
                        e.wait_ge(VE.sem, kv)
                        if gidx >= 1:
                            e.wait_ge(VE.sem, QROPE_V[gidx - 1])
                    ins = None
                    for hh in range(8):
                        h = og * 8 + hh
                        ins = e.matmul(B["p_bank"][hh][:],
                                       B["wpan"][sl2][:, h * 128:(h + 1) * 128],
                                       B["xn_t"][:, dc * T:(dc + 1) * T],
                                       start=(dc == 0), stop=(dc == DC - 1))
                    ins.then_inc(PE.sem, 1)
                P.em("tensor", tq)
                QPM[QPN[0]] = PE.inc(1)
                QPN[0] += 1

            eofs = 256 + t * T
            vb4 = VE.n
            def vqr(e, t=t, og=og, pm=PE.n, eofs=eofs, vb4=vb4,
                    qwp=(QR_WR[t - 1] if (t >= 1 and og == 0) else None)):
                e.wait_ge(PE.sem, pm)
                if qwp is not None:
                    e.wait_ge(qwp[0], qwp[1])
                sl_ = slice(eofs, eofs + T)
                k_ = vb4
                for hh in range(8):
                    h = og * 8 + hh
                    Qp = B["p_bank"][hh][:]
                    e.wait_ge(VE.sem, k_)
                    e.tensor_tensor(B["qc_t"][:], Qp, B["cos_e"][:, sl_],
                                    ALU.mult).then_inc(VE.sem, 1)
                    e.tensor_tensor(B["qs_t"][0:64, :], Qp[64:128, :],
                                    B["sinr_e"][0:64, sl_], ALU.mult).then_inc(VE.sem, 1)
                    e.tensor_tensor(B["qs_t"][64:128, :], Qp[0:64, :],
                                    B["sinr_e"][64:128, sl_], ALU.mult).then_inc(VE.sem, 1)
                    k_ += 3
                    e.wait_ge(VE.sem, k_)
                    e.tensor_tensor(B["qr_t"][:, h * T:(h + 1) * T],
                                    B["qc_t"][:], B["qs_t"][:],
                                    ALU.add).then_inc(VE.sem, 1)
                    k_ += 1
            P.em("vector", vqr)
            VE.inc(32)
            QROPE_V.append(VE.n)

        def gqw(e, t=t, vm=QROPE_V[-1]):
            e.wait_ge(VE.sem, vm)
            dst = q_scr.rearrange("h p t -> p h t")[:, :, t * T:(t + 1) * T]
            src = B["qr_t"][:].rearrange("p (h t) -> p h t", h=H)
            e.dma_start(out=dst, in_=src).then_inc(S["qsw"][t % 2].sem, 16)
        P.em("gpsimd", gqw)
        q_w.append(S["qsw"][t % 2].inc(16))
        QT_PE.append(PE.n)
        QR_WR.append((S["qsw"][t % 2].sem, q_w[-1]))

    ex = S["ex"]
    m1 = ex.inc(16 * 6)
    def gcon(e, kv=kv_ev[-1]):
        e.wait_ge(VE.sem, kv)
        e.dma_start(out=kc_d[0:128, :], in_=B["k_ext"][:, 256:512]).then_inc(ex.sem, 16)
        e.dma_start(out=kc_d[128:256, :], in_=B["k_ext"][:, 2048:2304]).then_inc(ex.sem, 16)
        for si in range(2):
            e.dma_start(out=vc_d[si * 128:(si + 1) * 128, :],
                        in_=B["v_ext"][:, (2 + si) * 128:(3 + si) * 128]).then_inc(ex.sem, 16)
        for si in range(2):
            e.dma_start(out=vc_d[(2 + si) * 128:(3 + si) * 128, :],
                        in_=B["v_ext"][:, (16 + si) * 128:(17 + si) * 128]).then_inc(ex.sem, 16)
    P.em("gpsimd", gcon)

    cc = S["cc"]
    def gcc(e, m1=m1):
        e.wait_ge(ex.sem, m1)
        e.collective_compute("AllGather", ALU.bypass,
                             replica_groups=[list(range(ncores))],
                             ins=[kc_d.opt()], outs=[kg_d.opt()]).then_inc(cc.sem, 1)
        e.collective_compute("AllGather", ALU.bypass,
                             replica_groups=[list(range(ncores))],
                             ins=[vc_d.opt()], outs=[vg_d.opt()]).then_inc(cc.sem, 1)
    P.em("gpsimd", gcc)
    cc_m = cc.inc(2)

    m2 = ex.inc(16 * 6)
    def ghal(e, cc_m=cc_m):
        e.wait_ge(cc.sem, cc_m)
        pid = e.partition_id()
        lidx = (pid + ncores - 1) % ncores
        ridx = (pid + 1) % ncores
        e.dma_start(out=B["k_ext"][:, 0:256],
                    in_=kg_d[bass.ts(lidx * 2 + 1, 128)]).then_inc(ex.sem, 16)
        e.dma_start(out=B["k_ext"][:, 2304:2560],
                    in_=kg_d[bass.ts(ridx * 2, 128)]).then_inc(ex.sem, 16)
        for si in range(2):
            e.dma_start(out=B["v_ext"][:, si * 128:(si + 1) * 128],
                        in_=vg_d[bass.ts(lidx * 4 + 2 + si, 128)]).then_inc(ex.sem, 16)
        for si in range(2):
            e.dma_start(out=B["v_ext"][:, (18 + si) * 128:(19 + si) * 128],
                        in_=vg_d[bass.ts(ridx * 4 + si, 128)]).then_inc(ex.sem, 16)
    P.em("gpsimd", ghal)

    xrl = S["xrl"]
    arsw = S["arsw"]
    axr_m = {}
    res_m = {}
    och_pe = {}

    blend_cnt = [0]

    for t in range(NTT + 1):
        lo = max(512 * t, 256)
        hi = 512 * t + 768
        if t == NTT:
            lo, hi = 2048, 2304
            rofs = 512
            tt_ = 3
        else:
            rofs = lo - 512 * t
            tt_ = t
        m = S["qrl"][t % 2].inc(16)
        def gqr(e, t=t, lo=lo, hi=hi, rofs=rofs, tt_=tt_,
                qw=(QR_WR[min(tt_, NTT - 1)]),
                qwall=([(S["qsw"][k].sem, S["qsw"][k].n) for k in range(2)]
                       if t == 0 else None),
                exm=(ex.n if t == 0 else None),
                pvw=(WIN_PE[-1] if (t >= 2 and WIN_PE) else None)):
            e.wait_ge(qw[0], qw[1])
            if qwall is not None:
                for sm, vv_ in qwall:
                    e.wait_ge(sm, vv_)
            if exm is not None:
                e.wait_ge(ex.sem, exm)
            if pvw is not None:
                e.wait_ge(PE.sem, pvw)
            if t == NTT:
                dst = B["q_roll"][:].rearrange("p (h t) -> p h t", h=H)[:, :, 512:768]
            else:
                dst = B["q_roll"][:].rearrange("p (h t) -> p h t", h=H)[:, :, rofs:768]
            src = q_scr.rearrange("h p t -> p h t")[:, :, lo - 256:hi - 256]
            e.dma_start(out=dst, in_=src).then_inc(S["qrl"][t % 2].sem, 16)
        P.em("gpsimd", gqr)

        if t == 0:
            wins = [(0, 256, 512), (1, 256, 768)]
        elif t < NTT:
            wins = [(2 * t, 512 * t, 512 * t + 512), (2 * t + 1, 512 * t + 256, 512 * t + 768)]
        else:
            wins = [(8, 2048, 2304)]

        for h in range(H):
            prev_r = None
            for wi_, (j, qlo, qhi) in enumerate(wins):
                qn = qhi - qlo
                klo = 256 * j
                rbase = 512 * tt_ if t < NTT else 512 * 3
                qof = qlo - rbase if t < NTT else 512
                def tsc(e, t=t, h=h, j=j, qn=qn, klo=klo, qof=qof,
                        m=m, sgm=(SIG_V[-1] if SIG_V else None),
                        opm=(OP_EV[-1] if OP_EV else None)):
                    e.wait_ge(S["qrl"][t % 2].sem, m)
                    if sgm is not None:
                        e.wait_ge(AC.sem, sgm)
                    if opm is not None:
                        e.wait_ge(VE.sem, opm)
                    pb = B["p_bank"]
                    ins = None
                    for ks in range(4):
                        ins = e.matmul(pb[ks][:, 0:qn],
                                       B["k_ext"][:, klo + ks * 128: klo + (ks + 1) * 128],
                                       B["q_roll"][:, h * 768 + qof: h * 768 + qof + qn],
                                       start=True, stop=True)
                    ins.then_inc(PE.sem, 1)
                P.em("tensor", tsc)
                sc_pe = PE.inc(1)

                def asg(e, t=t, h=h, j=j, qn=qn, klo=klo, pm=sc_pe,
                        wv=(WOUT_V[-1] if WOUT_V else None)):
                    e.wait_ge(PE.sem, pm)
                    if wv is not None:
                        e.wait_ge(VE.sem, wv)
                    pb = B["p_bank"]
                    ins = None
                    for ks in range(4):
                        ins = e.activation(B["sig"][ks][:, 0:qn], pb[ks][:, 0:qn],
                                           AF.Sigmoid, scale=SCALE,
                                           bias=B["kmask_c"][:, klo // 128 + ks: klo // 128 + ks + 1])
                    ins.then_inc(AC.sem, 1)
                P.em("scalar", asg)
                SIG_V.append(AC.inc(1))

                def tsum(e, t=t, h=h, qn=qn, klo=klo, am=SIG_V[-1],
                         rv=(RRD_V[-1] if RRD_V else None)):
                    e.wait_ge(AC.sem, am)
                    if rv is not None:
                        e.wait_ge(VE.sem, rv)
                    ins = None
                    for ks in range(4):
                        ins = e.matmul(B["p_sx"][0:1, 0:qn], B["ones_b"][:],
                                       B["sig"][ks][:, 0:qn],
                                       start=(ks == 0), stop=(ks == 3))
                    for ks in range(4):
                        ins = e.matmul(B["p_mu"][:, 0:qn],
                                       B["v_ext"][:, (klo // 128 + ks) * 128:(klo // 128 + ks + 1) * 128],
                                       B["sig"][ks][:, 0:qn],
                                       start=(ks == 0), stop=(ks == 3))
                    ins.then_inc(PE.sem, 1)
                P.em("tensor", tsum)
                sum_pe = PE.inc(1)
                WIN_PE.append(sum_pe)

                rslot = wi_ % 2
                vb6 = VE.n
                def vwo(e, t=t, h=h, qn=qn, pm=sum_pe, rslot=rslot, vb6=vb6):
                    e.wait_ge(PE.sem, pm)
                    e.tensor_copy(B["w_sb"][rslot][:, 0:qn],
                                  B["p_mu"][:, 0:qn]).then_inc(VE.sem, 1)
                    e.tensor_scalar(B["rrow"][0:1, rslot * T: rslot * T + qn],
                                    B["p_sx"][0:1, 0:qn], 1.0, 1e-30,
                                    ALU.mult, ALU.add).then_inc(VE.sem, 1)
                    e.wait_ge(VE.sem, vb6 + 2)
                    e.reciprocal(B["rrow"][0:1, rslot * T: rslot * T + qn],
                                 B["rrow"][0:1, rslot * T: rslot * T + qn]).then_inc(VE.sem, 1)
                P.em("vector", vwo)
                VE.inc(3)
                WOUT_V.append(VE.n)
                RRD_V.append(VE.n)

                emit_blk = None
                if t == 0 and wi_ == 1:
                    emit_blk = ("b0",)
                elif 0 < t < NTT:
                    if wi_ == 0:
                        emit_blk = ("carry",)
                    else:
                        emit_blk = ("intra",)
                elif t == NTT:
                    emit_blk = ("carry",)

                if emit_blk is not None:
                    kind = emit_blk[0]
                    bslot = blend_cnt[0] % 2
                    blend_cnt[0] += 1
                    bank_a = 7 if bslot == 0 else 5
                    vb7 = VE.n
                    def vmr(e, kind=kind, rslot=rslot, vb7=vb7, wv=WOUT_V[-1]):
                        e.wait_ge(VE.sem, wv)
                        e.tensor_copy(B["mrow"][0:1, 0:256],
                                      B["rrow"][0:1, rslot * T: rslot * T + 256]).then_inc(VE.sem, 1)
                        if kind != "carry":
                            src_ = (B["rrow"][0:1, (1 - rslot) * T:(1 - rslot) * T + 256]
                                    if kind == "b0" else
                                    B["rrow"][0:1, (1 - rslot) * T + 256:(1 - rslot) * T + 512])
                            e.wait_ge(VE.sem, vb7 + 1)
                            e.tensor_copy(B["mrow"][0:1, 256:512], src_).then_inc(VE.sem, 1)
                    P.em("vector", vmr)
                    VE.inc(1 if kind == "carry" else 2)
                    mr_m = VE.n

                    def tbr(e, bank_a=bank_a, vm=mr_m,
                            bv=(BLD_V[-2] if len(BLD_V) >= 2 else None)):
                        e.wait_ge(VE.sem, vm)
                        if bv is not None:
                            e.wait_ge(VE.sem, bv)
                        e.matmul(B["p_bank"][bank_a][:, 0:512], B["ones_f"][:],
                                 B["mrow"][0:1, 0:512],
                                 start=True, stop=True).then_inc(PE.sem, 1)
                    P.em("tensor", tbr)
                    br_pe = PE.inc(1)

                    vb8 = VE.n
                    def vbl(e, t=t, h=h, kind=kind, rslot=rslot, pm=br_pe,
                            bank_a=bank_a, vb8=vb8):
                        e.wait_ge(PE.sem, pm)
                        e.wait_ge(VE.sem, vb8)
                        pbk = B["p_bank"][bank_a]
                        e.tensor_tensor(B["qc_t"][:, 0:256], B["w_sb"][rslot][:, 0:256],
                                        pbk[:, 0:256], ALU.mult).then_inc(VE.sem, 1)
                        if kind == "carry":
                            e.tensor_copy(B["qs_t"][:, 0:256],
                                          B["carry"][:, h * 256:(h + 1) * 256]).then_inc(VE.sem, 1)
                        else:
                            pw = B["w_sb"][1 - rslot][:, 0:256] if kind == "b0" \
                                else B["w_sb"][1 - rslot][:, 256:512]
                            e.tensor_tensor(B["qs_t"][:, 0:256], pw, pbk[:, 256:512],
                                            ALU.mult).then_inc(VE.sem, 1)
                        e.wait_ge(VE.sem, vb8 + 2)
                        aoff = (512, 768) if kind == "b0" else (0, 256)
                        e.tensor_tensor(B["qs_t"][:, 256:512], B["qs_t"][:, 0:256],
                                        B["ab"][:, aoff[0]:aoff[0] + 256],
                                        ALU.mult).then_inc(VE.sem, 1)
                        e.tensor_tensor(B["qc_t"][:, 256:512], B["qc_t"][:, 0:256],
                                        B["ab"][:, aoff[1]:aoff[1] + 256],
                                        ALU.mult).then_inc(VE.sem, 1)
                        e.wait_ge(VE.sem, vb8 + 4)
                        if kind == "b0" or kind == "carry":
                            cof = 0
                        else:
                            cof = 256
                        e.tensor_tensor(B["att_o"][:, h * T + cof: h * T + cof + 256],
                                        B["qs_t"][:, 256:512], B["qc_t"][:, 256:512],
                                        ALU.add).then_inc(VE.sem, 1)
                    P.em("vector", vbl)
                    VE.inc(5)
                    BLD_V.append(VE.n)

                if wi_ == len(wins) - 1 and t < NTT:
                    bank_c = 5 if (blend_cnt[0] % 2 == 0) else 7
                    vbc = VE.n
                    def vcr(e, rslot=rslot, vbc=vbc,
                            bv=(BLD_V[-1] if BLD_V else None)):
                        if bv is not None:
                            e.wait_ge(VE.sem, bv)
                        e.tensor_copy(B["mrow"][0:1, 0:256],
                                      B["rrow"][0:1, rslot * T + 256: rslot * T + 512]).then_inc(VE.sem, 1)
                    P.em("vector", vcr)
                    VE.inc(1)
                    vcr_m = VE.n

                    def tcr(e, bank_c=bank_c, vm=vcr_m,
                            bv=(BLD_V[-2] if len(BLD_V) >= 2 else None)):
                        e.wait_ge(VE.sem, vm)
                        if bv is not None:
                            e.wait_ge(VE.sem, bv)
                        e.matmul(B["p_bank"][bank_c][:, 0:256], B["ones_f"][:],
                                 B["mrow"][0:1, 0:256],
                                 start=True, stop=True).then_inc(PE.sem, 1)
                    P.em("tensor", tcr)
                    cr_pe = PE.inc(1)

                    def vcu(e, t=t, h=h, rslot=rslot, pm=cr_pe, bank_c=bank_c):
                        e.wait_ge(PE.sem, pm)
                        e.tensor_tensor(B["carry"][:, h * 256:(h + 1) * 256],
                                        B["w_sb"][rslot][:, 256:512],
                                        B["p_bank"][bank_c][:, 0:256],
                                        ALU.mult).then_inc(VE.sem, 1)
                    P.em("vector", vcu)
                    VE.inc(1)
                    BLD_V.append(VE.n)

        if t == 0:
            cn_, tok0 = 256, 0
        elif t < NTT:
            cn_, tok0 = 512, 512 * t - 256
        else:
            cn_, tok0 = 256, 1792
        for og in range(4):
            for hc in range(H):
                sl3 = QPN[0] % 6
                m3 = S["wqp"][sl3].inc(16)
                def pldo(e, hc=hc, sl3=sl3, pv=(QPM.get(QPN[0] - 6)), woT_d=woT_d):
                    if pv is not None:
                        e.wait_ge(PE.sem, pv)
                    e.dma_start(out=B["wpan"][sl3][:],
                                in_=woT_d[hc * 128:(hc + 1) * 128, :]).then_inc(S["wqp"][sl3].sem, 16)
                P.em("sync" if QPN[0] % 2 == 0 else "scalar", pldo)

                def to_(e, t=t, og=og, hc=hc, sl3=sl3, m3=m3, cn_=cn_,
                        bv=BLD_V[-1], wv=(OP_EV[-1] if OP_EV else None)):
                    e.wait_ge(S["wqp"][sl3].sem, m3)
                    if hc == 0:
                        e.wait_ge(VE.sem, bv)
                        if wv is not None:
                            e.wait_ge(VE.sem, wv)
                    ins = None
                    for oc in range(8):
                        o = og * 8 + oc
                        ins = e.matmul(B["p_bank"][oc][:, 0:cn_],
                                       B["wpan"][sl3][:, o * 128:(o + 1) * 128],
                                       B["att_o"][:, hc * T: hc * T + cn_],
                                       start=(hc == 0), stop=(hc == H - 1))
                    ins.then_inc(PE.sem, 1)
                P.em("tensor", to_)
                QPM[QPN[0]] = PE.inc(1)
                QPN[0] += 1
            op_pe = PE.n

            for oc in range(8):
                dc0 = og * 8 + oc
                axslot = AXN[0] % 2
                m4 = S["axl"][axslot].inc(16)
                def gxa(e, tok0=tok0, dc0=dc0, axslot=axslot, cn_=cn_,
                        pv=(AXM.get(AXN[0] - 2))):
                    if pv is not None:
                        e.wait_ge(arsw[axslot].sem, pv)
                    e.dma_start(out=B["axr"][axslot][:, 0:cn_],
                                in_=xfm[dc0, :, tok0:tok0 + cn_]).then_inc(S["axl"][axslot].sem, 16)
                P.em("gpsimd", gxa)

                def vo(e, oc=oc, axslot=axslot, m4=m4, pm=op_pe, cn_=cn_):
                    e.wait_ge(PE.sem, pm)
                    e.wait_ge(S["axl"][axslot].sem, m4)
                    e.tensor_tensor(B["axr"][axslot][:, 0:cn_],
                                    B["axr"][axslot][:, 0:cn_],
                                    B["p_bank"][oc][:, 0:cn_], ALU.add).then_inc(VE.sem, 1)
                P.em("vector", vo)
                OP_EV.append(VE.inc(1))

                def gwa(e, tok0=tok0, dc0=dc0, axslot=axslot, cn_=cn_,
                        vm=OP_EV[-1]):
                    e.wait_ge(VE.sem, vm)
                    e.dma_start(out=xfm[dc0, :, tok0:tok0 + cn_],
                                in_=B["axr"][axslot][:, 0:cn_]).then_inc(arsw[axslot].sem, 16)
                P.em("gpsimd", gwa)
                AXM[AXN[0]] = arsw[axslot].inc(16)
                AXN[0] += 1

    return {"res_done": [(arsw[k].sem, arsw[k].n) for k in range(2)]}


AC2M = {}
WS2M = {}
WS4M = {}
WSKV = []
ROPE_VM = []
AROW_V = []
NORM_V = []
QPM = {}
QPN = [0]
QT_PE = []
QROPE_V = []
QR_WR = []
WIN_PE = []
SIG_V = []
WOUT_V = []
RRD_V = []
BLD_V = []
OP_EV = []
AXM = {}
AXN = [0]


def emit_readback(P, S, B, xfm, out_ap, dep_marks):
    rbl, rbs = S["rbl"], S["rbs"]
    st_marks = {}
    for dc in range(DC):
        m = rbl[dc % 2].inc(16)
        def sld(e, dc=dc, dep_marks=dep_marks,
                pv=(st_marks.get(dc - 2))):
            if dc == 0:
                for sm, vv_ in dep_marks:
                    e.wait_ge(sm, vv_)
            if dc >= 2:
                e.wait_ge(rbs[dc % 2].sem, st_marks[dc - 2])
            e.dma_start(out=B["ig_x"][dc % 2][:, 0:2048],
                        in_=xfm[dc]).then_inc(rbl[dc % 2].sem, 16)
        P.em("sync", sld)

        def gst(e, dc=dc, m=m):
            e.wait_ge(rbl[dc % 2].sem, m)
            e.dma_start(out=out_ap[dc * 128:(dc + 1) * 128, :],
                        in_=B["ig_x"][dc % 2][:, 0:2048]).then_inc(rbs[dc % 2].sem, 16)
        P.em("gpsimd", gst)
        st_marks[dc] = rbs[dc % 2].inc(16)

    def fin(e):
        e.wait_ge(rbs[0].sem, rbs[0].n)
        e.wait_ge(rbs[1].sem, rbs[1].n)
    P.em("gpsimd", fin)


def emit_egress(P, S, B, xfm, out_ap, dep_marks, p0m):
    PE, VE = S["PE"], S["VE"]
    igx, igst = S["igx"], S["igst"]
    NSUB = N // 128
    ev_marks = []
    st_marks = []
    ld_marks = {}
    gi = 0
    for s in range(NSUB):
        slot = s % 2
        xm = igx[slot].inc(16)
        def gld(e, s=s, slot=slot, deps=(dep_marks if s == 0 else None),
                pv=(ld_marks.get(s - 2))):
            if deps is not None:
                for sm, vv_ in deps:
                    e.wait_ge(sm, vv_)
            if pv is not None:
                e.wait_ge(PE.sem, pv)
            src_ = xfm.rearrange("dc p t -> p dc t")[:, :, s * 128:(s + 1) * 128]
            e.dma_start(out=B["ig_x"][slot][:].rearrange("p (dc j) -> p dc j", dc=32),
                        in_=src_).then_inc(igx[slot].sem, 16)
        P.em("gpsimd", gld)

        for grp in range(8):
            def t(e, s=s, slot=slot, grp=grp, gi=gi, xm=xm):
                if grp == 0:
                    e.wait_ge(igx[slot].sem, xm)
                if gi >= 2:
                    e.wait_ge(VE.sem, ev_marks[gi - 2])
                pg = B["p_ing"][gi % 2]
                ins = None
                for k in range(4):
                    dc = grp * 4 + k
                    ins = e.transpose(pg[:, k * 128:(k + 1) * 128],
                                      B["ig_x"][slot][:, dc * 128:(dc + 1) * 128],
                                      B["eye_f"][:])
                ins.then_inc(PE.sem, 1)
            P.em("tensor", t)
            PE.inc(1)
            if grp == 7:
                ld_marks[s] = PE.n

            def v(e, gi=gi, pm=PE.n,
                  stm=(st_marks[gi - 2] if gi >= 2 else None)):
                e.wait_ge(PE.sem, pm)
                if stm is not None:
                    e.wait_ge(igst[gi % 2].sem, stm)
                e.tensor_copy(B["ig_st"][gi % 2][:],
                              B["p_ing"][gi % 2][:, 0:512]).then_inc(VE.sem, 1)
            P.em("vector", v)
            ev_marks.append(VE.inc(1))

            def g2(e, s=s, grp=grp, gi=gi, vm=VE.n):
                e.wait_ge(VE.sem, vm)
                dst = out_ap[s * 128:(s + 1) * 128,
                             grp * 512:(grp + 1) * 512]
                e.dma_start(out=dst, in_=B["ig_st"][gi % 2][:]).then_inc(igst[gi % 2].sem, 16)
            P.em("gpsimd", g2)
            st_marks.append(igst[gi % 2].inc(16))
            gi += 1

    def fin(e):
        e.wait_ge(igst[0].sem, igst[0].n)
        e.wait_ge(igst[1].sem, igst[1].n)
    P.em("gpsimd", fin)


def _reset_emit_state():
    for d in (AC2M, WS2M, WS4M, QPM, AXM):
        d.clear()
    for l in (WSKV, ROPE_VM, AROW_V, NORM_V, QT_PE, QROPE_V, QR_WR, WIN_PE,
              SIG_V, WOUT_V, RRD_V, BLD_V, OP_EV):
        l.clear()
    QPN[0] = 0
    AXN[0] = 0


def build(mode="dequant", ncores=8):
    _reset_emit_state()
    nc = bass.Bass("TRN2", target_bir_lowering=False)
    x_ap = nc.dram_tensor("x", [N, D], F32, kind="ExternalInput").ap()
    wq_ap = nc.dram_tensor("wq", [D, D], F32, kind="ExternalInput").ap()
    wk_ap = nc.dram_tensor("wk", [128, D], F32, kind="ExternalInput").ap()
    wv_ap = nc.dram_tensor("wv", [128, D], F32, kind="ExternalInput").ap()
    wo_ap = nc.dram_tensor("wo", [D, D], F32, kind="ExternalInput").ap()
    tpos_ap = nc.dram_tensor("tpos", [1, EE], F32, kind="ExternalInput").ap()
    kmask_ap = nc.dram_tensor("kmask", [20, 128], F32, kind="ExternalInput").ap()
    flo_ap = nc.dram_tensor("flo", [1, 1], F32, kind="ExternalInput").ap()
    gq_ap = nc.dram_tensor("gq", [D, I], I32, kind="ExternalInput").ap()
    gs_ap = nc.dram_tensor("gs", [DC, I], F32, kind="ExternalInput").ap()
    gz_ap = nc.dram_tensor("gz", [DC, I], F32, kind="ExternalInput").ap()
    uq_ap = nc.dram_tensor("uq", [D, I], I32, kind="ExternalInput").ap()
    us_ap = nc.dram_tensor("us", [DC, I], F32, kind="ExternalInput").ap()
    uz_ap = nc.dram_tensor("uz", [DC, I], F32, kind="ExternalInput").ap()
    dq_ap = nc.dram_tensor("dq", [I, D], I32, kind="ExternalInput").ap()
    ds_ap = nc.dram_tensor("ds", [ICN, D], F32, kind="ExternalInput").ap()
    dz_ap = nc.dram_tensor("dz", [ICN, D], F32, kind="ExternalInput").ap()
    out_ap = nc.dram_tensor("out", [N, D], F32, kind="ExternalOutput").ap()

    xfm = nc.dram_tensor("xfm", [DC, 128, N], F32, kind="Internal").ap()
    h_scr = nc.dram_tensor("h_scr", [ICN, 128, N], BF16, kind="Internal").ap()
    ds2 = nc.dram_tensor("ds2", [ICN, D], F32, kind="Internal").ap()
    dz2 = nc.dram_tensor("dz2", [ICN, D], F32, kind="Internal").ap()
    d_eyef = nc.dram_tensor("d_eyef", [128, 128], F32, kind="Internal").ap()
    cache_g = nc.dram_tensor("cache_g", [ICN, 128, D], BF16, kind="Internal").ap()
    cache_u = nc.dram_tensor("cache_u", [ICN, 128, D], BF16, kind="Internal").ap()
    cache_d = nc.dram_tensor("cache_d", [DC, 128, I], BF16, kind="Internal").ap()
    q_scr = nc.dram_tensor("q_scr", [H, 128, N], BF16, kind="Internal").ap()
    wq_bf = nc.dram_tensor("wq_bf", [D, D], BF16, kind="Internal").ap()
    wo_bf = nc.dram_tensor("wo_bf", [D, D], BF16, kind="Internal").ap()
    wqT_d = nc.dram_tensor("wqT_d", [D, D], BF16, kind="Internal").ap()
    woT_d = nc.dram_tensor("woT_d", [D, D], BF16, kind="Internal").ap()
    wk_bf = nc.dram_tensor("wk_bf", [128, D], BF16, kind="Internal").ap()
    wv_bf = nc.dram_tensor("wv_bf", [128, D], BF16, kind="Internal").ap()
    kc_d = nc.dram_tensor("kc_d", [256, 256], BF16, kind="Internal").ap()
    vc_d = nc.dram_tensor("vc_d", [512, 128], BF16, kind="Internal").ap()
    kg_d = nc.dram_tensor("kg_d", [256 * ncores, 256], BF16, kind="Internal").ap()
    vg_d = nc.dram_tensor("vg_d", [512 * ncores, 128], BF16, kind="Internal").ap()
    caches = {"g": cache_g, "u": cache_u, "d": cache_d}
    qsrc = {"gq": gq_ap, "gs": gs_ap, "gz": gz_ap,
            "uq": uq_ap, "us": us_ap, "uz": uz_ap,
            "dq": dq_ap, "ds2": ds2, "dz2": dz2}

    B = alloc_buffers(nc)
    S = {k: Ctr(nc, k) for k in ("PE", "VE", "AC")}

    S["p0a"] = DS(nc, "p0a"); S["p0b"] = DS(nc, "p0b"); S["p0c"] = DS(nc, "p0c")
    S["igx"] = [DS(nc, f"igx{i}") for i in range(2)]
    S["igst"] = [DS(nc, f"igst{i}") for i in range(2)]
    S["lnx"] = [DS(nc, f"lnx{i}") for i in range(2)]
    S["dqs"] = [DS(nc, f"dqs{i}") for i in range(2)]
    S["dqs2"] = [DS(nc, f"dqs2{i}") for i in range(2)]
    S["cwg"] = [DS(nc, f"cwg{i}") for i in range(2)]
    S["cwu"] = [DS(nc, f"cwu{i}") for i in range(2)]
    S["cwd"] = [DS(nc, f"cwd{i}") for i in range(2)]
    S["pang"] = [DS(nc, f"pang{i}") for i in range(2)]
    S["panu"] = [DS(nc, f"panu{i}") for i in range(2)]
    S["pand"] = [DS(nc, f"pand{i}") for i in range(2)]
    S["hstm"] = [DS(nc, f"hstm{i}") for i in range(4)]
    S["hld"] = [DS(nc, f"hld{i}") for i in range(2)]
    S["xrl"] = [DS(nc, f"xrl{i}") for i in range(4)]
    S["rsw"] = [DS(nc, f"rsw{i}") for i in range(4)]
    S["rbl"] = [DS(nc, f"rbl{i}") for i in range(2)]
    S["rbs"] = [DS(nc, f"rbs{i}") for i in range(2)]
    S["GP"] = Ctr(nc, "GP")
    S["ws1"] = [DS(nc, f"ws1{i}") for i in range(2)]
    S["ws2"] = [DS(nc, f"ws2{i}") for i in range(2)]
    S["ws3"] = [DS(nc, f"ws3{i}") for i in range(2)]
    S["ws4"] = [DS(nc, f"ws4{i}") for i in range(2)]
    S["ws5"] = DS(nc, "ws5")
    S["ac2"] = [Ctr(nc, f"ac2{i}") for i in range(2)]
    S["rp"] = DS(nc, "rp")
    S["qsw"] = [DS(nc, f"qsw{i}") for i in range(2)]
    S["qrl"] = [DS(nc, f"qrl{i}") for i in range(2)]
    S["wqp"] = [DS(nc, f"wqp{i}") for i in range(6)]
    S["ex"] = DS(nc, "ex")
    S["cc"] = Ctr(nc, "cc")
    S["axl"] = [DS(nc, f"axl{i}") for i in range(2)]
    S["arsw"] = [DS(nc, f"arsw{i}") for i in range(2)]
    P = Prog()

    p0m = emit_p0(P, S, B, ds_ap, dz_ap, ds2, dz2, d_eyef)
    global p0m_sz
    p0m_sz = (S["p0b"].sem, p0m["sz_p0b"])
    igm = emit_ingress(P, S, B, x_ap, xfm, p0m)
    S["wbfd"] = [wq_bf, wo_bf]
    stage = "kern"
    if stage == "ing":
        emit_readback(P, S, B, xfm, out_ap, igm["wr_done"])
    elif stage in ("attn", "mod", "kern"):
        ffm = emit_ffn(P, S, B, xfm, h_scr, caches, qsrc, mode,
                       dep_marks=igm["wr_done"], psum_free_ve=igm["last_ev"],
                       stage="full", dump_ap=None)
        wsm = emit_wstage(P, S, B, wq_ap, wk_ap, wv_ap, wo_ap, wqT_d, woT_d,
                          wk_bf, wv_bf, dep_marks=ffm["res_done"])
        rm = emit_rope_alpha(P, S, B, tpos_ap, kmask_ap, flo_ap, wsm["eyeb_ve"],
                             dep_yev=ffm["last_yev"])
        adeps = (ffm["res_done"] +
                 [(S["ws1"][k].sem, S["ws1"][k].n) for k in range(2)] +
                 [(S["ws2"][k].sem, S["ws2"][k].n) for k in range(2)])
        am_ = emit_attn(P, S, B, xfm, q_scr, wqT_d, woT_d, kg_d, vg_d, kc_d, vc_d,
                        dep_marks=adeps, wsm=wsm, rm=rm, ncores=ncores)
        if stage == "mod":
            ffm2 = emit_ffn(P, S, B, xfm, h_scr, caches, qsrc, "cached",
                            dep_marks=am_["res_done"], psum_free_ve=OP_EV[-1],
                            stage="full", dump_ap=None, pan_gate_pe=QPM[QPN[0] - 1])
            emit_readback(P, S, B, xfm, out_ap, ffm2["res_done"])
        elif stage == "kern":
            ffm2 = emit_ffn(P, S, B, xfm, h_scr, caches, qsrc, "cached",
                            dep_marks=am_["res_done"], psum_free_ve=OP_EV[-1],
                            stage="full", dump_ap=None, pan_gate_pe=QPM[QPN[0] - 1])
            emit_egress(P, S, B, xfm, out_ap, ffm2["res_done"], p0m)
        else:
            emit_readback(P, S, B, xfm, out_ap, am_["res_done"])
    else:
        ffm = emit_ffn(P, S, B, xfm, h_scr, caches, qsrc, mode,
                       dep_marks=igm["wr_done"], psum_free_ve=igm["last_ev"],
                       stage=stage, dump_ap=out_ap)
        if stage == "full":
            emit_readback(P, S, B, xfm, out_ap, ffm["res_done"])
        else:
            def fin2(e, dm=ffm["res_done"]):
                for sm, vv_ in dm:
                    e.wait_ge(sm, vv_)
            P.em("gpsimd", fin2)

    P.install(nc)
    if os.environ.get("MKAUDIT"):
        import collections
        tally = collections.Counter()
        for ins in nc.all_instructions():
            si = ins.sync_info
            if si is None:
                continue
            for su in si.on_update:
                if su.update_mode in ("sem-inc", "sem-add-imm"):
                    tally[su.id] += su.update_value
        def flat(d):
            for k, v in d.items():
                if isinstance(v, list):
                    for i, x in enumerate(v):
                        yield f"{k}{i}", x
                else:
                    yield k, v
        for k, c in flat(S):
            if tally.get(c.sem.num, 0) != c.n:
                print(f"AUDIT MISMATCH {k}: ledger={c.n} emitted={tally.get(c.sem.num)}")
        print("AUDIT done")
    return nc


def emulate(x, gq, gs, gz, uq, us, uz, dq, ds, dz):
    f32 = np.float32
    xbf = x.astype(bfd).astype(f32)
    mu = xbf.mean(axis=1, keepdims=True)
    sq = (xbf ** 2).astype(bfd).astype(f32)
    var = sq.mean(axis=1, keepdims=True) - mu ** 2
    r = np.exp(-0.5 * np.log(var + EPS))
    xn = ((xbf - mu) * r).astype(bfd).astype(f32)

    def deq(q, s, z, half=False):
        G = q.shape[0] // s.shape[0]
        sf = np.repeat(s, G, axis=0).astype(bfd).astype(f32)
        zf = np.repeat(z, G, axis=0).astype(bfd).astype(f32)
        if half:
            sf = np.repeat(0.5 * s, G, axis=0).astype(bfd).astype(f32)
            zf = np.repeat(0.5 * z, G, axis=0).astype(bfd).astype(f32)
        return ((q.astype(f32) - 8.0) * sf + zf).astype(bfd).astype(f32)

    Wg = deq(gq, gs, gz)
    Wu = deq(uq, us, uz)
    Wd = deq(dq, ds, dz, half=True)
    gate = xn @ Wg
    up = xn @ Wu
    sil = gate / (1.0 + np.exp(-gate))
    h = (sil * up).astype(bfd).astype(f32)
    y = h @ Wd
    return x + y


_NC_CACHE = {}


def kernel(L0_attn_x, L0_attn_Wq, L0_attn_Wk, L0_attn_Wv, L0_attn_Wo,
           L0_ffn_gate_q, L0_ffn_gate_s, L0_ffn_gate_z,
           L0_ffn_up_q, L0_ffn_up_s, L0_ffn_up_z,
           L0_ffn_down_q, L0_ffn_down_s, L0_ffn_down_z):
    x = np.ascontiguousarray(np.asarray(L0_attn_x, dtype=np.float32))
    BB, SS, _ = x.shape
    NCORES = 8
    CHUNKS = 4

    if "nc" not in _NC_CACHE:
        _NC_CACHE["nc"] = build("dequant", ncores=NCORES)
    nc = _NC_CACHE["nc"]

    wq = np.ascontiguousarray(np.asarray(L0_attn_Wq, dtype=np.float32))
    wk = np.ascontiguousarray(np.asarray(L0_attn_Wk, dtype=np.float32))
    wv = np.ascontiguousarray(np.asarray(L0_attn_Wv, dtype=np.float32))
    wo = np.ascontiguousarray(np.asarray(L0_attn_Wo, dtype=np.float32))
    gq = np.ascontiguousarray(np.asarray(L0_ffn_gate_q, dtype=np.int32))
    gs = np.ascontiguousarray(np.asarray(L0_ffn_gate_s, dtype=np.float32))
    gz = np.ascontiguousarray(np.asarray(L0_ffn_gate_z, dtype=np.float32))
    uq = np.ascontiguousarray(np.asarray(L0_ffn_up_q, dtype=np.int32))
    us = np.ascontiguousarray(np.asarray(L0_ffn_up_s, dtype=np.float32))
    uz = np.ascontiguousarray(np.asarray(L0_ffn_up_z, dtype=np.float32))
    dq_ = np.ascontiguousarray(np.asarray(L0_ffn_down_q, dtype=np.int32))
    dsx = np.ascontiguousarray(np.asarray(L0_ffn_down_s, dtype=np.float32))
    dzx = np.ascontiguousarray(np.asarray(L0_ffn_down_z, dtype=np.float32))

    in_maps = []
    for c in range(NCORES):
        b, k = c // CHUNKS, c % CHUNKS
        t0 = k * N
        tpos = np.arange(t0 - 256, t0 + N + 256, dtype=np.float32).reshape(1, EE)
        km = np.zeros(EE, np.float32)
        if k == 0:
            km[0:256] = -1e30
        if k == CHUNKS - 1:
            km[EE - 256:EE] = -1e30
        km = km.reshape(20, 128)
        flo = np.array([[0.0 if k == 0 else 1.0]], dtype=np.float32)
        in_maps.append({
            "x": np.ascontiguousarray(x[b, t0:t0 + N, :]),
            "wq": wq, "wk": wk, "wv": wv, "wo": wo,
            "gq": gq, "gs": gs, "gz": gz,
            "uq": uq, "us": us, "uz": uz,
            "dq": dq_, "ds": dsx, "dz": dzx,
            "tpos": tpos, "kmask": km, "flo": flo,
        })

    res = run_bass_kernel_spmd(nc, in_maps, core_ids=list(range(NCORES)))
    out = np.empty((BB, SS, D), dtype=np.float32)
    for c in range(NCORES):
        b, k = c // CHUNKS, c % CHUNKS
        out[b, k * N:(k + 1) * N, :] = res.results[c]["out"]
    return out


# version 05 (final)
# speedup vs baseline: 1.2002x; 1.0815x over previous
import sys, os, time
sys.path.insert(0, "/opt/trn_rl_repo")
import numpy as np
import ml_dtypes
import concourse.bass as bass
import concourse.mybir as mybir
from concourse.bass_utils import run_bass_kernel_spmd

F32 = mybir.dt.float32
BF16 = mybir.dt.bfloat16
I32 = mybir.dt.int32
AF = mybir.ActivationFunctionType
ALU = mybir.AluOpType
bfd = ml_dtypes.bfloat16

D, I = 4096, 11008
DC, ICN = 32, 86
T, NT = 512, 4
N = T * NT
EPS = 1e-5
KB = 1024

DWN_SLICES = [(k * 1024, 1024) for k in range(10)] + [(10240, 768)]
GUP_SLICES = [(k * 1024, 1024) for k in range(4)]


class Ctr:
    def __init__(self, nc, name):
        self.sem = nc.alloc_semaphore(name)
        self.n = 0

    def inc(self, k=1):
        self.n += k
        return self.n


class DS(Ctr):
    pass


class Prog:
    ENGS = ("sync", "gpsimd", "vector", "scalar", "tensor")

    def __init__(self):
        self.streams = {e: [] for e in self.ENGS}

    def em(self, eng, fn):
        self.streams[eng].append(fn)

    def install(self, nc):
        with nc.Block() as block:
            @block.sync
            def _(eng):
                for fn in self.streams["sync"]:
                    fn(eng)

            @block.gpsimd
            def _(eng):
                for fn in self.streams["gpsimd"]:
                    fn(eng)

            @block.vector
            def _(eng):
                for fn in self.streams["vector"]:
                    fn(eng)

            @block.scalar
            def _(eng):
                for fn in self.streams["scalar"]:
                    fn(eng)

            @block.tensor
            def _(eng):
                for fn in self.streams["tensor"]:
                    fn(eng)


def alloc_buffers(nc):
    B = {}
    BASE = 16512

    def A(name, shape, dtype, offset):
        return nc.alloc_sbuf_tensor_at(name, shape, dtype, offset=BASE + offset)

    B["xn"] = A("xn", [128, DC * N], BF16, offset=0)
    B["h_half"] = [A(f"hh{i}", [128, 43 * T], BF16, offset=i * 43 * KB) for i in range(2)]
    B["xr"] = [A(f"xr{i}", [128, T], F32, offset=86 * KB + i * 2 * KB) for i in range(4)]
    B["dnq"] = [A(f"dnq{i}", [128, 1024], I32, offset=94 * KB + i * 4 * KB)
                for i in range(2)]
    B["dns"] = [A(f"dns{i}", [128, 1024], BF16, offset=102 * KB + i * 2 * KB)
                for i in range(2)]
    B["dnz"] = [A(f"dnz{i}", [128, 1024], BF16, offset=106 * KB + i * 2 * KB)
                for i in range(2)]
    B["dnt"] = A("dnt", [128, 1024], F32, offset=110 * KB)
    B["ig_x"] = [A(f"igx{i}", [128, D], F32, offset=i * 16 * KB) for i in range(2)]
    B["ig_st"] = [A(f"igst{i}", [128, 512], F32, offset=32 * KB + i * 4 * KB)
                  for i in range(2)]
    B["szrow"] = A("szrow", [86, D], F32, offset=64 * KB)
    B["szrow2"] = A("szrow2", [86, D], F32, offset=80 * KB)

    B["pan_g"] = [A(f"pang{i}", [128, D], BF16, offset=128 * KB + i * 8 * KB)
                  for i in range(2)]
    B["pan_u"] = [A(f"panu{i}", [128, D], BF16, offset=144 * KB + i * 8 * KB)
                  for i in range(2)]
    B["pan_d"] = [A(f"pand{i}", [128, I], BF16, offset=128 * KB + i * 22 * KB)
                  for i in range(2)]
    B["guq"] = [A(f"guq{i}", [128, 1024], I32, offset=160 * KB + i * 4 * KB)
                for i in range(2)]
    B["gus"] = [A(f"gus{i}", [128, 1024], BF16, offset=168 * KB + i * 2 * KB)
                for i in range(2)]
    B["guz"] = [A(f"guz{i}", [128, 1024], BF16, offset=172 * KB + i * 2 * KB)
                for i in range(2)]
    B["gut"] = A("gut", [128, 1024], F32, offset=176 * KB)

    B["ln_x"] = [A(f"lnx{i}", [128, T], F32, offset=180 * KB + i * 2 * KB)
                 for i in range(2)]
    B["ln_sq"] = [A(f"lnsq{i}", [128, T], BF16, offset=184 * KB + i * KB)
                  for i in range(2)]
    B["ln_t"] = A("ln_t", [128, T], F32, offset=186 * KB)
    B["silu"] = [A(f"silu{i}", [128, T], F32, offset=188 * KB + i * 2 * KB)
                 for i in range(2)]
    B["h_out"] = [A(f"hout{i}", [128, T], BF16, offset=192 * KB + i * KB)
                  for i in range(4)]
    B["rows"] = A("rows", [1, 4 * T], F32, offset=196 * KB)
    B["ones_f"] = A("ones_f", [1, 128], F32, offset=204 * KB)
    B["ones_b"] = A("ones_b", [128, 1], BF16, offset=204 * KB + 512)
    B["eye_f"] = A("eye_f", [128, 128], F32, offset=204 * KB + 544)
    B["zeros128"] = A("zeros128", [128, 128], F32, offset=205 * KB + 64)

    B["xn_t"] = A("xn_t", [128, DC * T], BF16, offset=0)
    B["qr_t"] = A("qr_t", [128, 32 * T], BF16, offset=32 * KB)
    B["q_roll"] = A("q_roll", [128, 32 * 768], BF16, offset=0)
    B["att_o"] = A("att_o", [128, 32 * T], BF16, offset=48 * KB)
    B["carry"] = A("carry", [128, 32 * 256], BF16, offset=80 * KB)
    B["ab"] = A("ab", [128, 4 * 256], BF16, offset=96 * KB)
    B["mrow"] = A("mrow", [1, 512], F32, offset=98 * KB)
    B["rrow"] = A("rrow", [1, 1024], F32, offset=100 * KB)
    B["jrow"] = A("jrow", [1, 128], F32, offset=104 * KB)
    B["flo"] = A("flo", [1, 8], F32, offset=104 * KB + 512)
    B["tpos"] = A("tpos", [1, 2560], F32, offset=111 * KB)
    B["arows"] = A("arows", [1, 1024], F32, offset=123 * KB)
    B["wpan"] = [A(f"wpan{i}", [128, 1024], BF16, offset=128 * KB + i * 2 * KB)
                 for i in range(12)]
    B["wst"] = [A(f"wst{i}", [128, D], BF16, offset=32 * KB + i * 8 * KB)
                for i in range(2)]
    B["k_ext"] = A("k_ext", [128, 2560], BF16, offset=115 * KB)
    B["v_ext"] = A("v_ext", [128, 2560], BF16, offset=120 * KB)
    B["cos_e"] = A("cos_e", [128, 2560], BF16, offset=64 * KB)
    B["sinr_e"] = A("sinr_e", [128, 2560], BF16, offset=69 * KB)
    B["wkT"] = A("wkT", [128, D], BF16, offset=74 * KB)
    B["wvT"] = A("wvT", [128, D], BF16, offset=82 * KB)
    B["axr"] = [A(f"axr{i}", [128, T], F32, offset=111 * KB + i * 2 * KB)
                for i in range(2)]
    B["sig"] = [A(f"sig{i}", [128, T], BF16, offset=105 * KB + i * KB)
                for i in range(4)]
    B["w_sb"] = [A(f"wsb{i}", [128, T], BF16, offset=109 * KB + i * KB)
                 for i in range(2)]
    B["rope_f"] = A("rope_f", [128, T], F32, offset=188 * KB)
    B["rope_g"] = A("rope_g", [128, T], F32, offset=190 * KB)
    B["rope_i"] = A("rope_i", [128, T], I32, offset=192 * KB)
    B["qc_t"] = A("qc_t", [128, T], F32, offset=188 * KB)
    B["qs_t"] = A("qs_t", [128, T], F32, offset=190 * KB)
    B["kmask"] = A("kmask", [20, 128], F32, offset=205 * KB + 640)
    B["kmask_c"] = A("kmask_c", [128, 20], F32, offset=206 * KB + 128)
    B["eye_b"] = A("eye_b", [128, 128], BF16, offset=206 * KB + 256)

    def row(i):
        return B["rows"][0:1, i * T:(i + 1) * T]
    B["r_mu"], B["r_var"], B["r_t1"], B["r_r"] = row(0), row(1), row(2), row(3)

    B["p_bank"] = [nc.alloc_psum_tensor(f"pb{i}", [128, T], F32) for i in range(8)]
    B["p_g"] = B["p_bank"][0:2]
    B["p_u"] = B["p_bank"][2:4]
    B["p_sx"] = B["p_bank"][4][0:1, :]
    B["p_sq"] = B["p_bank"][5][0:1, :]
    B["p_mu"] = B["p_bank"][6]
    B["p_r"] = B["p_bank"][7]
    B["p_y"] = B["p_g"]
    B["p_ing"] = [B["p_g"][0], B["p_u"][0]]
    return B


def emit_p0(P, S, B, ds_ap, dz_ap, ds2, dz2, d_eyef, szsrc=None, szb_d=None):
    VE = S["VE"]
    p0a, p0b = S["p0a"], S["p0b"]
    a0 = p0a.inc(32)
    def sld(e):
        e.dma_start(out=B["szrow"][:], in_=ds_ap).then_inc(p0a.sem, 16)
        e.dma_start(out=B["szrow2"][:], in_=dz_ap).then_inc(p0a.sem, 16)
    P.em("sync", sld)

    def v(e, a0=a0):
        e.memset(B["ones_f"][:], 1.0)
        e.memset(B["ones_b"][:], 1.0)
        e.memset(B["zeros128"][:], 0.0)
        e.wait_ge(p0a.sem, a0)
        e.tensor_scalar(B["szrow"][:], B["szrow"][:], 0.5, None, ALU.mult)
        e.tensor_scalar(B["szrow2"][:], B["szrow2"][:], 0.5, None,
                        ALU.mult).then_inc(VE.sem, 1)
    P.em("vector", v)
    v0 = VE.inc(1)

    b0 = p0b.inc(48)
    def g(e, v0=v0, b0=b0):
        e.wait_ge(VE.sem, v0)
        e.dma_start(out=ds2, in_=B["szrow"][:]).then_inc(p0b.sem, 16)
        e.dma_start(out=dz2, in_=B["szrow2"][:]).then_inc(p0b.sem, 16)
        e.dma_start(out=d_eyef, in_=B["zeros128"][:]).then_inc(p0b.sem, 16)
        e.wait_ge(p0b.sem, b0)
        diag = bass.AP(d_eyef.tensor, 0, [[129, 128]])
        e.dma_start(out=diag, in_=B["ones_f"][:],
                    allow_slow_non_contiguous=True).then_inc(p0b.sem, 16)
    P.em("gpsimd", g)
    b1 = p0b.inc(16)

    p0c = S["p0c"]
    b2 = p0c.inc(16)
    def sld2(e, b1=b1):
        e.wait_ge(p0b.sem, b1)
        e.dma_start(out=B["eye_f"][:], in_=d_eyef).then_inc(p0c.sem, 16)
    P.em("sync", sld2)

    szb = S["szb"]
    szb.inc(16 * 6)
    def gcast(e, v0=v0):
        e.wait_ge(VE.sem, v0)
        e.dma_start(out=szb_d["dsb"], in_=B["szrow"][:]).then_inc(szb.sem, 16)
        e.dma_start(out=szb_d["dzb"], in_=B["szrow2"][:]).then_inc(szb.sem, 16)
        e.dma_start(out=szb_d["gs_b"], in_=szsrc["gs"]).then_inc(szb.sem, 16)
        e.dma_start(out=szb_d["gz_b"], in_=szsrc["gz"]).then_inc(szb.sem, 16)
        e.dma_start(out=szb_d["us_b"], in_=szsrc["us"]).then_inc(szb.sem, 16)
        e.dma_start(out=szb_d["uz_b"], in_=szsrc["uz"]).then_inc(szb.sem, 16)
    P.em("gpsimd", gcast)
    return {"sz_p0b": b1, "eye_p0b": b2, "szb": (szb.sem, szb.n)}


def emit_ingress(P, S, B, x_in, xfm, p0m):
    PE, VE = S["PE"], S["VE"]
    igx, igst = S["igx"], S["igst"]
    NSUB = N // 128
    pe_sub = {}
    ev_marks = []
    st_marks = []
    gi = 0
    for s in range(NSUB):
        slot = s % 2
        xm = igx[slot].inc(16)
        def gld(e, s=s, slot=slot):
            if s >= 2:
                e.wait_ge(PE.sem, pe_sub[s - 2])
            e.dma_start(out=B["ig_x"][slot][:],
                        in_=x_in[s * 128:(s + 1) * 128, :]).then_inc(igx[slot].sem, 16)
        P.em("gpsimd", gld)

        for grp in range(8):
            def t(e, s=s, slot=slot, grp=grp, gi=gi, xm=xm):
                if grp == 0:
                    e.wait_ge(igx[slot].sem, xm)
                    if s == 0:
                        e.wait_ge(S["p0c"].sem, p0m["eye_p0b"])
                if gi >= 2:
                    e.wait_ge(VE.sem, ev_marks[gi - 2])
                pg = B["p_ing"][gi % 2]
                ins = None
                for k in range(4):
                    dc = grp * 4 + k
                    ins = e.transpose(pg[:, k * 128:(k + 1) * 128],
                                      B["ig_x"][slot][:, dc * 128:(dc + 1) * 128],
                                      B["eye_f"][:])
                ins.then_inc(PE.sem, 1)
            P.em("tensor", t)
            PE.inc(1)
            if grp == 7:
                pe_sub[s] = PE.n

            def v(e, gi=gi, pm=PE.n,
                  stm=(st_marks[gi - 2] if gi >= 2 else None)):
                e.wait_ge(PE.sem, pm)
                if stm is not None:
                    e.wait_ge(igst[gi % 2].sem, stm)
                e.tensor_copy(B["ig_st"][gi % 2][:],
                              B["p_ing"][gi % 2][:, 0:512]).then_inc(VE.sem, 1)
            P.em("vector", v)
            ev_marks.append(VE.inc(1))

            def g2(e, s=s, grp=grp, gi=gi, vm=VE.n):
                e.wait_ge(VE.sem, vm)
                dst = xfm.rearrange("dc p t -> p dc t")[:, grp * 4:(grp + 1) * 4,
                                                        s * 128:(s + 1) * 128]
                src = B["ig_st"][gi % 2][:].rearrange("p (dc j) -> p dc j", dc=4)
                e.dma_start(out=dst, in_=src).then_inc(igst[gi % 2].sem, 16)
            P.em("gpsimd", g2)
            st_marks.append(igst[gi % 2].inc(16))
            gi += 1
    return {"wr_done": [(igst[0].sem, igst[0].n), (igst[1].sem, igst[1].n)],
            "last_ev": ev_marks[-1]}


def emit_ffn(P, S, B, xfm, h_scr, caches, qsrc, mode, dep_marks, psum_free_ve,
             stage="full", dump_ap=None, pan_gate_pe=None):
    PE, VE, AC = S["PE"], S["VE"], S["AC"]
    lnx, dqs = S["lnx"], S["dqs"]

    ln_parked, ln_mm, ln_norm = [], [], []
    vrow_marks = []
    for t in range(NT):
        for dc in range(DC):
            i = t * DC + dc
            xm = lnx[i % 2].inc(16)
            def gld(e, t=t, dc=dc, i=i):
                if i == 0:
                    for sm, vv_ in dep_marks:
                        e.wait_ge(sm, vv_)
                if i >= 2:
                    e.wait_ge(VE.sem, ln_parked[i - 2])
                e.dma_start(out=B["ln_x"][i % 2][:],
                            in_=xfm[dc, :, t * T:(t + 1) * T]).then_inc(lnx[i % 2].sem, 16)
            P.em("gpsimd", gld)

            def vpark(e, t=t, dc=dc, i=i, xm=xm,
                      szw=(p0m_szb if i == 0 else None)):
                if szw is not None:
                    e.wait_ge(szw[0], szw[1])
                e.wait_ge(lnx[i % 2].sem, xm)
                e.tensor_copy(B["xn"][:, dc * N + t * T: dc * N + t * T + T],
                              B["ln_x"][i % 2][:]).then_inc(VE.sem, 1)
            P.em("vector", vpark)
            ln_parked.append(VE.inc(1))

            def apark(e, t=t, dc=dc, i=i, vm=VE.n):
                e.wait_ge(VE.sem, vm)
                if i >= 2:
                    e.wait_ge(PE.sem, ln_mm[i - 2])
                e.activation(B["ln_sq"][i % 2][:],
                             B["xn"][:, dc * N + t * T: dc * N + t * T + T],
                             AF.Square).then_inc(AC.sem, 1)
            P.em("scalar", apark)
            AC.inc(1)

            vrm = vrow_marks[t - 1] if (t >= 1 and dc == 0) else None

            def tmm(e, t=t, dc=dc, i=i, am=AC.n, vrm=vrm):
                e.wait_ge(AC.sem, am)
                if vrm is not None:
                    e.wait_ge(VE.sem, vrm)
                if i == 0:
                    e.wait_ge(VE.sem, psum_free_ve)
                e.matmul(B["p_sx"], B["ones_b"][:],
                         B["xn"][:, dc * N + t * T: dc * N + t * T + T],
                         start=(dc == 0), stop=(dc == DC - 1))
                e.matmul(B["p_sq"], B["ones_b"][:], B["ln_sq"][i % 2][:],
                         start=(dc == 0), stop=(dc == DC - 1)).then_inc(PE.sem, 1)
            P.em("tensor", tmm)
            ln_mm.append(PE.inc(1))

        v_base = VE.n
        arow_prev = AC.n if t >= 1 else None

        def vrow(e, t=t, pm=PE.n, v_base=v_base, arow_prev=arow_prev):
            e.wait_ge(PE.sem, pm)
            if arow_prev is not None:
                e.wait_ge(AC.sem, arow_prev)
            e.tensor_scalar(B["r_mu"], B["p_sx"], 1.0 / D, None,
                            ALU.mult).then_inc(VE.sem, 1)
            e.wait_ge(VE.sem, v_base + 1)
            e.tensor_tensor(B["r_t1"], B["r_mu"], B["r_mu"],
                            ALU.mult).then_inc(VE.sem, 1)
            e.wait_ge(VE.sem, v_base + 2)
            e.tensor_scalar(B["r_t1"], B["r_t1"], EPS, None,
                            ALU.subtract).then_inc(VE.sem, 1)
            e.wait_ge(VE.sem, v_base + 3)
            e.scalar_tensor_tensor(B["r_var"], B["p_sq"], 1.0 / D,
                                   B["r_t1"], ALU.mult,
                                   ALU.subtract).then_inc(VE.sem, 1)
        P.em("vector", vrow)
        VE.inc(4)
        vrow_marks.append(VE.n)

        a_base = AC.n

        def arow(e, t=t, vm=VE.n, a_base=a_base):
            e.wait_ge(VE.sem, vm)
            e.activation(B["r_t1"], B["r_var"], AF.Ln).then_inc(AC.sem, 1)
            e.wait_ge(AC.sem, a_base + 1)
            e.activation(B["r_r"], B["r_t1"], AF.Exp,
                         scale=-0.5).then_inc(AC.sem, 1)
        P.em("scalar", arow)
        AC.inc(2)

        def tbc(e, t=t, am=AC.n):
            e.wait_ge(AC.sem, am)
            if t >= 1:
                e.wait_ge(VE.sem, ln_norm[t * DC - 1])
            e.matmul(B["p_mu"][:], B["ones_f"][:], B["r_mu"], start=True, stop=True)
            e.matmul(B["p_r"][:], B["ones_f"][:], B["r_r"],
                     start=True, stop=True).then_inc(PE.sem, 1)
        P.em("tensor", tbc)
        bc_pe = PE.inc(1)

        for dc in range(DC):
            v_base2 = VE.n

            def vn(e, t=t, dc=dc, pm=bc_pe, v_base2=v_base2):
                if dc == 0:
                    e.wait_ge(PE.sem, pm)
                e.wait_ge(VE.sem, v_base2)
                sl = slice(dc * N + t * T, dc * N + t * T + T)
                e.tensor_tensor(B["ln_t"][:], B["xn"][:, sl], B["p_mu"][:],
                                ALU.subtract).then_inc(VE.sem, 1)
                e.wait_ge(VE.sem, v_base2 + 1)
                e.tensor_tensor(B["xn"][:, sl], B["ln_t"][:], B["p_r"][:],
                                ALU.mult).then_inc(VE.sem, 1)
            P.em("vector", vn)
            VE.inc(2)
            ln_norm.append(VE.n)

    if stage == "ln":
        marks = []
        rs = S["rsw"][0]
        for dc in range(DC):
            def gdump(e, dc=dc, vm=ln_norm[-1], base=rs.n):
                e.wait_ge(VE.sem, vm)
                e.wait_ge(rs.sem, base)
                e.dma_start(out=dump_ap[dc * 128:(dc + 1) * 128, :],
                            in_=B["xn"][:, dc * N:(dc + 1) * N]).then_inc(rs.sem, 16)
            P.em("gpsimd", gdump)
            marks.append(rs.inc(16))
        return {"res_done": [(rs.sem, rs.n)], "last_yev": 0}

    pan_ready = {}
    gu_done = []
    dq_hist = []
    gu_hist, dn_hist = [], []
    gu_cnt, dn_cnt = [0], [0]
    dqs2 = S["dqs2"]
    dqq, dqq2 = S["dqq"], S["dqq2"]
    cw_marks = {}

    def produce_gu(kind, idx):
        slot = idx % 2
        prev_pe = None
        if idx >= 2:
            prev_pe = gu_done[(idx - 2) * NT + NT - 1]
        psem = S["pang"][slot] if kind == "g" else S["panu"][slot]
        if mode == "cached":
            m = psem.inc(16)
            def sld(e, kind=kind, idx=idx, slot=slot, prev_pe=prev_pe, psem=psem,
                    pg=(pan_gate_pe if idx < 2 else None)):
                if pg is not None:
                    e.wait_ge(PE.sem, pg)
                if prev_pe is not None:
                    e.wait_ge(PE.sem, prev_pe)
                e.dma_start(out=B["pan_" + kind][slot][:],
                            in_=caches[kind][idx]).then_inc(psem.sem, 16)
            P.em("sync", sld)
            pan_ready[(kind, idx)] = (psem.sem, m)
            return

        qap = qsrc[kind + "q"]
        sap = qsrc[kind + "s_b"]
        zap = qsrc[kind + "z_b"]
        cwsem = (S["cwg"] if kind == "g" else S["cwu"])[slot]
        cwm_prev = cw_marks.get((kind, idx - 2))
        for sl, (c0, cn) in enumerate(GUP_SLICES):
            gl0, gn = c0 // 128, cn // 128
            q_in = qap.rearrange("(g p) (ic j) -> ic p g j", p=128, j=128)[idx, :, gl0:gl0 + gn, :]
            s_in = sap.rearrange("g (ic j) -> ic g j", j=128)[idx, gl0:gl0 + gn, :]\
                      .unsqueeze(0).partition_broadcast(128)
            z_in = zap.rearrange("g (ic j) -> ic g j", j=128)[idx, gl0:gl0 + gn, :]\
                      .unsqueeze(0).partition_broadcast(128)
            st = gu_cnt[0] % 2
            stg_free = gu_hist[-2] if len(gu_hist) >= 2 else None
            gu_cnt[0] += 1

            d0 = dqs[st].inc(32)
            q0 = dqq[st].inc(16)
            def gqq(e, q_in=q_in, stg_free=stg_free, st=st):
                if stg_free is not None:
                    e.wait_ge(VE.sem, stg_free)
                e.dma_start(out=B["guq"][st][:], in_=q_in).then_inc(dqq[st].sem, 16)
            P.em("sync" if st == 0 else "scalar", gqq)
            def gq(e, s_in=s_in, z_in=z_in, stg_free=stg_free, st=st,
                   szw=(p0m_szb if (idx < 2 and sl == 0) else None)):
                if szw is not None:
                    e.wait_ge(szw[0], szw[1])
                if stg_free is not None:
                    e.wait_ge(VE.sem, stg_free)
                e.dma_start(out=B["gus"][st][:], in_=s_in).then_inc(dqs[st].sem, 16)
                e.dma_start(out=B["guz"][st][:], in_=z_in).then_inc(dqs[st].sem, 16)
            P.em("scalar", gq)

            v_base3 = VE.n

            def vdq(e, kind=kind, slot=slot, c0=c0, d0=d0, q0=q0, prev_pe=prev_pe,
                    sl=sl, v_base3=v_base3, st=st):
                e.wait_ge(dqs[st].sem, d0)
                e.wait_ge(dqq[st].sem, q0)
                e.wait_ge(VE.sem, v_base3)
                if sl == 0 and prev_pe is not None:
                    e.wait_ge(PE.sem, prev_pe)
                if sl == 0 and cwm_prev is not None:
                    e.wait_ge(cwsem.sem, cwm_prev)
                e.scalar_tensor_tensor(B["gut"][:], B["guq"][st][:], -8.0,
                                       B["gus"][st][:], ALU.add,
                                       ALU.mult).then_inc(VE.sem, 1)
                e.wait_ge(VE.sem, v_base3 + 1)
                e.tensor_tensor(B["pan_" + kind][slot][:, c0:c0 + 1024],
                                B["gut"][:], B["guz"][st][:],
                                ALU.add).then_inc(VE.sem, 1)
            P.em("vector", vdq)
            VE.inc(2)
            gu_hist.append(VE.n)
            dq_hist.append(VE.n)
        pan_ready[(kind, idx)] = (VE.sem, VE.n)

        def gw(e, kind=kind, idx=idx, slot=slot, vm=VE.n, cwsem=cwsem):
            e.wait_ge(VE.sem, vm)
            e.dma_start(out=caches[kind][idx],
                        in_=B["pan_" + kind][slot][:]).then_inc(cwsem.sem, 16)
        P.em("sync" if slot == 0 else "scalar", gw)
        cw_marks[(kind, idx)] = cwsem.inc(16)

    down_mm = {}

    def produce_d(idx, pe_gate=None):
        slot = idx % 2
        prev_pe = down_mm.get(idx - 2)
        psem = S["pand"][slot]
        gu_cw_finals = ([(S["cwg"][k].sem, S["cwg"][k].n) for k in range(2)] +
                        [(S["cwu"][k].sem, S["cwu"][k].n) for k in range(2)]) \
            if (mode == "dequant" and idx < 2) else None
        if mode == "cached":
            m = psem.inc(16)
            def sld(e, idx=idx, slot=slot, prev_pe=prev_pe, pe_gate=pe_gate, psem=psem):
                if pe_gate is not None:
                    e.wait_ge(PE.sem, pe_gate)
                if prev_pe is not None:
                    e.wait_ge(PE.sem, prev_pe)
                e.dma_start(out=B["pan_d"][slot][:],
                            in_=caches["d"][idx]).then_inc(psem.sem, 16)
            P.em("sync", sld)
            pan_ready[("d", idx)] = (psem.sem, m)
            return
        qap, sap, zap = qsrc["dq"], qsrc["dsb"], qsrc["dzb"]
        cwsem = S["cwd"][slot]
        cwm_prev = cw_marks.get(("d", idx - 2))
        for sl, (c0, cn) in enumerate(DWN_SLICES):
            gl0, gn = c0 // 128, cn // 128
            q_in = qap.rearrange("(g p) (ic j) -> ic p g j", p=128, j=128)[idx, :, gl0:gl0 + gn, :]
            s_in = sap.rearrange("g (ic j) -> ic g j", j=128)[idx, gl0:gl0 + gn, :]\
                      .unsqueeze(0).partition_broadcast(128)
            z_in = zap.rearrange("g (ic j) -> ic g j", j=128)[idx, gl0:gl0 + gn, :]\
                      .unsqueeze(0).partition_broadcast(128)
            st = dn_cnt[0] % 2
            stg_free = dn_hist[-2] if len(dn_hist) >= 2 else None
            dn_cnt[0] += 1

            d0 = dqs2[st].inc(32)
            q0 = dqq2[st].inc(16)
            def gqq(e, q_in=q_in, stg_free=stg_free, cn=cn,
                    pe_gate=pe_gate, sz=p0m_sz, st=st):
                e.wait_ge(sz[0], sz[1])
                if pe_gate is not None:
                    e.wait_ge(PE.sem, pe_gate)
                if stg_free is not None:
                    e.wait_ge(VE.sem, stg_free)
                e.dma_start(out=B["dnq"][st][:, 0:cn], in_=q_in).then_inc(dqq2[st].sem, 16)
            P.em("sync" if st == 0 else "scalar", gqq)
            def gq(e, s_in=s_in, z_in=z_in, stg_free=stg_free, cn=cn,
                   pe_gate=pe_gate, sz=p0m_szb, st=st):
                e.wait_ge(sz[0], sz[1])
                if pe_gate is not None:
                    e.wait_ge(PE.sem, pe_gate)
                if stg_free is not None:
                    e.wait_ge(VE.sem, stg_free)
                e.dma_start(out=B["dns"][st][:, 0:cn], in_=s_in).then_inc(dqs2[st].sem, 16)
                e.dma_start(out=B["dnz"][st][:, 0:cn], in_=z_in).then_inc(dqs2[st].sem, 16)
            P.em("scalar", gq)

            v_base4 = VE.n

            def vdq(e, slot=slot, c0=c0, cn=cn, d0=d0, q0=q0, prev_pe=prev_pe, sl=sl,
                    v_base4=v_base4, gu_cw_finals=gu_cw_finals, st=st):
                e.wait_ge(dqs2[st].sem, d0)
                e.wait_ge(dqq2[st].sem, q0)
                e.wait_ge(VE.sem, v_base4)
                if sl == 0 and prev_pe is not None:
                    e.wait_ge(PE.sem, prev_pe)
                if sl == 0 and cwm_prev is not None:
                    e.wait_ge(cwsem.sem, cwm_prev)
                if sl == 0 and gu_cw_finals is not None:
                    for sm, vv_ in gu_cw_finals:
                        e.wait_ge(sm, vv_)
                e.scalar_tensor_tensor(B["dnt"][:, 0:cn], B["dnq"][st][:, 0:cn], -8.0,
                                       B["dns"][st][:, 0:cn], ALU.add,
                                       ALU.mult).then_inc(VE.sem, 1)
                e.wait_ge(VE.sem, v_base4 + 1)
                e.tensor_tensor(B["pan_d"][slot][:, c0:c0 + cn],
                                B["dnt"][:, 0:cn], B["dnz"][st][:, 0:cn],
                                ALU.add).then_inc(VE.sem, 1)
            P.em("vector", vdq)
            VE.inc(2)
            dn_hist.append(VE.n)
            dq_hist.append(VE.n)
        pan_ready[("d", idx)] = (VE.sem, VE.n)

        def gw(e, idx=idx, slot=slot, vm=VE.n, cwsem=cwsem):
            e.wait_ge(VE.sem, vm)
            e.dma_start(out=caches["d"][idx],
                        in_=B["pan_d"][slot][:]).then_inc(cwsem.sem, 16)
        P.em("sync" if slot == 0 else "scalar", gw)
        cw_marks[("d", idx)] = cwsem.inc(16)

    if stage == "deq":
        produce_gu("g", 0)
        produce_gu("u", 0)
        rs = S["rsw"][0]
        semg, vg = pan_ready[("g", 0)]
        semu, vu = pan_ready[("u", 0)]
        def gdumpg(e, vg=vg, semg=semg):
            e.wait_ge(semg, vg)
            e.dma_start(out=dump_ap[0:256, 0:2048].rearrange("(a p) c -> p a c", p=128),
                        in_=B["pan_g"][0][:].rearrange("p (a c) -> p a c", a=2)).then_inc(rs.sem, 16)
        P.em("gpsimd", gdumpg)
        rs.inc(16)
        def gdumpu(e, vu=vu, semu=semu):
            e.wait_ge(semu, vu)
            e.dma_start(out=dump_ap[256:512, 0:2048].rearrange("(a p) c -> p a c", p=128),
                        in_=B["pan_u"][0][:].rearrange("p (a c) -> p a c", a=2)).then_inc(rs.sem, 16)
        P.em("gpsimd", gdumpu)
        return {"res_done": [(rs.sem, rs.inc(16))], "last_yev": 0}

    hep, hst_marks = [], []
    hstm = S["hstm"]
    for ic in range(min(2, ICN)):
        produce_gu("g", ic)
        produce_gu("u", ic)

    for ic in range(ICN):
        semg, vg = pan_ready[("g", ic)]
        semu, vu = pan_ready[("u", ic)]
        for t in range(NT):
            gi = ic * NT + t
            def tmm(e, ic=ic, t=t, gi=gi, semg=semg, vg=vg, semu=semu, vu=vu):
                if t == 0:
                    e.wait_ge(semg, vg)
                    e.wait_ge(semu, vu)
                if ic == 0:
                    e.wait_ge(VE.sem, ln_norm[(t + 1) * DC - 1])
                if gi >= 2:
                    e.wait_ge(VE.sem, hep[gi - 2])
                pg, pu = B["p_g"][t % 2], B["p_u"][t % 2]
                for dc in range(DC):
                    rhs = B["xn"][:, dc * N + t * T: dc * N + t * T + T]
                    e.matmul(pg[:], B["pan_g"][ic % 2][:, dc * 128:(dc + 1) * 128], rhs,
                             start=(dc == 0), stop=(dc == DC - 1))
                ins = None
                for dc in range(DC):
                    rhs = B["xn"][:, dc * N + t * T: dc * N + t * T + T]
                    ins = e.matmul(pu[:], B["pan_u"][ic % 2][:, dc * 128:(dc + 1) * 128],
                                   rhs, start=(dc == 0), stop=(dc == DC - 1))
                ins.then_inc(PE.sem, 1)
            P.em("tensor", tmm)
            gu_done.append(PE.inc(1))

            def aep(e, t=t, gi=gi, pm=PE.n,
                    hv=(hep[gi - 2] if gi >= 2 else None)):
                e.wait_ge(PE.sem, pm)
                if hv is not None:
                    e.wait_ge(VE.sem, hv)
                e.activation(B["silu"][gi % 2][:], B["p_g"][t % 2][:],
                             AF.Silu).then_inc(AC.sem, 1)
            P.em("scalar", aep)
            AC.inc(1)

            def vep(e, t=t, gi=gi, am=AC.n,
                    hsm=(hst_marks[gi - 4] if gi >= 4 else None)):
                e.wait_ge(AC.sem, am)
                if hsm is not None:
                    e.wait_ge(hstm[gi % 4].sem, hsm)
                e.tensor_tensor(B["h_out"][gi % 4][:], B["silu"][gi % 2][:],
                                B["p_u"][t % 2][:], ALU.mult).then_inc(VE.sem, 1)
            P.em("vector", vep)
            hep.append(VE.inc(1))

            def ghw(e, ic=ic, t=t, gi=gi, vm=VE.n):
                e.wait_ge(VE.sem, vm)
                e.dma_start(out=h_scr[ic, :, t * T:(t + 1) * T],
                            in_=B["h_out"][gi % 4][:]).then_inc(hstm[gi % 4].sem, 16)
            P.em("scalar", ghw)
            hst_marks.append(hstm[gi % 4].inc(16))

        if ic + 2 < ICN:
            produce_gu("g", ic + 2)
            produce_gu("u", ic + 2)

    if stage == "gate":
        marks = []
        for ic in range(DC):
            rs = S["rbl"][ic % 2]
            def gdump(e, ic=ic, rs=rs,
                      hm=[(hstm[k].sem, hstm[k].n) for k in range(4)],
                      pv=(marks[ic - 2] if ic >= 2 else None), base=rs.n):
                for sm, vv_ in hm:
                    e.wait_ge(sm, vv_)
                if pv is not None:
                    e.wait_ge(rs.sem, pv)
                e.dma_start(out=B["ln_x"][ic % 2][:],
                            in_=h_scr[ic, :, 0:T]).then_inc(rs.sem, 16)
                e.wait_ge(rs.sem, base + 16)
                e.dma_start(out=dump_ap[ic * 128:(ic + 1) * 128, 0:T],
                            in_=B["ln_x"][ic % 2][:]).then_inc(rs.sem, 16)
            P.em("gpsimd", gdump)
            marks.append(rs.inc(32))
        return {"res_done": [(S["rbl"][0].sem, S["rbl"][0].n),
                             (S["rbl"][1].sem, S["rbl"][1].n)], "last_yev": 0}

    if mode == "dequant":
        produce_d(0, pe_gate=gu_done[-1])
        produce_d(1, pe_gate=gu_done[-1])
    yev = []
    hld = S["hld"]
    xrl = S["xrl"]
    rsw = S["rsw"]
    res_marks = []
    hload = {}
    down_tile_last = {}
    hst_all = [(hstm[k].sem, hstm[k].n) for k in range(4)]
    for t in range(NT):
        for half in range(2):
            lo = 0 if half == 0 else 43
            def ghl(e, t=t, half=half, lo=lo):
                if t >= 1:
                    e.wait_ge(PE.sem, down_tile_last[t - 1])
                if t == 0 and half == 0:
                    for sm, vv_ in hst_all:
                        e.wait_ge(sm, vv_)
                src = h_scr[lo:lo + 43, :, t * T:(t + 1) * T].rearrange("ic p t -> p ic t")
                e.dma_start(out=B["h_half"][half][:, 0:21 * T],
                            in_=src[:, 0:21, :]).then_inc(hld[half].sem, 16)
                e.dma_start(out=B["h_half"][half][:, 21 * T:43 * T],
                            in_=src[:, 21:43, :]).then_inc(hld[half].sem, 16)
            P.em("scalar", ghl)
            hload[(t, half)] = hld[half].inc(32)

        for dc in range(DC):
            di = t * DC + dc
            if mode == "dequant":
                if t == 0:
                    pass
                else:
                    prev_pe = down_mm.get(di - 2)
                    psem = S["pand"][dc % 2]
                    m = psem.inc(16)
                    def sld(e, dc=dc, prev_pe=prev_pe, psem=psem,
                            cwm=[(S["cwd"][k].sem, S["cwd"][k].n) for k in range(2)]):
                        if prev_pe is not None:
                            e.wait_ge(PE.sem, prev_pe)
                        for sm, vv_ in cwm:
                            e.wait_ge(sm, vv_)
                        e.dma_start(out=B["pan_d"][dc % 2][:],
                                    in_=caches["d"][dc]).then_inc(psem.sem, 16)
                    P.em("sync", sld)
                    pan_ready[("d", dc)] = (psem.sem, m)
            else:
                prev_pe = down_mm.get(di - 2)
                psem = S["pand"][dc % 2]
                m = psem.inc(16)
                pe_gate2 = gu_done[-1] if di < 2 else None
                def sldc(e, dc=dc, prev_pe=prev_pe, psem=psem, pe_gate2=pe_gate2):
                    if pe_gate2 is not None:
                        e.wait_ge(PE.sem, pe_gate2)
                    if prev_pe is not None:
                        e.wait_ge(PE.sem, prev_pe)
                    e.dma_start(out=B["pan_d"][dc % 2][:],
                                in_=caches["d"][dc]).then_inc(psem.sem, 16)
                P.em("sync", sldc)
                pan_ready[("d", dc)] = (psem.sem, m)

            semd, vd = pan_ready[("d", dc)]
            xm = xrl[di % 4].inc(16)
            def gxr(e, t=t, dc=dc, di=di,
                    rm=(res_marks[di - 4] if di >= 4 else None),
                    gug=(gu_done[-1] if di < 4 else None)):
                if gug is not None:
                    e.wait_ge(PE.sem, gug)
                if di >= 4:
                    e.wait_ge(VE.sem, yev[di - 4])
                if rm is not None:
                    e.wait_ge(rsw[di % 4].sem, rm)
                e.dma_start(out=B["xr"][di % 4][:],
                            in_=xfm[dc, :, t * T:(t + 1) * T]).then_inc(xrl[di % 4].sem, 16)
            P.em("gpsimd", gxr)

            def tdn(e, t=t, dc=dc, di=di, semd=semd, vd=vd,
                    hm0=hload[(t, 0)], hm1=hload[(t, 1)]):
                e.wait_ge(semd, vd)
                e.wait_ge(hld[0].sem, hm0)
                e.wait_ge(hld[1].sem, hm1)
                if di >= 2:
                    e.wait_ge(VE.sem, yev[di - 2])
                if di == 0:
                    e.wait_ge(VE.sem, hep[-1])
                py = B["p_y"][dc % 2]
                ins = None
                for ic in range(ICN):
                    half = 0 if ic < 43 else 1
                    ofs = (ic - (0 if half == 0 else 43)) * T
                    ins = e.matmul(py[:], B["pan_d"][dc % 2][:, ic * 128:(ic + 1) * 128],
                                   B["h_half"][half][:, ofs:ofs + T],
                                   start=(ic == 0), stop=(ic == ICN - 1))
                ins.then_inc(PE.sem, 1)
            P.em("tensor", tdn)
            down_mm[di] = PE.inc(1)
            if dc == DC - 1:
                down_tile_last[t] = PE.n

            def vy(e, t=t, dc=dc, di=di, pm=PE.n, xm=xm):
                e.wait_ge(PE.sem, pm)
                e.wait_ge(xrl[di % 4].sem, xm)
                e.tensor_tensor(B["xr"][di % 4][:], B["xr"][di % 4][:],
                                B["p_y"][dc % 2][:], ALU.add).then_inc(VE.sem, 1)
            P.em("vector", vy)
            yev.append(VE.inc(1))

            def gyw(e, t=t, dc=dc, di=di, vm=VE.n):
                e.wait_ge(VE.sem, vm)
                e.dma_start(out=xfm[dc, :, t * T:(t + 1) * T],
                            in_=B["xr"][di % 4][:]).then_inc(rsw[di % 4].sem, 16)
            P.em("gpsimd", gyw)
            res_marks.append(rsw[di % 4].inc(16))
            if mode == "dequant" and t == 0 and dc + 2 < DC:
                produce_d(dc + 2)

    return {"res_done": [(rsw[k].sem, rsw[k].n) for k in range(4)],
            "last_yev": yev[-1]}


H = 32
DH = 128
SCALE = 1.0 / float(np.sqrt(128.0))
EE = 2560
L2C = float(np.log2(10000.0) / 64.0)
L_HI = float(np.float32(round(L2C * (1 << 17)) / (1 << 17)))
L_LO = float(np.float32(L2C - L_HI))
LN2 = float(np.log(2.0))
INV2PI = float(1.0 / (2 * np.pi))
C1 = float(np.float32(6.28125))
C2 = float(np.float32(2 * np.pi - 6.28125))
C3 = float(np.float32(2 * np.pi - C1 - float(np.float32(2 * np.pi - 6.28125))))
PI = float(np.pi)


def emit_wstage(P, S, B, wq_ap, wk_ap, wv_ap, wo_ap, wqT_d, woT_d, wk_bf_d, wv_bf_d,
                dep_marks):
    PE, VE, AC = S["PE"], S["VE"], S["AC"]
    ws1, ws2, ws3 = S["ws1"], S["ws2"], S["ws3"]
    def vcast(e):
        e.tensor_copy(B["eye_b"][:], B["eye_f"][:]).then_inc(VE.sem, 1)
    P.em("vector", vcast)
    eyeb_ve = VE.inc(1)

    for wi, (w_ap, wT_d) in enumerate([(wq_ap, wqT_d), (wo_ap, woT_d)]):
        for ch in range(32):
            wch = wi * 32 + ch
            sl = wch % 2
            m = ws1[sl].inc(16)
            def ld(e, w_ap=w_ap, ch=ch, sl=sl, wch=wch,
                   pv=(AC2M.get(wch - 2)), dep_marks=dep_marks):
                if wch == 0:
                    for sm, vv_ in dep_marks:
                        e.wait_ge(sm, vv_)
                if pv is not None:
                    e.wait_ge(S["ac2"][sl].sem, pv)
                e.dma_start(out=B["ig_x"][sl][:],
                            in_=w_ap[ch * 128:(ch + 1) * 128, :]).then_inc(ws1[sl].sem, 16)
            P.em("sync", ld)

            def cst(e, sl=sl, m=m, wch=wch, wv=(WS2M.get(wch - 2))):
                e.wait_ge(ws1[sl].sem, m)
                if wv is not None:
                    e.wait_ge(ws2[sl].sem, wv)
                e.activation(B["wst"][sl][:], B["ig_x"][sl][:],
                             AF.Copy).then_inc(S["ac2"][sl].sem, 1)
            P.em("scalar", cst)
            AC2M[wch] = S["ac2"][sl].inc(1)

            def st(e, ch=ch, sl=sl, wch=wch, am=AC2M[wch], wbf=(S["wbfd"][wi])):
                e.wait_ge(S["ac2"][sl].sem, am)
                e.dma_start(out=wbf[ch * 128:(ch + 1) * 128, :],
                            in_=B["wst"][sl][:]).then_inc(ws2[sl].sem, 16)
            P.em("gpsimd", st)
            WS2M[wch] = ws2[sl].inc(16)

    for wi, wT_d in enumerate([wqT_d, woT_d]):
        for dc in range(32):
            tch = wi * 32 + dc
            sl = tch % 2
            m = ws3[sl].inc(16)
            def tld(e, wi=wi, dc=dc, sl=sl, tch=tch,
                    wm=[(ws2[k].sem, ws2[k].n) for k in range(2)],
                    pv=(WS4M.get(tch - 2)), wbf=(S["wbfd"][wi])):
                for sm, vv_ in wm:
                    e.wait_ge(sm, vv_)
                if pv is not None:
                    e.wait_ge(S["ws4"][sl].sem, pv)
                e.dma_start_transpose(B["wst"][sl][:],
                                      wbf[:, dc * 128:(dc + 1) * 128]).then_inc(ws3[sl].sem, 16)
            P.em("sync", tld)

            def tst(e, dc=dc, sl=sl, m=m, wT_d=wT_d):
                e.wait_ge(ws3[sl].sem, m)
                e.dma_start(out=wT_d[dc * 128:(dc + 1) * 128, :],
                            in_=B["wst"][sl][:]).then_inc(S["ws4"][sl].sem, 16)
            P.em("gpsimd", tst)
            WS4M[tch] = S["ws4"][sl].inc(16)

    for wi2, (w_ap, wbf) in enumerate([(wk_ap, wk_bf_d), (wv_ap, wv_bf_d)]):
        sl = wi2 % 2
        m = ws1[sl].inc(16)
        def ldk(e, w_ap=w_ap, sl=sl,
                kv=[(S["ws4"][k].sem, S["ws4"][k].n) for k in range(2)]):
            for sm, vv_ in kv:
                e.wait_ge(sm, vv_)
            e.dma_start(out=B["ig_x"][sl][:], in_=w_ap).then_inc(ws1[sl].sem, 16)
        P.em("sync", ldk)

        def cstk(e, sl=sl, m=m):
            e.wait_ge(ws1[sl].sem, m)
            e.activation(B["wst"][sl][:], B["ig_x"][sl][:],
                         AF.Copy).then_inc(S["ac2"][sl].sem, 1)
        P.em("scalar", cstk)
        am = S["ac2"][sl].inc(1)

        def stk(e, sl=sl, am=am, wbf=wbf):
            e.wait_ge(S["ac2"][sl].sem, am)
            e.dma_start(out=wbf, in_=B["wst"][sl][:]).then_inc(ws2[sl].sem, 16)
        P.em("gpsimd", stk)
        WSKV.append(ws2[sl].inc(16))

    for wi, (wbf, dst) in enumerate([(wk_bf_d, "wkT"), (wv_bf_d, "wvT")]):
        for dc in range(32):
            m2 = S["ws5"].inc(16)
            def tl2(e, wi=wi, dc=dc, dst=dst, wbf=wbf,
                    kv=[(ws2[k].sem, ws2[k].n) for k in range(2)]):
                if dc == 0:
                    for sm, vv_ in kv:
                        e.wait_ge(sm, vv_)
                e.dma_start_transpose(B[dst][:, dc * 128:(dc + 1) * 128],
                                      wbf[:, dc * 128:(dc + 1) * 128]).then_inc(S["ws5"].sem, 16)
            P.em("sync", tl2)
    return {"wT_done": [(S["ws4"][0].sem, S["ws4"][0].n), (S["ws4"][1].sem, S["ws4"][1].n),
                        (ws2[0].sem, ws2[0].n), (ws2[1].sem, ws2[1].n)],
            "kvT_done": [(S["ws5"].sem, S["ws5"].n)], "eyeb_ve": eyeb_ve}


def emit_rope_alpha(P, S, B, tpos_ap, kmask_ap, flo_ap, eyeb_ve, dep_yev):
    PE, VE, AC = S["PE"], S["VE"], S["AC"]
    rp = S["rp"]
    m0 = rp.inc(48)
    def ld(e):
        e.dma_start(out=B["tpos"][:], in_=tpos_ap).then_inc(rp.sem, 16)
        e.dma_start(out=B["kmask"][:], in_=kmask_ap).then_inc(rp.sem, 16)
        e.dma_start(out=B["flo"][0:1, 0:1], in_=flo_ap).then_inc(rp.sem, 16)
    P.em("gpsimd", ld)

    def gio(e, dy=dep_yev):
        e.wait_ge(S["VE"].sem, dy)
        e.iota(B["jrow"][0:1, :].rearrange("p (a b) -> p a b", b=2), [[1, 64], [0, 2]],
               base=0, channel_multiplier=0, allow_small_or_imprecise_dtypes=True)
        e.iota(B["arows"][0:1, 256:512], [[1, 256]], base=0, channel_multiplier=0,
               allow_small_or_imprecise_dtypes=True).then_inc(S["GP"].sem, 1)
    P.em("gpsimd", gio)
    gp0 = S["GP"].inc(1)

    vb = VE.n
    def vif(e, gp0=gp0, m0=m0, vb=vb):
        e.wait_ge(S["GP"].sem, gp0)
        e.wait_ge(rp.sem, m0)
        J, G_, NF, NI, PH = (B["jrow"][0:1, :], B["rope_g"][0:1, 0:128],
                            B["rope_f"][0:1, 0:128], B["rope_i"][0:1, 0:128],
                            B["rope_f"][0:1, 128:256])
        e.tensor_scalar(G_, J, L_HI, None, ALU.mult).then_inc(VE.sem, 1)
        e.wait_ge(VE.sem, vb + 1)
        e.scalar_tensor_tensor(G_, J, L_LO, G_, ALU.mult, ALU.add).then_inc(VE.sem, 1)
        e.wait_ge(VE.sem, vb + 2)
        e.tensor_scalar(NF, G_, -0.5, 12582912.0, ALU.add, ALU.add).then_inc(VE.sem, 1)
        e.wait_ge(VE.sem, vb + 3)
        e.tensor_scalar(NF, NF, 12582912.0, None, ALU.subtract).then_inc(VE.sem, 1)
        e.wait_ge(VE.sem, vb + 4)
        e.nop().then_inc(VE.sem, 1)
        e.wait_ge(VE.sem, vb + 5)
        e.tensor_tensor(PH, G_, NF, ALU.subtract).then_inc(VE.sem, 1)
        e.wait_ge(VE.sem, vb + 6)
        e.tensor_scalar(PH, PH, -LN2, None, ALU.mult).then_inc(VE.sem, 1)
        e.wait_ge(VE.sem, vb + 7)
        e.memset(G_, 1.0).then_inc(VE.sem, 1)
        k_ = vb + 8
        for kk in range(10, 0, -1):
            e.wait_ge(VE.sem, k_)
            e.tensor_tensor(B["rope_g"][0:1, 128:256], G_, PH, ALU.mult).then_inc(VE.sem, 1)
            k_ += 1
            e.wait_ge(VE.sem, k_)
            e.tensor_scalar(G_, B["rope_g"][0:1, 128:256], 1.0 / kk, 1.0,
                            ALU.mult, ALU.add).then_inc(VE.sem, 1)
            k_ += 1
        e.wait_ge(VE.sem, k_)
        e.tensor_scalar(NF, NF, -8388608.0, 1065353216.0, ALU.mult, ALU.add).then_inc(VE.sem, 1)
        k_ += 1
        e.wait_ge(VE.sem, k_)
        e.tensor_copy(NI, NF).then_inc(VE.sem, 1)
        k_ += 1
        e.wait_ge(VE.sem, k_)
        e.tensor_tensor(B["rope_f"][0:1, 0:128], G_, NI.bitcast(F32),
                        ALU.mult).then_inc(VE.sem, 1)
    P.em("vector", vif)
    VE.inc(8 + 20 + 3)
    vif_m = VE.n

    vb2 = VE.n
    def val(e, vb2=vb2):
        AR = B["arows"]
        e.tensor_scalar(AR[0:1, 256:512], AR[0:1, 256:512], 1.0 / 255.0, None,
                        ALU.mult).then_inc(VE.sem, 1)
        e.wait_ge(VE.sem, vb2 + 1)
        e.tensor_scalar(AR[0:1, 0:256], AR[0:1, 256:512], -1.0, 1.0,
                        ALU.mult, ALU.add).then_inc(VE.sem, 1)
        e.wait_ge(VE.sem, vb2 + 2)
        e.tensor_scalar(AR[0:1, 512:768], AR[0:1, 0:256], B["flo"][0:1, 0:1], None,
                        ALU.mult).then_inc(VE.sem, 1)
        e.wait_ge(VE.sem, vb2 + 3)
        e.tensor_scalar(AR[0:1, 768:1024], AR[0:1, 512:768], -1.0, 1.0,
                        ALU.mult, ALU.add).then_inc(VE.sem, 1)
    P.em("vector", val)
    VE.inc(4)
    val_m = VE.n

    def tkm(e, vm=val_m, dy=dep_yev):
        e.wait_ge(VE.sem, max(vm, dy))
        e.transpose(B["p_u"][0][:, 0:20], B["kmask"][:], B["eye_f"][0:20, 0:20]).then_inc(PE.sem, 1)
    P.em("tensor", tkm)
    tk_m = PE.inc(1)

    def vkm(e, pm=tk_m):
        e.wait_ge(PE.sem, pm)
        e.tensor_copy(B["kmask_c"][:], B["p_u"][0][:, 0:20]).then_inc(VE.sem, 1)
    P.em("vector", vkm)
    vkm_m = VE.inc(1)

    def tab(e, vm=vkm_m, dy=dep_yev):
        e.wait_ge(VE.sem, max(vm, dy))
        e.matmul(B["p_bank"][2][:, 0:512], B["ones_f"][:], B["arows"][0:1, 0:512],
                 start=True, stop=True)
        e.matmul(B["p_bank"][3][:, 0:512], B["ones_f"][:], B["arows"][0:1, 512:1024],
                 start=True, stop=True).then_inc(PE.sem, 1)
    P.em("tensor", tab)
    ab_pe = PE.inc(1)

    def vab(e, pm=ab_pe):
        e.wait_ge(PE.sem, pm)
        e.tensor_copy(B["ab"][:, 0:512], B["p_bank"][2][:, 0:512]).then_inc(VE.sem, 1)
        e.tensor_copy(B["ab"][:, 512:1024], B["p_bank"][3][:, 0:512]).then_inc(VE.sem, 1)
    P.em("vector", vab)
    VE.inc(2)
    ab_ve = VE.n

    for ss in range(EE // T):
        def tmf(e, ss=ss, vm=vif_m, vk=vkm_m, dy=dep_yev):
            if ss == 0:
                e.wait_ge(VE.sem, max(vm, vk, dy))
            else:
                e.wait_ge(VE.sem, ROPE_VM[ss - 1])
            e.matmul(B["p_g"][0][:], B["rope_f"][0:1, 0:128],
                     B["tpos"][0:1, ss * T:(ss + 1) * T],
                     start=True, stop=True).then_inc(PE.sem, 1)
        P.em("tensor", tmf)
        pm = PE.inc(1)

        vb3 = VE.n
        def vrope(e, ss=ss, pm=pm, vb3=vb3):
            e.wait_ge(PE.sem, pm)
            FF, TT_, KI = B["rope_f"], B["rope_g"], B["rope_i"]
            e.tensor_copy(FF[:], B["p_g"][0][:]).then_inc(VE.sem, 1)
            e.wait_ge(VE.sem, vb3 + 1)
            e.tensor_scalar(TT_[:], FF[:], INV2PI, 12582912.0, ALU.mult,
                            ALU.add).then_inc(VE.sem, 1)
            e.wait_ge(VE.sem, vb3 + 2)
            e.tensor_scalar(TT_[:], TT_[:], 12582912.0, None,
                            ALU.subtract).then_inc(VE.sem, 1)
            e.wait_ge(VE.sem, vb3 + 3)
            e.nop().then_inc(VE.sem, 1)
            e.wait_ge(VE.sem, vb3 + 4)
            e.scalar_tensor_tensor(FF[:], TT_[:], -C1, FF[:], ALU.mult, ALU.add).then_inc(VE.sem, 1)
            e.wait_ge(VE.sem, vb3 + 5)
            e.scalar_tensor_tensor(FF[:], TT_[:], -C2, FF[:], ALU.mult, ALU.add).then_inc(VE.sem, 1)
            e.wait_ge(VE.sem, vb3 + 6)
            e.scalar_tensor_tensor(FF[:], TT_[:], -C3, FF[:], ALU.mult, ALU.add).then_inc(VE.sem, 1)
        P.em("vector", vrope)
        VE.inc(7)
        vr_m = VE.n

        def arope(e, ss=ss, vm=vr_m):
            e.wait_ge(VE.sem, vm)
            e.activation(B["sinr_e"][:, ss * T:(ss + 1) * T], B["rope_f"][:],
                         AF.Sin).then_inc(AC.sem, 1)
        P.em("scalar", arope)
        a1 = AC.inc(1)

        vb4 = VE.n
        def vwrap(e, ss=ss, am=a1, vb4=vb4):
            e.wait_ge(AC.sem, am)
            FF, TT_ = B["rope_f"], B["rope_g"]
            e.tensor_scalar(FF[:], FF[:], PI / 2, None, ALU.add).then_inc(VE.sem, 1)
            e.wait_ge(VE.sem, vb4 + 1)
            e.tensor_single_scalar(TT_[:], FF[:], PI, ALU.is_gt).then_inc(VE.sem, 1)
            e.wait_ge(VE.sem, vb4 + 2)
            e.scalar_tensor_tensor(FF[:], TT_[:], -2 * PI, FF[:], ALU.mult,
                                   ALU.add).then_inc(VE.sem, 1)
        P.em("vector", vwrap)
        VE.inc(3)
        vw_m = VE.n

        def arope2(e, ss=ss, vm=vw_m):
            e.wait_ge(VE.sem, vm)
            e.activation(B["cos_e"][:, ss * T:(ss + 1) * T], B["rope_f"][:],
                         AF.Sin).then_inc(AC.sem, 1)
        P.em("scalar", arope2)
        a2 = AC.inc(1)

        vb5 = VE.n
        def vsr(e, ss=ss, am=a2, vb5=vb5):
            e.wait_ge(AC.sem, am)
            sl_ = slice(ss * T, (ss + 1) * T)
            e.tensor_scalar(B["sinr_e"][0:64, sl_], B["sinr_e"][0:64, sl_],
                            -1.0, None, ALU.mult).then_inc(VE.sem, 1)
        P.em("vector", vsr)
        ROPE_VM.append(VE.inc(1))
    return {"rope_ve": max(ROPE_VM[-1], ab_ve), "alpha_ve": val_m}


def emit_attn(P, S, B, xfm, q_scr, wqT_d, woT_d, kg_d, vg_d, kc_d, vc_d,
              dep_marks, wsm, rm, ncores):
    PE, VE, AC, GP = S["PE"], S["VE"], S["AC"], S["GP"]
    lnx = S["lnx"]
    NTT = NT

    kv_ev = []
    q_w = []
    ex = S["ex"]
    cc = S["cc"]

    def emit_p2b():
        m1 = ex.inc(16 * 6)
        def gcon(e, kv=kv_ev[-1]):
            e.wait_ge(VE.sem, kv)
            e.dma_start(out=kc_d[0:128, :], in_=B["k_ext"][:, 256:512]).then_inc(ex.sem, 16)
            e.dma_start(out=kc_d[128:256, :], in_=B["k_ext"][:, 2048:2304]).then_inc(ex.sem, 16)
            for si in range(2):
                e.dma_start(out=vc_d[si * 128:(si + 1) * 128, :],
                            in_=B["v_ext"][:, (2 + si) * 128:(3 + si) * 128]).then_inc(ex.sem, 16)
            for si in range(2):
                e.dma_start(out=vc_d[(2 + si) * 128:(3 + si) * 128, :],
                            in_=B["v_ext"][:, (16 + si) * 128:(17 + si) * 128]).then_inc(ex.sem, 16)
        P.em("gpsimd", gcon)

        def gcc(e, m1=m1):
            e.wait_ge(ex.sem, m1)
            e.collective_compute("AllGather", ALU.bypass,
                                 replica_groups=[list(range(ncores))],
                                 ins=[kc_d.opt()], outs=[kg_d.opt()]).then_inc(cc.sem, 1)
            e.collective_compute("AllGather", ALU.bypass,
                                 replica_groups=[list(range(ncores))],
                                 ins=[vc_d.opt()], outs=[vg_d.opt()]).then_inc(cc.sem, 1)
        P.em("gpsimd", gcc)
        cc_m = cc.inc(2)

        m2 = ex.inc(16 * 6)
        def ghal(e, cc_m=cc_m):
            e.wait_ge(cc.sem, cc_m)
            pid = e.partition_id()
            lidx = (pid + ncores - 1) % ncores
            ridx = (pid + 1) % ncores
            e.dma_start(out=B["k_ext"][:, 0:256],
                        in_=kg_d[bass.ts(lidx * 2 + 1, 128)]).then_inc(ex.sem, 16)
            e.dma_start(out=B["k_ext"][:, 2304:2560],
                        in_=kg_d[bass.ts(ridx * 2, 128)]).then_inc(ex.sem, 16)
            for si in range(2):
                e.dma_start(out=B["v_ext"][:, si * 128:(si + 1) * 128],
                            in_=vg_d[bass.ts(lidx * 4 + 2 + si, 128)]).then_inc(ex.sem, 16)
            for si in range(2):
                e.dma_start(out=B["v_ext"][:, (18 + si) * 128:(19 + si) * 128],
                            in_=vg_d[bass.ts(ridx * 4 + si, 128)]).then_inc(ex.sem, 16)
        P.em("gpsimd", ghal)


    for t in range(NTT):
        parked, mm_m = [], []
        for dc in range(DC):
            i = t * DC + dc
            xm = lnx[i % 2].inc(16)
            def gld(e, t=t, dc=dc, i=i,
                    deps=(dep_marks if (t == 0 and dc == 0) else None),
                    pk=(parked[dc - 2] if dc >= 2 else None),
                    qpe=(QT_PE[t - 1] if (t >= 1 and dc == 0) else None)):
                if deps is not None:
                    for sm, vv_ in deps:
                        e.wait_ge(sm, vv_)
                if pk is not None:
                    e.wait_ge(VE.sem, pk)
                if qpe is not None:
                    e.wait_ge(PE.sem, qpe)
                e.dma_start(out=B["ln_x"][i % 2][:],
                            in_=xfm[dc, :, t * T:(t + 1) * T]).then_inc(lnx[i % 2].sem, 16)
            P.em("gpsimd", gld)

            def vpark(e, dc=dc, i=i, xm=xm):
                e.wait_ge(lnx[i % 2].sem, xm)
                e.tensor_copy(B["xn_t"][:, dc * T:(dc + 1) * T],
                              B["ln_x"][i % 2][:]).then_inc(VE.sem, 1)
            P.em("vector", vpark)
            parked.append(VE.inc(1))

            def apark(e, dc=dc, i=i, vm=VE.n,
                      pmm=(mm_m[dc - 2] if dc >= 2 else None)):
                e.wait_ge(VE.sem, vm)
                if pmm is not None:
                    e.wait_ge(PE.sem, pmm)
                e.activation(B["ln_sq"][i % 2][:], B["xn_t"][:, dc * T:(dc + 1) * T],
                             AF.Square).then_inc(AC.sem, 1)
            P.em("scalar", apark)
            AC.inc(1)

            def tmm(e, t=t, dc=dc, i=i, am=AC.n,
                    rope_dep=(rm["rope_ve"] if (t == 0 and dc == 0) else None),
                    qev=(QROPE_V[-1] if (t >= 1 and dc == 0 and QROPE_V) else None)):
                e.wait_ge(AC.sem, am)
                if rope_dep is not None:
                    e.wait_ge(VE.sem, rope_dep)
                if qev is not None:
                    e.wait_ge(VE.sem, qev)
                if dc == 0 and t >= 1:
                    e.wait_ge(VE.sem, AROW_V[t - 1])
                e.matmul(B["p_sx"], B["ones_b"][:], B["xn_t"][:, dc * T:(dc + 1) * T],
                         start=(dc == 0), stop=(dc == DC - 1))
                e.matmul(B["p_sq"], B["ones_b"][:], B["ln_sq"][i % 2][:],
                         start=(dc == 0), stop=(dc == DC - 1)).then_inc(PE.sem, 1)
            P.em("tensor", tmm)
            mm_m.append(PE.inc(1))

        vb = VE.n
        def vrow(e, pm=PE.n, vb=vb):
            e.wait_ge(PE.sem, pm)
            e.tensor_scalar(B["r_mu"], B["p_sx"], 1.0 / D, None,
                            ALU.mult).then_inc(VE.sem, 1)
            e.wait_ge(VE.sem, vb + 1)
            e.tensor_tensor(B["r_t1"], B["r_mu"], B["r_mu"], ALU.mult).then_inc(VE.sem, 1)
            e.wait_ge(VE.sem, vb + 2)
            e.tensor_scalar(B["r_t1"], B["r_t1"], EPS, None, ALU.subtract).then_inc(VE.sem, 1)
            e.wait_ge(VE.sem, vb + 3)
            e.scalar_tensor_tensor(B["r_var"], B["p_sq"], 1.0 / D, B["r_t1"],
                                   ALU.mult, ALU.subtract).then_inc(VE.sem, 1)
        P.em("vector", vrow)
        VE.inc(4)
        AROW_V.append(VE.n)
        vrow_m = VE.n

        ab = AC.n
        def arow(e, vm=vrow_m, ab=ab):
            e.wait_ge(VE.sem, vm)
            e.activation(B["r_t1"], B["r_var"], AF.Ln).then_inc(AC.sem, 1)
            e.wait_ge(AC.sem, ab + 1)
            e.activation(B["r_r"], B["r_t1"], AF.Exp, scale=-0.5).then_inc(AC.sem, 1)
        P.em("scalar", arow)
        AC.inc(2)

        def tbc(e, t=t, am=AC.n, nv=(NORM_V[t - 1] if t >= 1 else None)):
            e.wait_ge(AC.sem, am)
            if nv is not None:
                e.wait_ge(VE.sem, nv)
            e.matmul(B["p_mu"][:], B["ones_f"][:], B["r_mu"], start=True, stop=True)
            e.matmul(B["p_r"][:], B["ones_f"][:], B["r_r"],
                     start=True, stop=True).then_inc(PE.sem, 1)
        P.em("tensor", tbc)
        bc_pe = PE.inc(1)

        norm_m = []
        for dc in range(DC):
            vb2 = VE.n
            def vn(e, dc=dc, pm=bc_pe, vb2=vb2):
                if dc == 0:
                    e.wait_ge(PE.sem, pm)
                e.wait_ge(VE.sem, vb2)
                sl_ = slice(dc * T, (dc + 1) * T)
                e.tensor_tensor(B["ln_t"][:], B["xn_t"][:, sl_], B["p_mu"][:],
                                ALU.subtract).then_inc(VE.sem, 1)
                e.wait_ge(VE.sem, vb2 + 1)
                e.tensor_tensor(B["xn_t"][:, sl_], B["ln_t"][:], B["p_r"][:],
                                ALU.mult).then_inc(VE.sem, 1)
            P.em("vector", vn)
            VE.inc(2)
            norm_m.append(VE.n)
        NORM_V.append(VE.n)

        def tkv(e, t=t, nm=norm_m[-1], kvm=(wsm["kvT_done"])):
            e.wait_ge(VE.sem, nm)
            if t == 0:
                for sm, vv_ in kvm:
                    e.wait_ge(sm, vv_)
            for dc in range(DC):
                e.matmul(B["p_g"][0][:], B["wkT"][:, dc * 128:(dc + 1) * 128],
                         B["xn_t"][:, dc * T:(dc + 1) * T],
                         start=(dc == 0), stop=(dc == DC - 1))
            ins = None
            for dc in range(DC):
                ins = e.matmul(B["p_g"][1][:], B["wvT"][:, dc * 128:(dc + 1) * 128],
                               B["xn_t"][:, dc * T:(dc + 1) * T],
                               start=(dc == 0), stop=(dc == DC - 1))
            ins.then_inc(PE.sem, 1)
        P.em("tensor", tkv)
        kv_pe = PE.inc(1)

        ofs = 256 + t * T
        vb3 = VE.n
        def vkr(e, t=t, pm=kv_pe, ofs=ofs, vb3=vb3):
            e.wait_ge(PE.sem, pm)
            e.wait_ge(VE.sem, vb3)
            sl_ = slice(ofs, ofs + T)
            e.tensor_tensor(B["qc_t"][:], B["p_g"][0][:], B["cos_e"][:, sl_],
                            ALU.mult).then_inc(VE.sem, 1)
            e.tensor_tensor(B["qs_t"][0:64, :], B["p_g"][0][64:128, :],
                            B["sinr_e"][0:64, sl_], ALU.mult).then_inc(VE.sem, 1)
            e.tensor_tensor(B["qs_t"][64:128, :], B["p_g"][0][0:64, :],
                            B["sinr_e"][64:128, sl_], ALU.mult).then_inc(VE.sem, 1)
            e.wait_ge(VE.sem, vb3 + 3)
            e.tensor_tensor(B["k_ext"][:, sl_], B["qc_t"][:], B["qs_t"][:],
                            ALU.add).then_inc(VE.sem, 1)
            e.wait_ge(VE.sem, vb3 + 4)
            e.tensor_copy(B["w_sb"][0][:], B["p_g"][1][:]).then_inc(VE.sem, 1)
        P.em("vector", vkr)
        VE.inc(5)
        vkr_m = VE.n

        def tvt(e, t=t, vm=vkr_m, ev=wsm["eyeb_ve"]):
            e.wait_ge(VE.sem, vm)
            pview = B["p_bank"][2][:].bitcast(BF16)
            ins = None
            for k in range(4):
                ins = e.transpose(pview[:, k * 128:(k + 1) * 128],
                                  B["w_sb"][0][:, k * 128:(k + 1) * 128],
                                  B["eye_b"][:])
            ins.then_inc(PE.sem, 1)
        P.em("tensor", tvt)
        vt_pe = PE.inc(1)

        def vvt(e, t=t, pm=vt_pe, ofs=ofs):
            e.wait_ge(PE.sem, pm)
            e.tensor_copy(B["v_ext"][:, ofs:ofs + T],
                          B["p_bank"][2][:].bitcast(BF16)[:, 0:512]).then_inc(VE.sem, 1)
        P.em("vector", vvt)
        kv_ev.append(VE.inc(1))

        if t == NTT - 1:
            emit_p2b()

        for og in range(4):
            gidx = t * 4 + og
            for dc in range(DC):
                sl2 = QPN[0] % 12
                m = S["wqp"][sl2].inc(16)
                def pld(e, dc=dc, og=og, sl2=sl2,
                        pv=(QPM.get(QPN[0] - 12)), wtd=wsm["wT_done"],
                        first=(QPN[0] < 12), wqT_d=wqT_d):
                    if first:
                        for sm, vv_ in wtd:
                            e.wait_ge(sm, vv_)
                    if pv is not None:
                        e.wait_ge(PE.sem, pv)
                    e.dma_start(out=B["wpan"][sl2][:],
                                in_=wqT_d[dc * 128:(dc + 1) * 128,
                                          og * 1024:(og + 1) * 1024]).then_inc(S["wqp"][sl2].sem, 16)
                P.em("sync" if QPN[0] % 2 == 0 else "scalar", pld)

                def tq(e, t=t, og=og, dc=dc, sl2=sl2, m=m, gidx=gidx,
                       kv=kv_ev[-1]):
                    e.wait_ge(S["wqp"][sl2].sem, m)
                    if dc == 0:
                        e.wait_ge(VE.sem, kv)
                        if gidx >= 1:
                            e.wait_ge(VE.sem, QROPE_V[gidx - 1])
                    ins = None
                    for hh in range(8):
                        ins = e.matmul(B["p_bank"][hh][:],
                                       B["wpan"][sl2][:, hh * 128:(hh + 1) * 128],
                                       B["xn_t"][:, dc * T:(dc + 1) * T],
                                       start=(dc == 0), stop=(dc == DC - 1))
                    ins.then_inc(PE.sem, 1)
                P.em("tensor", tq)
                QPM[QPN[0]] = PE.inc(1)
                QPN[0] += 1

            eofs = 256 + t * T
            vb4 = VE.n
            def vqr(e, t=t, og=og, pm=PE.n, eofs=eofs, vb4=vb4,
                    qwp=(QR_WR[t - 1] if (t >= 1 and og == 0) else None)):
                e.wait_ge(PE.sem, pm)
                if qwp is not None:
                    e.wait_ge(qwp[0], qwp[1])
                sl_ = slice(eofs, eofs + T)
                k_ = vb4
                for hh in range(8):
                    h = og * 8 + hh
                    Qp = B["p_bank"][hh][:]
                    e.wait_ge(VE.sem, k_)
                    e.tensor_tensor(B["qc_t"][:], Qp, B["cos_e"][:, sl_],
                                    ALU.mult).then_inc(VE.sem, 1)
                    e.tensor_tensor(B["qs_t"][0:64, :], Qp[64:128, :],
                                    B["sinr_e"][0:64, sl_], ALU.mult).then_inc(VE.sem, 1)
                    e.tensor_tensor(B["qs_t"][64:128, :], Qp[0:64, :],
                                    B["sinr_e"][64:128, sl_], ALU.mult).then_inc(VE.sem, 1)
                    k_ += 3
                    e.wait_ge(VE.sem, k_)
                    e.tensor_tensor(B["qr_t"][:, h * T:(h + 1) * T],
                                    B["qc_t"][:], B["qs_t"][:],
                                    ALU.add).then_inc(VE.sem, 1)
                    k_ += 1
            P.em("vector", vqr)
            VE.inc(32)
            QROPE_V.append(VE.n)

        def gqw(e, t=t, vm=QROPE_V[-1]):
            e.wait_ge(VE.sem, vm)
            dst = q_scr.rearrange("h p t -> p h t")[:, :, t * T:(t + 1) * T]
            src = B["qr_t"][:].rearrange("p (h t) -> p h t", h=H)
            e.dma_start(out=dst, in_=src).then_inc(S["qsw"][t % 2].sem, 16)
        P.em("gpsimd", gqw)
        q_w.append(S["qsw"][t % 2].inc(16))
        QT_PE.append(PE.n)
        QR_WR.append((S["qsw"][t % 2].sem, q_w[-1]))

    xrl = S["xrl"]
    arsw = S["arsw"]
    axr_m = {}
    res_m = {}
    och_pe = {}

    blend_cnt = [0]

    for t in range(NTT + 1):
        lo = max(512 * t, 256)
        hi = 512 * t + 768
        if t == NTT:
            lo, hi = 2048, 2304
            rofs = 512
            tt_ = 3
        else:
            rofs = lo - 512 * t
            tt_ = t
        m = S["qrl"][t % 2].inc(16)
        def gqr(e, t=t, lo=lo, hi=hi, rofs=rofs, tt_=tt_,
                qw=(QR_WR[min(tt_, NTT - 1)]),
                qwall=([(S["qsw"][k].sem, S["qsw"][k].n) for k in range(2)]
                       if t == 0 else None),
                exm=(ex.n if t == 0 else None),
                pvw=(WIN_PE[-1] if (t >= 2 and WIN_PE) else None)):
            e.wait_ge(qw[0], qw[1])
            if qwall is not None:
                for sm, vv_ in qwall:
                    e.wait_ge(sm, vv_)
            if exm is not None:
                e.wait_ge(ex.sem, exm)
            if pvw is not None:
                e.wait_ge(PE.sem, pvw)
            if t == NTT:
                dst = B["q_roll"][:].rearrange("p (h t) -> p h t", h=H)[:, :, 512:768]
            else:
                dst = B["q_roll"][:].rearrange("p (h t) -> p h t", h=H)[:, :, rofs:768]
            src = q_scr.rearrange("h p t -> p h t")[:, :, lo - 256:hi - 256]
            e.dma_start(out=dst, in_=src).then_inc(S["qrl"][t % 2].sem, 16)
        P.em("gpsimd", gqr)

        if t == 0:
            wins = [(0, 256, 512), (1, 256, 768)]
        elif t < NTT:
            wins = [(2 * t, 512 * t, 512 * t + 512), (2 * t + 1, 512 * t + 256, 512 * t + 768)]
        else:
            wins = [(8, 2048, 2304)]

        for h in range(H):
            prev_r = None
            for wi_, (j, qlo, qhi) in enumerate(wins):
                qn = qhi - qlo
                klo = 256 * j
                rbase = 512 * tt_ if t < NTT else 512 * 3
                qof = qlo - rbase if t < NTT else 512
                def tsc(e, t=t, h=h, j=j, qn=qn, klo=klo, qof=qof,
                        m=m, sgm=(SIG_V[-1] if SIG_V else None),
                        opm=(OP_EV[-1] if OP_EV else None)):
                    e.wait_ge(S["qrl"][t % 2].sem, m)
                    if sgm is not None:
                        e.wait_ge(AC.sem, sgm)
                    if opm is not None:
                        e.wait_ge(VE.sem, opm)
                    pb = B["p_bank"]
                    ins = None
                    for ks in range(4):
                        ins = e.matmul(pb[ks][:, 0:qn],
                                       B["k_ext"][:, klo + ks * 128: klo + (ks + 1) * 128],
                                       B["q_roll"][:, h * 768 + qof: h * 768 + qof + qn],
                                       start=True, stop=True)
                    ins.then_inc(PE.sem, 1)
                P.em("tensor", tsc)
                sc_pe = PE.inc(1)

                def asg(e, t=t, h=h, j=j, qn=qn, klo=klo, pm=sc_pe,
                        wv=(WOUT_V[-1] if WOUT_V else None)):
                    e.wait_ge(PE.sem, pm)
                    if wv is not None:
                        e.wait_ge(VE.sem, wv)
                    pb = B["p_bank"]
                    ins = None
                    for ks in range(4):
                        ins = e.activation(B["sig"][ks][:, 0:qn], pb[ks][:, 0:qn],
                                           AF.Sigmoid, scale=SCALE,
                                           bias=B["kmask_c"][:, klo // 128 + ks: klo // 128 + ks + 1])
                    ins.then_inc(AC.sem, 1)
                P.em("scalar", asg)
                SIG_V.append(AC.inc(1))

                def tsum(e, t=t, h=h, qn=qn, klo=klo, am=SIG_V[-1],
                         rv=(RRD_V[-1] if RRD_V else None)):
                    e.wait_ge(AC.sem, am)
                    if rv is not None:
                        e.wait_ge(VE.sem, rv)
                    ins = None
                    for ks in range(4):
                        ins = e.matmul(B["p_sx"][0:1, 0:qn], B["ones_b"][:],
                                       B["sig"][ks][:, 0:qn],
                                       start=(ks == 0), stop=(ks == 3))
                    for ks in range(4):
                        ins = e.matmul(B["p_mu"][:, 0:qn],
                                       B["v_ext"][:, (klo // 128 + ks) * 128:(klo // 128 + ks + 1) * 128],
                                       B["sig"][ks][:, 0:qn],
                                       start=(ks == 0), stop=(ks == 3))
                    ins.then_inc(PE.sem, 1)
                P.em("tensor", tsum)
                sum_pe = PE.inc(1)
                WIN_PE.append(sum_pe)

                rslot = wi_ % 2
                vb6 = VE.n
                def vwo(e, t=t, h=h, qn=qn, pm=sum_pe, rslot=rslot, vb6=vb6):
                    e.wait_ge(PE.sem, pm)
                    e.tensor_copy(B["w_sb"][rslot][:, 0:qn],
                                  B["p_mu"][:, 0:qn]).then_inc(VE.sem, 1)
                    e.tensor_scalar(B["rrow"][0:1, rslot * T: rslot * T + qn],
                                    B["p_sx"][0:1, 0:qn], 1.0, 1e-30,
                                    ALU.mult, ALU.add).then_inc(VE.sem, 1)
                    e.wait_ge(VE.sem, vb6 + 2)
                    e.reciprocal(B["rrow"][0:1, rslot * T: rslot * T + qn],
                                 B["rrow"][0:1, rslot * T: rslot * T + qn]).then_inc(VE.sem, 1)
                P.em("vector", vwo)
                VE.inc(3)
                WOUT_V.append(VE.n)
                RRD_V.append(VE.n)

                emit_blk = None
                if t == 0 and wi_ == 1:
                    emit_blk = ("b0",)
                elif 0 < t < NTT:
                    if wi_ == 0:
                        emit_blk = ("carry",)
                    else:
                        emit_blk = ("intra",)
                elif t == NTT:
                    emit_blk = ("carry",)

                if emit_blk is not None:
                    kind = emit_blk[0]
                    bslot = blend_cnt[0] % 2
                    blend_cnt[0] += 1
                    bank_a = 7 if bslot == 0 else 5
                    vb7 = VE.n
                    def vmr(e, kind=kind, rslot=rslot, vb7=vb7, wv=WOUT_V[-1]):
                        e.wait_ge(VE.sem, wv)
                        e.tensor_copy(B["mrow"][0:1, 0:256],
                                      B["rrow"][0:1, rslot * T: rslot * T + 256]).then_inc(VE.sem, 1)
                        if kind != "carry":
                            src_ = (B["rrow"][0:1, (1 - rslot) * T:(1 - rslot) * T + 256]
                                    if kind == "b0" else
                                    B["rrow"][0:1, (1 - rslot) * T + 256:(1 - rslot) * T + 512])
                            e.wait_ge(VE.sem, vb7 + 1)
                            e.tensor_copy(B["mrow"][0:1, 256:512], src_).then_inc(VE.sem, 1)
                    P.em("vector", vmr)
                    VE.inc(1 if kind == "carry" else 2)
                    mr_m = VE.n

                    def tbr(e, bank_a=bank_a, vm=mr_m,
                            bv=(BLD_V[-2] if len(BLD_V) >= 2 else None)):
                        e.wait_ge(VE.sem, vm)
                        if bv is not None:
                            e.wait_ge(VE.sem, bv)
                        e.matmul(B["p_bank"][bank_a][:, 0:512], B["ones_f"][:],
                                 B["mrow"][0:1, 0:512],
                                 start=True, stop=True).then_inc(PE.sem, 1)
                    P.em("tensor", tbr)
                    br_pe = PE.inc(1)

                    vb8 = VE.n
                    def vbl(e, t=t, h=h, kind=kind, rslot=rslot, pm=br_pe,
                            bank_a=bank_a, vb8=vb8):
                        e.wait_ge(PE.sem, pm)
                        e.wait_ge(VE.sem, vb8)
                        pbk = B["p_bank"][bank_a]
                        e.tensor_tensor(B["qc_t"][:, 0:256], B["w_sb"][rslot][:, 0:256],
                                        pbk[:, 0:256], ALU.mult).then_inc(VE.sem, 1)
                        if kind == "carry":
                            e.tensor_copy(B["qs_t"][:, 0:256],
                                          B["carry"][:, h * 256:(h + 1) * 256]).then_inc(VE.sem, 1)
                        else:
                            pw = B["w_sb"][1 - rslot][:, 0:256] if kind == "b0" \
                                else B["w_sb"][1 - rslot][:, 256:512]
                            e.tensor_tensor(B["qs_t"][:, 0:256], pw, pbk[:, 256:512],
                                            ALU.mult).then_inc(VE.sem, 1)
                        e.wait_ge(VE.sem, vb8 + 2)
                        aoff = (512, 768) if kind == "b0" else (0, 256)
                        e.tensor_tensor(B["qs_t"][:, 256:512], B["qs_t"][:, 0:256],
                                        B["ab"][:, aoff[0]:aoff[0] + 256],
                                        ALU.mult).then_inc(VE.sem, 1)
                        e.tensor_tensor(B["qc_t"][:, 256:512], B["qc_t"][:, 0:256],
                                        B["ab"][:, aoff[1]:aoff[1] + 256],
                                        ALU.mult).then_inc(VE.sem, 1)
                        e.wait_ge(VE.sem, vb8 + 4)
                        if kind == "b0" or kind == "carry":
                            cof = 0
                        else:
                            cof = 256
                        e.tensor_tensor(B["att_o"][:, h * T + cof: h * T + cof + 256],
                                        B["qs_t"][:, 256:512], B["qc_t"][:, 256:512],
                                        ALU.add).then_inc(VE.sem, 1)
                    P.em("vector", vbl)
                    VE.inc(5)
                    BLD_V.append(VE.n)

                if wi_ == len(wins) - 1 and t < NTT:
                    bank_c = 5 if (blend_cnt[0] % 2 == 0) else 7
                    vbc = VE.n
                    def vcr(e, rslot=rslot, vbc=vbc,
                            bv=(BLD_V[-1] if BLD_V else None)):
                        if bv is not None:
                            e.wait_ge(VE.sem, bv)
                        e.tensor_copy(B["mrow"][0:1, 0:256],
                                      B["rrow"][0:1, rslot * T + 256: rslot * T + 512]).then_inc(VE.sem, 1)
                    P.em("vector", vcr)
                    VE.inc(1)
                    vcr_m = VE.n

                    def tcr(e, bank_c=bank_c, vm=vcr_m,
                            bv=(BLD_V[-2] if len(BLD_V) >= 2 else None)):
                        e.wait_ge(VE.sem, vm)
                        if bv is not None:
                            e.wait_ge(VE.sem, bv)
                        e.matmul(B["p_bank"][bank_c][:, 0:256], B["ones_f"][:],
                                 B["mrow"][0:1, 0:256],
                                 start=True, stop=True).then_inc(PE.sem, 1)
                    P.em("tensor", tcr)
                    cr_pe = PE.inc(1)

                    def vcu(e, t=t, h=h, rslot=rslot, pm=cr_pe, bank_c=bank_c):
                        e.wait_ge(PE.sem, pm)
                        e.tensor_tensor(B["carry"][:, h * 256:(h + 1) * 256],
                                        B["w_sb"][rslot][:, 256:512],
                                        B["p_bank"][bank_c][:, 0:256],
                                        ALU.mult).then_inc(VE.sem, 1)
                    P.em("vector", vcu)
                    VE.inc(1)
                    BLD_V.append(VE.n)

        if t == 0:
            cn_, tok0 = 256, 0
        elif t < NTT:
            cn_, tok0 = 512, 512 * t - 256
        else:
            cn_, tok0 = 256, 1792
        for og in range(4):
            for hc in range(H):
                sl3 = QPN[0] % 12
                m3 = S["wqp"][sl3].inc(16)
                def pldo(e, hc=hc, og=og, sl3=sl3, pv=(QPM.get(QPN[0] - 12)),
                         woT_d=woT_d):
                    if pv is not None:
                        e.wait_ge(PE.sem, pv)
                    e.dma_start(out=B["wpan"][sl3][:],
                                in_=woT_d[hc * 128:(hc + 1) * 128,
                                          og * 1024:(og + 1) * 1024]).then_inc(S["wqp"][sl3].sem, 16)
                P.em("sync" if QPN[0] % 2 == 0 else "scalar", pldo)

                def to_(e, t=t, og=og, hc=hc, sl3=sl3, m3=m3, cn_=cn_,
                        bv=BLD_V[-1], wv=(OP_EV[-1] if OP_EV else None)):
                    e.wait_ge(S["wqp"][sl3].sem, m3)
                    if hc == 0:
                        e.wait_ge(VE.sem, bv)
                        if wv is not None:
                            e.wait_ge(VE.sem, wv)
                    ins = None
                    for oc in range(8):
                        ins = e.matmul(B["p_bank"][oc][:, 0:cn_],
                                       B["wpan"][sl3][:, oc * 128:(oc + 1) * 128],
                                       B["att_o"][:, hc * T: hc * T + cn_],
                                       start=(hc == 0), stop=(hc == H - 1))
                    ins.then_inc(PE.sem, 1)
                P.em("tensor", to_)
                QPM[QPN[0]] = PE.inc(1)
                QPN[0] += 1
            op_pe = PE.n

            for oc in range(8):
                dc0 = og * 8 + oc
                axslot = AXN[0] % 2
                m4 = S["axl"][axslot].inc(16)
                def gxa(e, tok0=tok0, dc0=dc0, axslot=axslot, cn_=cn_,
                        pv=(AXM.get(AXN[0] - 2))):
                    if pv is not None:
                        e.wait_ge(arsw[axslot].sem, pv)
                    e.dma_start(out=B["axr"][axslot][:, 0:cn_],
                                in_=xfm[dc0, :, tok0:tok0 + cn_]).then_inc(S["axl"][axslot].sem, 16)
                P.em("gpsimd", gxa)

                def vo(e, oc=oc, axslot=axslot, m4=m4, pm=op_pe, cn_=cn_):
                    e.wait_ge(PE.sem, pm)
                    e.wait_ge(S["axl"][axslot].sem, m4)
                    e.tensor_tensor(B["axr"][axslot][:, 0:cn_],
                                    B["axr"][axslot][:, 0:cn_],
                                    B["p_bank"][oc][:, 0:cn_], ALU.add).then_inc(VE.sem, 1)
                P.em("vector", vo)
                OP_EV.append(VE.inc(1))

                def gwa(e, tok0=tok0, dc0=dc0, axslot=axslot, cn_=cn_,
                        vm=OP_EV[-1]):
                    e.wait_ge(VE.sem, vm)
                    e.dma_start(out=xfm[dc0, :, tok0:tok0 + cn_],
                                in_=B["axr"][axslot][:, 0:cn_]).then_inc(arsw[axslot].sem, 16)
                P.em("gpsimd", gwa)
                AXM[AXN[0]] = arsw[axslot].inc(16)
                AXN[0] += 1

    return {"res_done": [(arsw[k].sem, arsw[k].n) for k in range(2)]}


AC2M = {}
WS2M = {}
WS4M = {}
WSKV = []
ROPE_VM = []
AROW_V = []
NORM_V = []
QPM = {}
QPN = [0]
QT_PE = []
QROPE_V = []
QR_WR = []
WIN_PE = []
SIG_V = []
WOUT_V = []
RRD_V = []
BLD_V = []
OP_EV = []
AXM = {}
AXN = [0]


def emit_readback(P, S, B, xfm, out_ap, dep_marks):
    rbl, rbs = S["rbl"], S["rbs"]
    st_marks = {}
    for dc in range(DC):
        m = rbl[dc % 2].inc(16)
        def sld(e, dc=dc, dep_marks=dep_marks,
                pv=(st_marks.get(dc - 2))):
            if dc == 0:
                for sm, vv_ in dep_marks:
                    e.wait_ge(sm, vv_)
            if dc >= 2:
                e.wait_ge(rbs[dc % 2].sem, st_marks[dc - 2])
            e.dma_start(out=B["ig_x"][dc % 2][:, 0:2048],
                        in_=xfm[dc]).then_inc(rbl[dc % 2].sem, 16)
        P.em("sync", sld)

        def gst(e, dc=dc, m=m):
            e.wait_ge(rbl[dc % 2].sem, m)
            e.dma_start(out=out_ap[dc * 128:(dc + 1) * 128, :],
                        in_=B["ig_x"][dc % 2][:, 0:2048]).then_inc(rbs[dc % 2].sem, 16)
        P.em("gpsimd", gst)
        st_marks[dc] = rbs[dc % 2].inc(16)

    def fin(e):
        e.wait_ge(rbs[0].sem, rbs[0].n)
        e.wait_ge(rbs[1].sem, rbs[1].n)
    P.em("gpsimd", fin)


def emit_egress(P, S, B, xfm, out_ap, dep_marks, p0m):
    PE, VE = S["PE"], S["VE"]
    igx, igst = S["igx"], S["igst"]
    NSUB = N // 128
    ev_marks = []
    st_marks = []
    ld_marks = {}
    gi = 0
    for s in range(NSUB):
        slot = s % 2
        xm = igx[slot].inc(16)
        def gld(e, s=s, slot=slot, deps=(dep_marks if s == 0 else None),
                pv=(ld_marks.get(s - 2))):
            if deps is not None:
                for sm, vv_ in deps:
                    e.wait_ge(sm, vv_)
            if pv is not None:
                e.wait_ge(PE.sem, pv)
            src_ = xfm.rearrange("dc p t -> p dc t")[:, :, s * 128:(s + 1) * 128]
            e.dma_start(out=B["ig_x"][slot][:].rearrange("p (dc j) -> p dc j", dc=32),
                        in_=src_).then_inc(igx[slot].sem, 16)
        P.em("gpsimd", gld)

        for grp in range(8):
            def t(e, s=s, slot=slot, grp=grp, gi=gi, xm=xm):
                if grp == 0:
                    e.wait_ge(igx[slot].sem, xm)
                if gi >= 2:
                    e.wait_ge(VE.sem, ev_marks[gi - 2])
                pg = B["p_ing"][gi % 2]
                ins = None
                for k in range(4):
                    dc = grp * 4 + k
                    ins = e.transpose(pg[:, k * 128:(k + 1) * 128],
                                      B["ig_x"][slot][:, dc * 128:(dc + 1) * 128],
                                      B["eye_f"][:])
                ins.then_inc(PE.sem, 1)
            P.em("tensor", t)
            PE.inc(1)
            if grp == 7:
                ld_marks[s] = PE.n

            def v(e, gi=gi, pm=PE.n,
                  stm=(st_marks[gi - 2] if gi >= 2 else None)):
                e.wait_ge(PE.sem, pm)
                if stm is not None:
                    e.wait_ge(igst[gi % 2].sem, stm)
                e.tensor_copy(B["ig_st"][gi % 2][:],
                              B["p_ing"][gi % 2][:, 0:512]).then_inc(VE.sem, 1)
            P.em("vector", v)
            ev_marks.append(VE.inc(1))

            def g2(e, s=s, grp=grp, gi=gi, vm=VE.n):
                e.wait_ge(VE.sem, vm)
                dst = out_ap[s * 128:(s + 1) * 128,
                             grp * 512:(grp + 1) * 512]
                e.dma_start(out=dst, in_=B["ig_st"][gi % 2][:]).then_inc(igst[gi % 2].sem, 16)
            P.em("gpsimd", g2)
            st_marks.append(igst[gi % 2].inc(16))
            gi += 1

    def fin(e):
        e.wait_ge(igst[0].sem, igst[0].n)
        e.wait_ge(igst[1].sem, igst[1].n)
    P.em("gpsimd", fin)


def _reset_emit_state():
    for d in (AC2M, WS2M, WS4M, QPM, AXM):
        d.clear()
    for l in (WSKV, ROPE_VM, AROW_V, NORM_V, QT_PE, QROPE_V, QR_WR, WIN_PE,
              SIG_V, WOUT_V, RRD_V, BLD_V, OP_EV):
        l.clear()
    QPN[0] = 0
    AXN[0] = 0


def build(mode="dequant", ncores=8):
    _reset_emit_state()
    nc = bass.Bass("TRN2", target_bir_lowering=False)
    x_ap = nc.dram_tensor("x", [N, D], F32, kind="ExternalInput").ap()
    wq_ap = nc.dram_tensor("wq", [D, D], F32, kind="ExternalInput").ap()
    wk_ap = nc.dram_tensor("wk", [128, D], F32, kind="ExternalInput").ap()
    wv_ap = nc.dram_tensor("wv", [128, D], F32, kind="ExternalInput").ap()
    wo_ap = nc.dram_tensor("wo", [D, D], F32, kind="ExternalInput").ap()
    tpos_ap = nc.dram_tensor("tpos", [1, EE], F32, kind="ExternalInput").ap()
    kmask_ap = nc.dram_tensor("kmask", [20, 128], F32, kind="ExternalInput").ap()
    flo_ap = nc.dram_tensor("flo", [1, 1], F32, kind="ExternalInput").ap()
    gq_ap = nc.dram_tensor("gq", [D, I], I32, kind="ExternalInput").ap()
    gs_ap = nc.dram_tensor("gs", [DC, I], F32, kind="ExternalInput").ap()
    gz_ap = nc.dram_tensor("gz", [DC, I], F32, kind="ExternalInput").ap()
    uq_ap = nc.dram_tensor("uq", [D, I], I32, kind="ExternalInput").ap()
    us_ap = nc.dram_tensor("us", [DC, I], F32, kind="ExternalInput").ap()
    uz_ap = nc.dram_tensor("uz", [DC, I], F32, kind="ExternalInput").ap()
    dq_ap = nc.dram_tensor("dq", [I, D], I32, kind="ExternalInput").ap()
    ds_ap = nc.dram_tensor("ds", [ICN, D], F32, kind="ExternalInput").ap()
    dz_ap = nc.dram_tensor("dz", [ICN, D], F32, kind="ExternalInput").ap()
    out_ap = nc.dram_tensor("out", [N, D], F32, kind="ExternalOutput").ap()

    xfm = nc.dram_tensor("xfm", [DC, 128, N], F32, kind="Internal").ap()
    h_scr = nc.dram_tensor("h_scr", [ICN, 128, N], BF16, kind="Internal").ap()
    ds2 = nc.dram_tensor("ds2", [ICN, D], F32, kind="Internal").ap()
    dz2 = nc.dram_tensor("dz2", [ICN, D], F32, kind="Internal").ap()
    dsb = nc.dram_tensor("dsb", [ICN, D], BF16, kind="Internal").ap()
    dzb = nc.dram_tensor("dzb", [ICN, D], BF16, kind="Internal").ap()
    gs_b = nc.dram_tensor("gs_b", [DC, I], BF16, kind="Internal").ap()
    gz_b = nc.dram_tensor("gz_b", [DC, I], BF16, kind="Internal").ap()
    us_b = nc.dram_tensor("us_b", [DC, I], BF16, kind="Internal").ap()
    uz_b = nc.dram_tensor("uz_b", [DC, I], BF16, kind="Internal").ap()
    d_eyef = nc.dram_tensor("d_eyef", [128, 128], F32, kind="Internal").ap()
    cache_g = nc.dram_tensor("cache_g", [ICN, 128, D], BF16, kind="Internal").ap()
    cache_u = nc.dram_tensor("cache_u", [ICN, 128, D], BF16, kind="Internal").ap()
    cache_d = nc.dram_tensor("cache_d", [DC, 128, I], BF16, kind="Internal").ap()
    q_scr = nc.dram_tensor("q_scr", [H, 128, N], BF16, kind="Internal").ap()
    wq_bf = nc.dram_tensor("wq_bf", [D, D], BF16, kind="Internal").ap()
    wo_bf = nc.dram_tensor("wo_bf", [D, D], BF16, kind="Internal").ap()
    wqT_d = nc.dram_tensor("wqT_d", [D, D], BF16, kind="Internal").ap()
    woT_d = nc.dram_tensor("woT_d", [D, D], BF16, kind="Internal").ap()
    wk_bf = nc.dram_tensor("wk_bf", [128, D], BF16, kind="Internal").ap()
    wv_bf = nc.dram_tensor("wv_bf", [128, D], BF16, kind="Internal").ap()
    kc_d = nc.dram_tensor("kc_d", [256, 256], BF16, kind="Internal").ap()
    vc_d = nc.dram_tensor("vc_d", [512, 128], BF16, kind="Internal").ap()
    kg_d = nc.dram_tensor("kg_d", [256 * ncores, 256], BF16, kind="Internal").ap()
    vg_d = nc.dram_tensor("vg_d", [512 * ncores, 128], BF16, kind="Internal").ap()
    caches = {"g": cache_g, "u": cache_u, "d": cache_d}
    qsrc = {"gq": gq_ap, "gs_b": gs_b, "gz_b": gz_b,
            "uq": uq_ap, "us_b": us_b, "uz_b": uz_b,
            "dq": dq_ap, "dsb": dsb, "dzb": dzb}

    B = alloc_buffers(nc)
    S = {k: Ctr(nc, k) for k in ("PE", "VE", "AC")}

    S["p0a"] = DS(nc, "p0a"); S["p0b"] = DS(nc, "p0b"); S["p0c"] = DS(nc, "p0c")
    S["szb"] = DS(nc, "szb")
    S["igx"] = [DS(nc, f"igx{i}") for i in range(2)]
    S["igst"] = [DS(nc, f"igst{i}") for i in range(2)]
    S["lnx"] = [DS(nc, f"lnx{i}") for i in range(2)]
    S["dqs"] = [DS(nc, f"dqs{i}") for i in range(2)]
    S["dqs2"] = [DS(nc, f"dqs2{i}") for i in range(2)]
    S["dqq"] = [DS(nc, f"dqq{i}") for i in range(2)]
    S["dqq2"] = [DS(nc, f"dqq2{i}") for i in range(2)]
    S["cwg"] = [DS(nc, f"cwg{i}") for i in range(2)]
    S["cwu"] = [DS(nc, f"cwu{i}") for i in range(2)]
    S["cwd"] = [DS(nc, f"cwd{i}") for i in range(2)]
    S["pang"] = [DS(nc, f"pang{i}") for i in range(2)]
    S["panu"] = [DS(nc, f"panu{i}") for i in range(2)]
    S["pand"] = [DS(nc, f"pand{i}") for i in range(2)]
    S["hstm"] = [DS(nc, f"hstm{i}") for i in range(4)]
    S["hld"] = [DS(nc, f"hld{i}") for i in range(2)]
    S["xrl"] = [DS(nc, f"xrl{i}") for i in range(4)]
    S["rsw"] = [DS(nc, f"rsw{i}") for i in range(4)]
    S["rbl"] = [DS(nc, f"rbl{i}") for i in range(2)]
    S["rbs"] = [DS(nc, f"rbs{i}") for i in range(2)]
    S["GP"] = Ctr(nc, "GP")
    S["ws1"] = [DS(nc, f"ws1{i}") for i in range(2)]
    S["ws2"] = [DS(nc, f"ws2{i}") for i in range(2)]
    S["ws3"] = [DS(nc, f"ws3{i}") for i in range(2)]
    S["ws4"] = [DS(nc, f"ws4{i}") for i in range(2)]
    S["ws5"] = DS(nc, "ws5")
    S["ac2"] = [Ctr(nc, f"ac2{i}") for i in range(2)]
    S["rp"] = DS(nc, "rp")
    S["qsw"] = [DS(nc, f"qsw{i}") for i in range(2)]
    S["qrl"] = [DS(nc, f"qrl{i}") for i in range(2)]
    S["wqp"] = [DS(nc, f"wqp{i}") for i in range(12)]
    S["ex"] = DS(nc, "ex")
    S["cc"] = Ctr(nc, "cc")
    S["axl"] = [DS(nc, f"axl{i}") for i in range(2)]
    S["arsw"] = [DS(nc, f"arsw{i}") for i in range(2)]
    P = Prog()

    p0m = emit_p0(P, S, B, ds_ap, dz_ap, ds2, dz2, d_eyef,
                  szsrc={"gs": gs_ap, "gz": gz_ap, "us": us_ap, "uz": uz_ap},
                  szb_d={"dsb": dsb, "dzb": dzb, "gs_b": gs_b, "gz_b": gz_b,
                         "us_b": us_b, "uz_b": uz_b})
    global p0m_sz, p0m_szb
    p0m_sz = (S["p0b"].sem, p0m["sz_p0b"])
    p0m_szb = p0m["szb"]
    igm = emit_ingress(P, S, B, x_ap, xfm, p0m)
    S["wbfd"] = [wq_bf, wo_bf]
    stage = "kern"
    if stage == "ing":
        emit_readback(P, S, B, xfm, out_ap, igm["wr_done"])
    elif stage in ("attn", "mod", "kern"):
        ffm = emit_ffn(P, S, B, xfm, h_scr, caches, qsrc, mode,
                       dep_marks=igm["wr_done"], psum_free_ve=igm["last_ev"],
                       stage="full", dump_ap=None)
        wsm = emit_wstage(P, S, B, wq_ap, wk_ap, wv_ap, wo_ap, wqT_d, woT_d,
                          wk_bf, wv_bf, dep_marks=ffm["res_done"])
        rm = emit_rope_alpha(P, S, B, tpos_ap, kmask_ap, flo_ap, wsm["eyeb_ve"],
                             dep_yev=ffm["last_yev"])
        adeps = (ffm["res_done"] +
                 [(S["ws1"][k].sem, S["ws1"][k].n) for k in range(2)] +
                 [(S["ws2"][k].sem, S["ws2"][k].n) for k in range(2)])
        am_ = emit_attn(P, S, B, xfm, q_scr, wqT_d, woT_d, kg_d, vg_d, kc_d, vc_d,
                        dep_marks=adeps, wsm=wsm, rm=rm, ncores=ncores)
        if stage == "mod":
            ffm2 = emit_ffn(P, S, B, xfm, h_scr, caches, qsrc, "cached",
                            dep_marks=am_["res_done"], psum_free_ve=OP_EV[-1],
                            stage="full", dump_ap=None, pan_gate_pe=QPM[QPN[0] - 1])
            emit_readback(P, S, B, xfm, out_ap, ffm2["res_done"])
        elif stage == "kern":
            ffm2 = emit_ffn(P, S, B, xfm, h_scr, caches, qsrc, "cached",
                            dep_marks=am_["res_done"], psum_free_ve=OP_EV[-1],
                            stage="full", dump_ap=None, pan_gate_pe=QPM[QPN[0] - 1])
            emit_egress(P, S, B, xfm, out_ap, ffm2["res_done"], p0m)
        else:
            emit_readback(P, S, B, xfm, out_ap, am_["res_done"])
    else:
        ffm = emit_ffn(P, S, B, xfm, h_scr, caches, qsrc, mode,
                       dep_marks=igm["wr_done"], psum_free_ve=igm["last_ev"],
                       stage=stage, dump_ap=out_ap)
        if stage == "full":
            emit_readback(P, S, B, xfm, out_ap, ffm["res_done"])
        else:
            def fin2(e, dm=ffm["res_done"]):
                for sm, vv_ in dm:
                    e.wait_ge(sm, vv_)
            P.em("gpsimd", fin2)

    P.install(nc)
    if os.environ.get("MKAUDIT"):
        import collections
        tally = collections.Counter()
        for ins in nc.all_instructions():
            si = ins.sync_info
            if si is None:
                continue
            for su in si.on_update:
                if su.update_mode in ("sem-inc", "sem-add-imm"):
                    tally[su.id] += su.update_value
        def flat(d):
            for k, v in d.items():
                if isinstance(v, list):
                    for i, x in enumerate(v):
                        yield f"{k}{i}", x
                else:
                    yield k, v
        for k, c in flat(S):
            if tally.get(c.sem.num, 0) != c.n:
                print(f"AUDIT MISMATCH {k}: ledger={c.n} emitted={tally.get(c.sem.num)}")
        print("AUDIT done")
    return nc


def emulate(x, gq, gs, gz, uq, us, uz, dq, ds, dz):
    f32 = np.float32
    xbf = x.astype(bfd).astype(f32)
    mu = xbf.mean(axis=1, keepdims=True)
    sq = (xbf ** 2).astype(bfd).astype(f32)
    var = sq.mean(axis=1, keepdims=True) - mu ** 2
    r = np.exp(-0.5 * np.log(var + EPS))
    xn = ((xbf - mu) * r).astype(bfd).astype(f32)

    def deq(q, s, z, half=False):
        G = q.shape[0] // s.shape[0]
        sf = np.repeat(s, G, axis=0).astype(bfd).astype(f32)
        zf = np.repeat(z, G, axis=0).astype(bfd).astype(f32)
        if half:
            sf = np.repeat(0.5 * s, G, axis=0).astype(bfd).astype(f32)
            zf = np.repeat(0.5 * z, G, axis=0).astype(bfd).astype(f32)
        return ((q.astype(f32) - 8.0) * sf + zf).astype(bfd).astype(f32)

    Wg = deq(gq, gs, gz)
    Wu = deq(uq, us, uz)
    Wd = deq(dq, ds, dz, half=True)
    gate = xn @ Wg
    up = xn @ Wu
    sil = gate / (1.0 + np.exp(-gate))
    h = (sil * up).astype(bfd).astype(f32)
    y = h @ Wd
    return x + y


_NC_CACHE = {}


def kernel(L0_attn_x, L0_attn_Wq, L0_attn_Wk, L0_attn_Wv, L0_attn_Wo,
           L0_ffn_gate_q, L0_ffn_gate_s, L0_ffn_gate_z,
           L0_ffn_up_q, L0_ffn_up_s, L0_ffn_up_z,
           L0_ffn_down_q, L0_ffn_down_s, L0_ffn_down_z):
    x = np.ascontiguousarray(np.asarray(L0_attn_x, dtype=np.float32))
    BB, SS, _ = x.shape
    NCORES = 8
    CHUNKS = 4

    if "nc" not in _NC_CACHE:
        _NC_CACHE["nc"] = build("dequant", ncores=NCORES)
    nc = _NC_CACHE["nc"]

    wq = np.ascontiguousarray(np.asarray(L0_attn_Wq, dtype=np.float32))
    wk = np.ascontiguousarray(np.asarray(L0_attn_Wk, dtype=np.float32))
    wv = np.ascontiguousarray(np.asarray(L0_attn_Wv, dtype=np.float32))
    wo = np.ascontiguousarray(np.asarray(L0_attn_Wo, dtype=np.float32))
    gq = np.ascontiguousarray(np.asarray(L0_ffn_gate_q, dtype=np.int32))
    gs = np.ascontiguousarray(np.asarray(L0_ffn_gate_s, dtype=np.float32))
    gz = np.ascontiguousarray(np.asarray(L0_ffn_gate_z, dtype=np.float32))
    uq = np.ascontiguousarray(np.asarray(L0_ffn_up_q, dtype=np.int32))
    us = np.ascontiguousarray(np.asarray(L0_ffn_up_s, dtype=np.float32))
    uz = np.ascontiguousarray(np.asarray(L0_ffn_up_z, dtype=np.float32))
    dq_ = np.ascontiguousarray(np.asarray(L0_ffn_down_q, dtype=np.int32))
    dsx = np.ascontiguousarray(np.asarray(L0_ffn_down_s, dtype=np.float32))
    dzx = np.ascontiguousarray(np.asarray(L0_ffn_down_z, dtype=np.float32))

    in_maps = []
    for c in range(NCORES):
        b, k = c // CHUNKS, c % CHUNKS
        t0 = k * N
        tpos = np.arange(t0 - 256, t0 + N + 256, dtype=np.float32).reshape(1, EE)
        km = np.zeros(EE, np.float32)
        if k == 0:
            km[0:256] = -1e30
        if k == CHUNKS - 1:
            km[EE - 256:EE] = -1e30
        km = km.reshape(20, 128)
        flo = np.array([[0.0 if k == 0 else 1.0]], dtype=np.float32)
        in_maps.append({
            "x": np.ascontiguousarray(x[b, t0:t0 + N, :]),
            "wq": wq, "wk": wk, "wv": wv, "wo": wo,
            "gq": gq, "gs": gs, "gz": gz,
            "uq": uq, "us": us, "uz": uz,
            "dq": dq_, "ds": dsx, "dz": dzx,
            "tpos": tpos, "kmask": km, "flo": flo,
        })

    res = run_bass_kernel_spmd(nc, in_maps, core_ids=list(range(NCORES)))
    out = np.empty((BB, SS, D), dtype=np.float32)
    for c in range(NCORES):
        b, k = c // CHUNKS, c % CHUNKS
        out[b, k * N:(k + 1) * N, :] = res.results[c]["out"]
    return out
